# version 05
# baseline (speedup 1.0000x reference)
import os
import numpy as np
import ml_dtypes
from contextlib import ExitStack

_A_STOP = os.environ.get("A_STOP", "")
_UNFUSED = os.environ.get("MK_UNFUSED", "") == "1"

import concourse.bass as bass
import concourse.mybir as mybir
from concourse.bass_utils import run_bass_kernel_spmd
from concourse.bass_core_rust import IndirectOffsetOnAxis

F32 = mybir.dt.float32
BF16 = mybir.dt.bfloat16
I8 = mybir.dt.int8
AF = mybir.ActivationFunctionType
OP = mybir.AluOpType
AX = mybir.AxisListType

NCORE = 8
S = 8192
DM = 4096
T = 1024
KC = 32
TH = T // 512
SCALE = float(1.0 / np.sqrt(128.0))
LOG_SCALE = float(np.log(1.0 / np.sqrt(128.0)))
LNEPS = 1e-5
NEG = -1e9
BF = ml_dtypes.bfloat16


class Sem:
    __slots__ = ("h", "n")

    def __init__(self, h):
        self.h = h
        self.n = 0

    def inc(self, inst, k=1):
        inst.then_inc(self.h, k)
        self.n += k
        return inst


class Chain:

    def __init__(self, eng, default_sem):
        self.eng = eng
        self.ds = default_sem
        self.pend = None

    def wait(self):
        if self.pend is not None:
            self.eng.wait_ge(self.pend[0].h, self.pend[1])
            self.pend = None

    def op(self, inst, sem=None):
        s = sem if sem is not None else self.ds
        s.inc(inst)
        self.pend = (s, s.n)
        return inst


class Prog:
    def __init__(self):
        self.nc = bass.Bass("TRN2", target_bir_lowering=False)
        self.ctx = ExitStack()

    def sem(self, name):
        return Sem(self.ctx.enter_context(self.nc.semaphore(name)))

    def sbuf(self, name, shape, dt):
        return self.ctx.enter_context(self.nc.sbuf_tensor(name, shape, dt))

    def din(self, name, shape, dt):
        return self.nc.dram_tensor(name, shape, dt, kind="ExternalInput").ap()

    def dout(self, name, shape, dt):
        return self.nc.dram_tensor(name, shape, dt, kind="ExternalOutput").ap()



def emit_ln(P, blk, xT_d, xn_s, b, sm):
    sq = sm["sq"]; pe = sm["pe"]; dv = sm["dv"]; ac = sm["ac"]
    xfd = sm["xfd"]; xfg = sm["xfg"]; xn = sm["xn"]; xng = sm["xng"]
    statd = sm["statd"]; bcp = sm["bcp"]; bcd = sm["bcd"]
    xbs = sm["xbs"]
    xfs = sm["xfs"]
    ps_stat = b["ps_stat"]; ps_bc = b["ps_bc"]; rows = b["rows"]

    @blk.vector
    def _(v):
        v.memset(b["ones_bf"][:], 1.0)
        dv.inc(v.memset(b["ones_f"][:], 1.0))
    ones_dv = dv.n

    pe_base = pe.n
    xb_marks = []

    @blk.gpsimd
    def _(g):
        for kc in range(KC):
            if kc >= 3:
                g.wait_ge(pe.h, pe_base + (kc - 2))
            xbs[kc % 3].inc(g.dma_start(out=b["xb"][:, kc % 3, :],
                                        in_=xT_d[128 * kc:128 * (kc + 1), :]), 16)
            xb_marks.append(xbs[kc % 3].n)

    sq_base = sq.n

    @blk.scalar
    def _(s):
        for kc in range(KC):
            s.wait_ge(xbs[kc % 3].h, xb_marks[kc])
            if kc >= 2:
                s.wait_ge(pe.h, pe_base + (kc - 1))
            sq.inc(s.activation(out=b["xsq"][:, kc % 2, :], in_=b["xb"][:, kc % 3, :],
                                func=AF.Square))

    @blk.tensor
    def _(t):
        t.wait_ge(dv.h, ones_dv)
        for kc in range(KC):
            t.wait_ge(xbs[kc % 3].h, xb_marks[kc])
            t.wait_ge(sq.h, sq_base + (kc + 1))
            insts = []
            for th in range(TH):
                insts.append(t.matmul(ps_stat[:, th, :], b["ones_bf"][:],
                                      b["xb"][:, kc % 3, 512 * th:512 * (th + 1)],
                                      start=(kc == 0), stop=(kc == KC - 1),
                                      skip_group_check=True))
            for th in range(TH):
                insts.append(t.matmul(ps_stat[:, TH + th, :], b["ones_bf"][:],
                                      b["xsq"][:, kc % 2, 512 * th:512 * (th + 1)],
                                      start=(kc == 0), stop=(kc == KC - 1),
                                      skip_group_check=True))
            if kc == KC - 1:
                for i in insts:
                    statd.inc(i)
            else:
                pe.inc(insts[-1])

    chd = sm["chd"]; cha = sm["cha"]

    @blk.vector
    def _(v):
        ch = Chain(v, chd)
        v.wait_ge(statd.h, statd.n)
        ch.op(v.tensor_scalar_mul(rows[:, 0:T], ps_stat[0:1, 0:TH, :], 1.0 / DM))
        ch.op(v.tensor_scalar_mul(rows[:, T:2 * T], ps_stat[0:1, TH:2 * TH, :], 1.0 / DM))
        ch.wait()
        ch.op(v.tensor_mul(rows[:, 2 * T:3 * T], rows[:, 0:T], rows[:, 0:T]))
        ch.wait()
        ch.op(v.tensor_sub(rows[:, T:2 * T], rows[:, T:2 * T], rows[:, 2 * T:3 * T]))
        ch.wait()
        ch.op(v.tensor_scalar_add(rows[:, T:2 * T], rows[:, T:2 * T], LNEPS), dv)
    dv_s0 = dv.n
    chd_stats = chd.n

    @blk.scalar
    def _(s):
        ch = Chain(s, cha)
        s.wait_ge(dv.h, dv_s0)
        ch.op(s.activation(out=rows[:, 2 * T:3 * T], in_=rows[:, T:2 * T], func=AF.Ln))
        ch.wait()
        ch.op(s.activation(out=rows[:, 2 * T:3 * T], in_=rows[:, 2 * T:3 * T],
                           func=AF.Exp, scale=-0.5), ac)
    ac_s = ac.n

    @blk.vector
    def _(v):
        v.wait_ge(ac.h, ac_s)
        v.wait_ge(chd.h, chd_stats)
        dv.inc(v.tensor_mul(rows[:, 3 * T:4 * T], rows[:, 0:T], rows[:, 2 * T:3 * T]))
    dv_s1 = dv.n

    @blk.tensor
    def _(t):
        t.wait_ge(dv.h, dv_s1)
        for th in range(TH):
            bcp.inc(t.matmul(ps_bc[:, 0, 512 * th:512 * (th + 1)], b["ones_f"][:],
                             rows[0:1, 2 * T + 512 * th:2 * T + 512 * (th + 1)],
                             start=True, stop=True, skip_group_check=True))
            bcp.inc(t.matmul(ps_bc[:, 1, 512 * th:512 * (th + 1)], b["ones_f"][:],
                             rows[0:1, 3 * T + 512 * th:3 * T + 512 * (th + 1)],
                             start=True, stop=True, skip_group_check=True))
    bcp_s = bcp.n

    @blk.vector
    def _(v):
        v.wait_ge(bcp.h, bcp_s)
        bcd.inc(v.tensor_copy(b["rstd_bc"][:], ps_bc[:, 0, :]))
        bcd.inc(v.tensor_copy(b["mr_bc"][:], ps_bc[:, 1, :]))
    bcd_s = bcd.n

    xfd_base = xfd.n
    xfg_base = xfg.n

    xf_marks = []

    @blk.sync
    def _(sp):
        for kc in range(KC):
            if kc >= 4:
                if kc % 2 == 0:
                    sp.wait_ge(xfd.h, xfd_base + (kc - 4) // 2 + 1)
                else:
                    sp.wait_ge(xfg.h, xfg_base + (kc - 4) // 2 + 1)
            xfs[kc % 4].inc(sp.dma_start(out=b["xf"][:, kc % 4, :],
                                         in_=xT_d[128 * kc:128 * (kc + 1), :]), 16)
            xf_marks.append(xfs[kc % 4].n)

    @blk.vector
    def _(v):
        prev_sub = None
        for kc in range(0, KC, 2):
            v.wait_ge(xfs[kc % 4].h, xf_marks[kc])
            v.wait_ge(bcd.h, bcd_s)
            if prev_sub is not None:
                v.wait_ge(xn.h, prev_sub)
            xfd.inc(v.tensor_mul(b["t1d"][:], b["xf"][:, kc % 4, :], b["rstd_bc"][:]))
            v.wait_ge(xfd.h, xfd.n)
            xn.inc(v.tensor_sub(xn_s[:, kc, :], b["t1d"][:], b["mr_bc"][:]))
            prev_sub = xn.n

    @blk.gpsimd
    def _(g):
        prev_sub = None
        for kc in range(1, KC, 2):
            g.wait_ge(xfs[kc % 4].h, xf_marks[kc])
            g.wait_ge(bcd.h, bcd_s)
            if prev_sub is not None:
                g.wait_ge(xng.h, prev_sub)
            xfg.inc(g.tensor_mul(b["t1g"][:], b["xf"][:, kc % 4, :], b["rstd_bc"][:]))
            g.wait_ge(xfg.h, xfg.n)
            xng.inc(g.tensor_sub(xn_s[:, kc, :], b["t1g"][:], b["mr_bc"][:]))
            prev_sub = xng.n



def emit_proj(P, blk, *, w_d, n_ocg, xn_s, sm, w_pan, ps_proj, writer, xn_total,
              prev_pe_mark=0, prev_evac_mark=0, panel_extra=None, mm_extra=None,
              mm_extra_wait=None, panel_sem_extra=None):
    lds = [sm["wld"], sm["wld1"]]; pe = sm["pe"]; ev = sm["evac"]
    pe_marks = [prev_pe_mark, prev_pe_mark]
    ev_marks = [prev_evac_mark, prev_evac_mark]

    for ocg in range(n_ocg):
        col0 = 256 * ocg
        par = ocg % 2
        ld = lds[par]

        @blk.gpsimd
        def _(g, ocg=ocg, col0=col0, par=par, ld=ld):
            g.wait_ge(pe.h, pe_marks[ocg])
            if panel_extra is not None:
                panel_extra(g, ocg, col0, par, ld)
            else:
                src_ = w_d[:, col0:col0 + 256].rearrange("(kc p) oc -> p kc oc", p=128)
                for sub in range(2):
                    ld.inc(g.dma_start(out=w_pan[:, par, 16 * sub:16 * sub + 16, :],
                                       in_=src_[:, 16 * sub:16 * sub + 16, :]), 16)
        ld_snap = ld.n
        pse_snap = panel_sem_extra.n if panel_sem_extra is not None else 0

        @blk.tensor
        def _(t, ocg=ocg, ld_snap=ld_snap, pse_snap=pse_snap, par=par, ld=ld):
            t.wait_ge(ld.h, ld_snap)
            if panel_sem_extra is not None:
                t.wait_ge(panel_sem_extra.h, pse_snap)
            for s_, c_ in xn_total:
                t.wait_ge(s_.h, c_)
            if mm_extra_wait is not None:
                t.wait_ge(mm_extra_wait[0].h, mm_extra_wait[1])
            t.wait_ge(ev.h, ev_marks[ocg])
            for ol in range(2):
                for th in range(TH):
                    inst = None
                    for kc in range(KC):
                        inst = t.matmul(ps_proj[:, 2 * par + ol, 512 * th:512 * (th + 1)],
                                        w_pan[:, par, kc, 128 * ol:128 * ol + 128],
                                        xn_s[:, kc, 512 * th:512 * (th + 1)],
                                        start=(kc == 0),
                                        stop=(kc == KC - 1 and mm_extra is None),
                                        skip_group_check=True)
                    if mm_extra is not None:
                        inst = mm_extra(t, ocg, 2 * par + ol, th)
                    pe.inc(inst)
        pe_marks.append(pe.n)

        writer(ocg, pe.n)
        ev_marks.append(ev.n)



A_SEMS = ["sq", "pe", "dv", "ac", "xfd", "xfg", "xn", "xng", "statd",
          "bcp", "bcd", "wld", "wld1", "evac", "st", "aux", "fr", "rscS", "trS", "cpS",
          "mulS", "chd", "cha", "chg", "rot"]


def make_sems(P, names, nslot=True):
    sm = {k: P.sem(k) for k in names}
    if nslot:
        sm["xbs"] = [P.sem(f"xbs{i}") for i in range(3)]
        sm["xfs"] = [P.sem(f"xfs{i}") for i in range(4)]
    return sm


def emit_A(P, blk, sm, layer, d, scope, tag="", ag_hook=None):
    nc = P.nc
    xT_d = d["xT"]
    wq_d = d["wqT"]
    wlat_d = d["wlatT"]
    if layer == 0:
        wb_d = d["wbT"]; bias_d = d["biasb"]; ar_d = d["arT"]
        btr_d = d["BTr"]; id_d = d["identF"]
    else:
        tern_d = d["ternT"]; scl_d = d["scalesT"]
        cosq_d = d["cosq"]; sinq_d = d["sinq"]
    qT_o = d["qT"]; lat_o = d["lat"]; xp_o = d["xp"]

    def SB(name, shape, dt):
        return scope.enter_context(nc.sbuf_tensor(name + tag, shape, dt))

    xn_s = SB("xn_s", [128, KC, T], BF16)
    w_pan = SB("w_pan", [128, 2, KC, 256], BF16)
    stageb = SB("stageb", [128, 4, T], BF16)
    stagef = SB("stagef", [128, 4, T], F32)
    xf = SB("xf", [128, 4, T], F32)
    ones_bf = SB("ones_bf", [128, 1], BF16)
    ones_f = SB("ones_f", [1, 128], F32)
    if layer == 1:
        cosq_s = SB("cosq_s", [128, T], BF16)
        sinq_s = SB("sinq_s", [128, T], BF16)
        qh_buf = SB("qh_buf", [128, T], BF16)
        qrot_buf = SB("qrot_buf", [128, T], BF16)
        tA = SB("tA", [128, T], BF16)
        tB = SB("tB", [128, T], BF16)

    with nc.sbuf_tensor("xb" + tag, [128, 3, T], BF16) as xb, \
         nc.sbuf_tensor("xsq" + tag, [128, 2, T], BF16) as xsq, \
         nc.sbuf_tensor("t1d" + tag, [128, T], F32) as t1d, \
         nc.sbuf_tensor("t1g" + tag, [128, T], F32) as t1g, \
         nc.sbuf_tensor("rstd_bc" + tag, [128, T], F32) as rstd_bc, \
         nc.sbuf_tensor("mr_bc" + tag, [128, T], F32) as mr_bc, \
         nc.sbuf_tensor("rows" + tag, [1, 4 * T], F32) as rows, \
         nc.psum_tensor("ps_stat" + tag, [1, 2 * TH, 512], F32) as ps_stat, \
         nc.psum_tensor("ps_bc" + tag, [128, 2, T], F32) as ps_bc:
        b = dict(xb=xb, xsq=xsq, t1d=t1d, t1g=t1g, rstd_bc=rstd_bc, mr_bc=mr_bc,
                 rows=rows, ones_bf=ones_bf, ones_f=ones_f, xf=xf,
                 ps_stat=ps_stat, ps_bc=ps_bc)
        emit_ln(P, blk, xT_d, xn_s, b, sm)
        xn_total = ((sm["xn"], sm["xn"].n), (sm["xng"], sm["xng"].n))

    st = sm["st"]
    st_marks = [0]

    if _A_STOP == "ln":
        @blk.gpsimd
        def _(g):
            for s_, c_ in xn_total:
                g.wait_ge(s_.h, c_)
        return P

    with nc.psum_tensor("ps_proj" + tag, [128, 4, T], F32) as ps_proj:
        if layer == 1:
            @blk.gpsimd
            def _(g):
                sm["aux"].inc(g.dma_start(out=cosq_s[0:64, :], in_=cosq_d[:]), 16)
                sm["aux"].inc(g.dma_start(out=cosq_s[64:128, :], in_=cosq_d[:]), 16)
                sm["aux"].inc(g.dma_start(out=sinq_s[0:64, :], in_=sinq_d[:]), 16)
                sm["aux"].inc(g.dma_start(out=sinq_s[64:128, :], in_=sinq_d[:]), 16)
            cs_ld = sm["aux"].n

        rope_prev = [None, 0, 0]
        tA_mark = [0]
        tB_mark = [0]

        def q_writer(ocg, mm_snap):
            par = ocg % 2
            if layer == 0:
                @blk.vector
                def _(v, ocg=ocg, mm_snap=mm_snap, par=par):
                    v.wait_ge(sm["pe"].h, mm_snap)
                    v.wait_ge(st.h, st_marks[max(0, ocg - 1)])
                    for ol in range(2):
                        og = 2 * ocg + ol
                        sm["evac"].inc(v.tensor_copy(stageb[:, og % 4, :],
                                                     ps_proj[:, 2 * par + ol, :]))
            else:
                rot = sm["rot"]
                for ol in range(2):
                    og = 2 * ocg + ol
                    @blk.vector
                    def _(v, ol=ol, mm_snap=mm_snap, par=par):
                        v.wait_ge(sm["pe"].h, mm_snap)
                        if rope_prev[0] is not None:
                            v.wait_ge(rot.h, rope_prev[0])
                            v.wait_ge(sm["chd"].h, rope_prev[1])
                        sm["evac"].inc(v.tensor_copy(qh_buf[:], ps_proj[:, 2 * par + ol, :]))
                    qh_done = sm["evac"].n

                    @blk.sync
                    def _(sp, qh_done=qh_done):
                        sp.wait_ge(sm["evac"].h, qh_done)
                        if rope_prev[0] is not None:
                            sp.wait_ge(sm["chd"].h, rope_prev[2])
                        rot.inc(sp.dma_start(out=qrot_buf[0:64, :], in_=qh_buf[64:128, :]), 16)
                        rot.inc(sp.dma_start(out=qrot_buf[64:128, :], in_=qh_buf[0:64, :]), 16)
                    rot_now = rot.n

                    @blk.vector
                    def _(v, og=og, rot_now=rot_now, ocg=ocg):
                        ch = Chain(v, sm["chd"])
                        v.wait_ge(rot.h, rot_now)
                        v.wait_ge(sm["aux"].h, cs_ld)
                        v.wait_ge(st.h, st_marks[max(0, ocg - 1)])
                        ch.op(v.tensor_scalar_mul(qrot_buf[0:64, :], qrot_buf[0:64, :], -1.0))
                        ch.op(v.tensor_mul(tA[:], qh_buf[:], cosq_s[:]))
                        tA_mark[0] = sm["chd"].n
                        ch.wait()
                        ch.op(v.tensor_mul(tB[:], qrot_buf[:], sinq_s[:]))
                        tB_mark[0] = sm["chd"].n
                        ch.wait()
                        ch.op(v.tensor_add(stageb[:, og % 4, :], tA[:], tB[:]), sm["evac"])
                    rope_prev[0] = rot_now
                    rope_prev[1] = tA_mark[0]
                    rope_prev[2] = tB_mark[0]
            ev_now = sm["evac"].n

            @blk.sync
            def _(sp, ocg=ocg, ev_now=ev_now):
                sp.wait_ge(sm["evac"].h, ev_now)
                for ol in range(2):
                    og = 2 * ocg + ol
                    st.inc(sp.dma_start(out=qT_o[128 * og:128 * (og + 1), :],
                                        in_=stageb[:, og % 4, :]), 16)
            st_marks.append(st.n)

        emit_proj(P, blk, w_d=wq_d, n_ocg=16, xn_s=xn_s, sm=sm, w_pan=w_pan,
                  ps_proj=ps_proj, writer=q_writer, xn_total=xn_total)
        qst_all = st.n

        def lat_writer(ocg, mm_snap):
            par = ocg % 2

            @blk.vector
            def _(v, mm_snap=mm_snap, ocg=ocg, par=par):
                v.wait_ge(sm["pe"].h, mm_snap)
                v.wait_ge(st.h, qst_all)
                for ol in range(2):
                    og = 2 * ocg + ol
                    sm["evac"].inc(v.tensor_copy(stageb[:, og, :],
                                                 ps_proj[:, 2 * par + ol, :]))
            ev_now = sm["evac"].n

            @blk.sync
            def _(sp, ev_now=ev_now, ocg=ocg):
                sp.wait_ge(sm["evac"].h, ev_now)
                for ol in range(2):
                    og = 2 * ocg + ol
                    st.inc(sp.dma_start(out=lat_o[128 * og:128 * (og + 1), :],
                                        in_=stageb[:, og, :]), 16)
            st_marks.append(st.n)

        if _A_STOP != "q":
            emit_proj(P, blk, w_d=wlat_d, n_ocg=2, xn_s=xn_s, sm=sm, w_pan=w_pan,
                      ps_proj=ps_proj, writer=lat_writer, xn_total=xn_total,
                      prev_pe_mark=sm["pe"].n, prev_evac_mark=sm["evac"].n)
        if ag_hook is not None:
            ag_hook(qst_all, st.n)

    if _A_STOP in ("q", "lat"):
        @blk.gpsimd
        def _(g):
            g.wait_ge(st.h, st.n)
        return P

    ffn_scope = ExitStack()
    if layer == 0:
        ar_pan = ffn_scope.enter_context(nc.sbuf_tensor("ar_pan" + tag, [128, KC, 80], BF16))
        bt_pan = ffn_scope.enter_context(nc.sbuf_tensor("bt_pan" + tag, [64, 2, 16, 256], BF16))
        r_tm = ffn_scope.enter_context(nc.sbuf_tensor("r_tm" + tag, [128, 8, 64], BF16))
        reT = ffn_scope.enter_context(nc.sbuf_tensor("reT" + tag, [64, 16, T], BF16))
        rsc = ffn_scope.enter_context(nc.sbuf_tensor("rsc" + tag, [128, 16, 64], BF16))
        comb = ffn_scope.enter_context(nc.sbuf_tensor("comb" + tag, [128, 16], F32))
        esc = ffn_scope.enter_context(nc.sbuf_tensor("esc" + tag, [128, 48], F32))
        srow = ffn_scope.enter_context(nc.sbuf_tensor("srow" + tag, [128, 12], F32))
        bias_s = ffn_scope.enter_context(nc.sbuf_tensor("bias_s" + tag, [128, KC], F32))
        ident_s = ffn_scope.enter_context(nc.sbuf_tensor("ident_s" + tag, [128, 128], BF16))

        @blk.gpsimd
        def _(g):
            src = ar_d.rearrange("(kc p) a -> p kc a", p=128)
            sm["aux"].inc(g.dma_start(out=ar_pan[:, 0:16, :], in_=src[:, 0:16, :]), 16)
            sm["aux"].inc(g.dma_start(out=ar_pan[:, 16:32, :], in_=src[:, 16:32, :]), 16)
            sm["aux"].inc(g.dma_start(out=bias_s[:], in_=bias_d[:]), 16)
            sm["aux"].inc(g.dma_start(out=ident_s[:], in_=id_d[:]), 16)
        ffn_ld = sm["aux"].n

        fr = sm["fr"]; rscS = sm["rscS"]; trS = sm["trS"]; cpS = sm["cpS"]
        fr_base = fr.n
        rsc_base = rscS.n
        tr_base = trS.n
        cp_base = cpS.n
        dv_marks = {}
        ac_marks = {}
        pe_prev = sm["pe"].n

        with nc.psum_tensor("ps_r" + tag, [128, 80], F32) as ps_r, \
             nc.psum_tensor("ps_tr0" + tag, [64, 128], BF16) as ps_tr0, \
             nc.psum_tensor("ps_tr1" + tag, [64, 128], BF16) as ps_tr1:
            ps_trs = [ps_tr0, ps_tr1]
            for ts in range(8):
                @blk.tensor
                def _(t, ts=ts):
                    t.wait_ge(sm["aux"].h, ffn_ld)
                    for s_, c_ in xn_total:
                        t.wait_ge(s_.h, c_)

                    if ts == 0:
                        t.wait_ge(sm["evac"].h, sm["evac"].n)
                    else:
                        t.wait_ge(sm["dv"].h, dv_marks[ts - 1][0])
                        t.wait_ge(sm["chd"].h, dv_marks[ts - 1][1])
                        t.wait_ge(sm["ac"].h, ac_marks[ts - 1])
                    inst = None
                    for kc in range(KC):
                        inst = t.matmul(ps_r[:], xn_s[:, kc, 128 * ts:128 * (ts + 1)],
                                        ar_pan[:, kc, :], start=(kc == 0),
                                        stop=(kc == KC - 1), skip_group_check=True)
                    fr.inc(inst)
                fr_now = fr.n

                @blk.vector
                def _(v, fr_now=fr_now, ts=ts):
                    ch = Chain(v, sm["chd"])
                    v.wait_ge(fr.h, fr_now)
                    ch.op(v.tensor_copy(r_tm[:, ts, :], ps_r[:, 0:64]))
                    ch.op(v.reduce_max(srow[:, 0:1], ps_r[:, 64:80], axis=AX.X))
                    ch.wait()
                    ch.op(v.tensor_scalar_mul(srow[:, 1:2], srow[:, 0:1], -1.0), sm["dv"])
                dv_m = sm["dv"].n
                dv_marks[ts] = (dv_m, sm["chd"].n)

                @blk.scalar
                def _(s, dv_m=dv_m):
                    s.wait_ge(sm["dv"].h, dv_m)
                    sm["ac"].inc(s.activation(out=comb[:], in_=ps_r[:, 64:80],
                                              func=AF.Exp, bias=srow[:, 1:2]))
                ac_marks[ts] = sm["ac"].n
                ac_m = sm["ac"].n

                @blk.vector
                def _(v, ts=ts, ac_m=ac_m):
                    ch = Chain(v, sm["chd"])
                    v.wait_ge(sm["ac"].h, ac_m)
                    ch.op(v.reduce_sum(srow[:, 2:3], comb[:], axis=AX.X))
                    ch.wait()
                    ch.op(v.reciprocal(srow[:, 3:4], srow[:, 2:3]))
                    ch.wait()
                    ch.op(v.tensor_scalar_mul(comb[:], comb[:], srow[:, 3:4]))
                    ch.wait()
                    ch.op(v.reduce_max(srow[:, 4:5], comb[:], axis=AX.X))
                    ch.wait()
                    ch.op(v.tensor_scalar(esc[:, 0:16], comb[:], srow[:, 4:5], None,
                                          OP.is_equal))
                    ch.wait()
                    ch.op(v.scalar_tensor_tensor(esc[:, 16:32], esc[:, 0:16], NEG, comb[:],
                                                 OP.mult, OP.add))
                    ch.wait()
                    ch.op(v.reduce_max(srow[:, 5:6], esc[:, 16:32], axis=AX.X))
                    ch.wait()
                    ch.op(v.tensor_scalar(esc[:, 32:48], comb[:], srow[:, 5:6], None,
                                          OP.is_equal))
                    ch.op(v.tensor_add(srow[:, 6:7], srow[:, 4:5], srow[:, 5:6]))
                    ch.wait()
                    ch.op(v.tensor_scalar_add(srow[:, 6:7], srow[:, 6:7], 1e-6))
                    ch.wait()
                    ch.op(v.reciprocal(srow[:, 7:8], srow[:, 6:7]))
                    ch.wait()
                    ch.op(v.tensor_mul(srow[:, 8:9], srow[:, 4:5], srow[:, 7:8]))
                    ch.op(v.tensor_mul(srow[:, 9:10], srow[:, 5:6], srow[:, 7:8]))
                    ch.wait()
                    ch.op(v.tensor_scalar_mul(srow[:, 8:9], srow[:, 8:9], 0.5))
                    ch.op(v.tensor_scalar_mul(srow[:, 9:10], srow[:, 9:10], 0.5))
                    ch.wait()
                    ch.op(v.tensor_scalar_mul(esc[:, 32:48], esc[:, 32:48], srow[:, 9:10]))
                    ch.wait()
                    ch.op(v.scalar_tensor_tensor(comb[:], esc[:, 0:16], srow[:, 8:9],
                                                 esc[:, 32:48], OP.mult, OP.add))
                    for e in range(16):
                        k = 16 * ts + e
                        ch.wait()
                        if k >= 16:
                            v.wait_ge(trS.h, tr_base + (k - 16) + 1)
                        ch.op(v.tensor_scalar_mul(rsc[:, e, :], r_tm[:, ts, :],
                                                  comb[:, e:e + 1]), rscS)

                @blk.tensor
                def _(t, ts=ts):
                    for e in range(16):
                        k = 16 * ts + e
                        t.wait_ge(rscS.h, rsc_base + k + 1)
                        if k >= 2:
                            t.wait_ge(cpS.h, cp_base + (k - 2) + 1)
                        trS.inc(t.transpose(ps_trs[k % 2][:], rsc[:, e, :], ident_s[:]))

                @blk.vector
                def _(v, ts=ts):
                    for e in range(16):
                        k = 16 * ts + e
                        v.wait_ge(trS.h, tr_base + k + 1)
                        cpS.inc(v.tensor_copy(reT[:, e, 128 * ts:128 * (ts + 1)],
                                              ps_trs[k % 2][:]))
        reT_done = cpS.n

        if _A_STOP == "prep":
            @blk.gpsimd
            def _(g):
                g.wait_ge(st.h, st.n)
                g.wait_ge(cpS.h, reT_done)
            ffn_scope.close()
            return P

        def panel_extra(g, ocg, col0, par, ld):
            src = wb_d[:, col0:col0 + 256].rearrange("(kc p) oc -> p kc oc", p=128)
            for sub in range(2):
                ld.inc(g.dma_start(out=w_pan[:, par, 16 * sub:16 * sub + 16, :],
                                   in_=src[:, 16 * sub:16 * sub + 16, :]), 16)
            ld.inc(g.dma_start(out=bt_pan[:, par, :, :],
                               in_=btr_d[:, :, col0:col0 + 256]), 16)

        def mm_extra(t, ocg, ol_abs, th):
            par = ocg % 2
            ol = ol_abs - 2 * par
            inst = None
            for e in range(16):
                inst = t.matmul(ps_proj2[:, ol_abs, 512 * th:512 * (th + 1)],
                                bt_pan[:, par, e, 128 * ol:128 * ol + 128],
                                reT[:, e, 512 * th:512 * (th + 1)],
                                start=False, stop=(e == 15), skip_group_check=True)
            return inst
        mm_extra_wait = (sm["cpS"], reT_done)
    else:
        s_pan = ffn_scope.enter_context(nc.sbuf_tensor("s_pan" + tag, [128, KC, 256], BF16))

        def panel_extra(g, ocg, col0, par, ld):
            for sub in range(2):
                ld.inc(g.dma_start(
                    out=s_pan[:, 16 * sub:16 * sub + 16, :],
                    in_=scl_d[16 * sub:16 * sub + 16, col0:col0 + 256][None, :, :]
                        .partition_broadcast(128)), 16)
                ld.inc(g.dma_start(
                    out=w_pan[:, par, 16 * sub:16 * sub + 16, :],
                    in_=tern_d[:, col0:col0 + 256]
                        .rearrange("(kc p) oc -> p kc oc", p=128)[:, 16 * sub:16 * sub + 16, :]), 16)
            g.wait_ge(ld.h, ld.n)
            sm["mulS"].inc(g.tensor_mul(w_pan[:, par, :, :], w_pan[:, par, :, :], s_pan[:]))
        mm_extra = None
        mm_extra_wait = None

    xfs = sm["xfs"]
    ffn_ev_marks = [sm["evac"].n]
    ln_xf_done = (sm["xfd"].n, sm["xfg"].n)

    with nc.psum_tensor("ps_proj2" + tag, [128, 4, T], F32) as ps_proj2:

        def ffn_writer(ocg, mm_snap):
            par = ocg % 2

            @blk.sync
            def _(sp, ocg=ocg):
                sp.wait_ge(sm["evac"].h, ffn_ev_marks[ocg])
                if ocg == 0:
                    sp.wait_ge(sm["xfd"].h, ln_xf_done[0])
                    sp.wait_ge(sm["xfg"].h, ln_xf_done[1])
                for ol in range(2):
                    og = 2 * ocg + ol
                    xfs[og % 4].inc(sp.dma_start(out=xf[:, og % 4, :],
                                                 in_=xT_d[128 * og:128 * (og + 1), :]), 16)
            marks = tuple(xfs[i].n for i in range(4))

            @blk.vector
            def _(v, ocg=ocg, mm_snap=mm_snap, marks=marks, par=par):
                v.wait_ge(sm["pe"].h, mm_snap)
                v.wait_ge(st.h, st_marks[-1] if ocg == 0 else st_marks[-2])
                for ol in range(2):
                    og = 2 * ocg + ol
                    v.wait_ge(xfs[og % 4].h, marks[og % 4])
                    if layer == 0:
                        inst = v.scalar_tensor_tensor(stagef[:, og % 4, :],
                                                      ps_proj2[:, 2 * par + ol, :],
                                                      bias_s[:, og:og + 1], xf[:, og % 4, :],
                                                      OP.add, OP.add)
                    else:
                        inst = v.tensor_add(stagef[:, og % 4, :],
                                            ps_proj2[:, 2 * par + ol, :],
                                            xf[:, og % 4, :])
                    sm["evac"].inc(inst)
            ev_now = sm["evac"].n

            @blk.sync
            def _(sp, ocg=ocg, ev_now=ev_now):
                sp.wait_ge(sm["evac"].h, ev_now)
                for ol in range(2):
                    og = 2 * ocg + ol
                    st.inc(sp.dma_start(out=xp_o[128 * og:128 * (og + 1), :],
                                        in_=stagef[:, og % 4, :]), 16)
            st_marks.append(st.n)
            ffn_ev_marks.append(sm["evac"].n)

        emit_proj(P, blk, w_d=None, n_ocg=16, xn_s=xn_s, sm=sm, w_pan=w_pan,
                  ps_proj=ps_proj2, writer=ffn_writer, xn_total=xn_total,
                  prev_pe_mark=sm["pe"].n, prev_evac_mark=sm["evac"].n,
                  panel_extra=panel_extra, mm_extra=mm_extra,
                  mm_extra_wait=mm_extra_wait,
                  panel_sem_extra=(sm["mulS"] if layer == 1 else None))

    ffn_scope.close()


def build_A(layer):
    P = Prog()
    d = {"xT": P.din("xT", [DM, T], F32),
         "wqT": P.din("wqT", [DM, DM], F32),
         "wlatT": P.din("wlatT", [DM, 512], F32),
         "qT": P.dout("qT", [DM, T], BF16),
         "lat": P.dout("latT", [512, T], BF16),
         "xp": P.dout("xp", [DM, T], F32)}
    if layer == 0:
        d.update(wbT=P.din("wbT", [DM, DM], F32),
                 biasb=P.din("biasb", [128, KC], F32),
                 arT=P.din("arT", [DM, 80], F32),
                 BTr=P.din("BTr", [64, 16, DM], F32),
                 identF=P.din("identF", [128, 128], BF16))
    else:
        d.update(ternT=P.din("ternT", [DM, DM], I8),
                 scalesT=P.din("scalesT", [KC, DM], F32),
                 cosq=P.din("cosq", [64, T], F32),
                 sinq=P.din("sinq", [64, T], F32))
    sm = make_sems(P, A_SEMS)
    blk = P.ctx.enter_context(P.nc.Block())
    emit_A(P, blk, sm, layer, d, P.ctx)

    @blk.gpsimd
    def _(g):
        g.wait_ge(sm["st"].h, sm["st"].n)

    return P



_CONSTS = None


def host_consts():
    global _CONSTS
    if _CONSTS is None:
        inv = 1.0 / (10000.0 ** (np.arange(64) / 64.0))
        fr = np.arange(S, dtype=np.float64)[None, :] * inv[:, None]
        cosT = np.cos(fr).astype(np.float32)
        sinT = np.sin(fr).astype(np.float32)
        ks, i, j = np.meshgrid(np.arange(4), np.arange(128), np.arange(512),
                               indexing="ij")
        masks = np.where(128 * ks + i > j, np.float32(NEG), np.float32(0.0))
        ident = np.eye(128, dtype=BF)
        _CONSTS = (cosT, sinT, np.ascontiguousarray(masks), ident)
    return _CONSTS


def prep_weights(inputs):
    C = lambda a: np.ascontiguousarray(a)
    w = {}
    w["wqT0"] = C(inputs["L0_attn_Wq"].T)
    w["wlatT0"] = C(np.concatenate([inputs["L0_attn_Wk_down"],
                                    inputs["L0_attn_Wv_down"]], axis=0).T)
    w["wbT0"] = C(inputs["L0_ffn_base_w"].T)
    w["biasb0"] = C(inputs["L0_ffn_base_b"].reshape(KC, 128).T)
    w["arT0"] = C(np.concatenate([inputs["L0_ffn_A"],
                                  inputs["L0_ffn_router_w"]], axis=0).T)
    w["BTr0"] = C(inputs["L0_ffn_B"].transpose(2, 0, 1))
    w["wkupT"] = [C(inputs["L0_attn_Wk_up"][128 * c:128 * (c + 1), :].T)
                  for c in range(NCORE)]
    w["wvupT"] = [C(inputs["L0_attn_Wv_up"][128 * c:128 * (c + 1), :].T)
                  for c in range(NCORE)]
    w["woT0"] = C(inputs["L0_attn_Wo"].T)
    w["wqT1"] = C(inputs["L1_attn_Wq"])
    w["wlatT1"] = C(inputs["L1_attn_Wc"])
    w["wk1"] = [C(inputs["L1_attn_Wk"][:, 512 * c:512 * (c + 1)]) for c in range(NCORE)]
    w["wv1"] = [C(inputs["L1_attn_Wv"][:, 512 * c:512 * (c + 1)]) for c in range(NCORE)]
    w["woT1"] = C(inputs["L1_attn_Wo"])
    w["ternT"] = C(inputs["L1_ffn_ternary"].T)
    w["scalesT"] = C(inputs["L1_ffn_scales"].reshape(DM, KC).T)
    return w


def run_A(layer, xT_cols, w):
    P = build_A(layer)
    cosT, sinT, masks, ident = host_consts()
    in_maps = []
    for c in range(NCORE):
        m = {"xT": xT_cols[c]}
        if layer == 0:
            m.update(wqT=w["wqT0"], wlatT=w["wlatT0"], wbT=w["wbT0"],
                     biasb=w["biasb0"], arT=w["arT0"], BTr=w["BTr0"], identF=ident)
        else:
            m.update(wqT=w["wqT1"], wlatT=w["wlatT1"], ternT=w["ternT"],
                     scalesT=w["scalesT"],
                     cosq=np.ascontiguousarray(cosT[:, T * c:T * (c + 1)]),
                     sinq=np.ascontiguousarray(sinT[:, T * c:T * (c + 1)]))
        in_maps.append(m)
    res = run_bass_kernel_spmd(P.nc, in_maps, core_ids=list(range(NCORE)))
    qT = [np.asarray(r["qT"]) for r in res.results]
    latT = [np.asarray(r["latT"]) for r in res.results]
    xp = [np.asarray(r["xp"]) for r in res.results]
    return qT, latT, xp



B_SEMS = ["gld", "sld", "qld", "st", "kb", "ke", "vb", "ve", "nq", "nmx",
          "sS", "sE", "sM", "sAV", "nrm", "trP", "ocp", "mcl", "acn", "rpd",
          "rotk", "chd", "cha", "dvm"]


def emit_B(P, blk, sm, layer, d, scope, tag=""):
    nc = P.nc
    qTg_d = d["qTg"]
    latg_d = d["latg"]
    id_d = d["ident"]
    if layer == 0:
        wku_d = d["wkupT"]; wvu_d = d["wvupT"]; mask_d = d["masks"]
    else:
        wk1_d = d["wk1"]; wv1_d = d["wv1"]; cos_d = d["cosT"]; sin_d = d["sinT"]
    out_d = d["outT"]

    def SB(name, shape, dt):
        return scope.enter_context(nc.sbuf_tensor(name + tag, shape, dt))

    lat_s = SB("lat_s", [128, 4, S], BF16)
    k_fm = SB("k_fm", [128, S], BF16)
    v_s = SB("v_s", [128, 64, 130], BF16)
    q_s = SB("q_s", [128, S], BF16)
    pT0 = SB("pT0", [128, 2, 512], BF16)
    pT1 = SB("pT1", [128, 2, 512], BF16)
    ot = SB("ot", [128, 4, 128], BF16)
    ostage = SB("ostage", [128, 512], BF16)
    sq_s = SB("sq_s", [128, S], BF16)
    nrow = SB("nrow", [1, 64], F32)
    mcol = SB("mcol", [128, 1], F32)
    rl_s = SB("rl_s", [128, 2], F32)
    ones_bf = SB("ones_bf", [128, 1], BF16)
    ones_f = SB("ones_f", [1, 128], F32)
    ident_s = SB("ident_s", [128, 128], BF16)
    if layer == 0:
        wka = SB("wka", [128, 4, 128], BF16)
        wkb = SB("wkb", [128, 4, 128], BF16)
        wva = SB("wva", [128, 4, 128], BF16)
        wvb = SB("wvb", [128, 4, 128], BF16)
        mask_s = SB("mask_s", [128, 4, 512], F32)
    else:
        wk1_s = SB("wk1_s", [128, 4, 512], BF16)
        wv1_s = SB("wv1_s", [128, 4, 512], BF16)
        cos_s = SB("cos_s", [128, S], BF16)
        sin_s = SB("sin_s", [128, S], BF16)
        CH = 2048
        krot = SB("krot", [128, CH], BF16)
        tA = SB("tA", [128, CH], BF16)
        tB = SB("tB", [128, CH], BF16)

    ps_S0 = scope.enter_context(nc.psum_tensor("ps_S0" + tag, [128, 2, 512], F32))
    ps_S1 = scope.enter_context(nc.psum_tensor("ps_S1" + tag, [128, 2, 512], F32))
    ps_av0 = scope.enter_context(nc.psum_tensor("ps_av0" + tag, [128, 2, 130], F32))
    ps_av1 = scope.enter_context(nc.psum_tensor("ps_av1" + tag, [128, 2, 130], F32))
    ps_tr = scope.enter_context(nc.psum_tensor("ps_tr" + tag, [128, 128], BF16))
    ps_n = scope.enter_context(nc.psum_tensor("ps_n" + tag, [128, 128], F32))
    psS_tiles = [ps_S0[:, 0, :], ps_S0[:, 1, :], ps_S1[:, 0, :], ps_S1[:, 1, :]]
    psav_tiles = [ps_av0[:, 0, 0:128], ps_av0[:, 1, 0:128],
                  ps_av1[:, 0, 0:128], ps_av1[:, 1, 0:128]]

    gld = sm["gld"]; sld = sm["sld"]; qld = sm["qld"]; st = sm["st"]

    if "qsel" in d:
        qsel_s = SB("qsel_s", [128, 32], mybir.dt.int32)

    @blk.gpsimd
    def _(g):
        if "qsel" in d:
            gld.inc(g.dma_start(out=qsel_s[:], in_=d["qsel"][:]), 16)
        if layer == 0:
            src = wku_d.rearrange("(j u) d -> u j d", u=64)
            gld.inc(g.dma_start(out=wka[0:64, :, :], in_=src), 16)
            gld.inc(g.dma_start(out=wkb[64:128, :, :], in_=src), 16)
            srcv = wvu_d.rearrange("(j u) d -> u j d", u=64)
            gld.inc(g.dma_start(out=wva[0:64, :, :], in_=srcv), 16)
            gld.inc(g.dma_start(out=wvb[64:128, :, :], in_=srcv), 16)
        else:
            gld.inc(g.dma_start(out=wk1_s[:], in_=wk1_d.rearrange("(lc p) o -> p lc o", p=128)), 16)
            gld.inc(g.dma_start(out=wv1_s[:], in_=wv1_d.rearrange("(lc p) o -> p lc o", p=128)), 16)
            gld.inc(g.dma_start(out=cos_s[0:64, :], in_=cos_d[:]), 16)
            gld.inc(g.dma_start(out=cos_s[64:128, :], in_=cos_d[:]), 16)
            gld.inc(g.dma_start(out=sin_s[0:64, :], in_=sin_d[:]), 16)
            gld.inc(g.dma_start(out=sin_s[64:128, :], in_=sin_d[:]), 16)
    gld_n = gld.n

    @blk.sync
    def _(sp):
        for r in range(NCORE):
            if layer == 0:
                for h in range(8):
                    p0 = 64 * (h % 2)
                    sld.inc(sp.dma_start(out=lat_s[p0:p0 + 32, h // 2, T * r:T * (r + 1)],
                                         in_=latg_d[512 * r + 32 * h:512 * r + 32 * h + 32, :]), 16)
                    sld.inc(sp.dma_start(out=lat_s[p0 + 32:p0 + 64, h // 2, T * r:T * (r + 1)],
                                         in_=latg_d[512 * r + 256 + 32 * h:512 * r + 256 + 32 * h + 32, :]), 16)
            else:
                sld.inc(sp.dma_start(
                    out=lat_s[:, :, T * r:T * (r + 1)],
                    in_=latg_d[512 * r:512 * (r + 1), :].rearrange("(lc p) t -> p lc t", p=128)), 16)
        if layer == 0:
            sld.inc(sp.dma_start(out=mask_s[:], in_=mask_d.rearrange("a p q -> p a q")), 16)
        sld.inc(sp.dma_start(out=ident_s[:], in_=id_d[:]), 16)
    sld_n = sld.n

    @blk.vector
    def _(v):
        ch = Chain(v, sm["dvm"])
        ch.op(v.memset(v_s[:, :, 128:129], 1.0))
        ch.op(v.memset(ones_bf[:], 1.0))
        ch.op(v.memset(ones_f[:], 1.0))
    memset_n = sm["dvm"].n

    NH = 4
    lat_r = lat_s.rearrange("p c (m j) -> p c m j", j=8) if layer == 0 else None

    def emit_kv_build(hh):
        kb = sm["kb"]; ke = sm["ke"]; vb = sm["vb"]; ve = sm["ve"]
        kb_base = kb.n
        ke_base = ke.n

        @blk.tensor
        def _(t, hh=hh):
            t.wait_ge(gld.h, gld_n)
            t.wait_ge(sld.h, sld_n)
            if layer == 1 and hh > 0:
                t.wait_ge(sm["sE"].h, sm["sE"].n)
            for tb in range(16):
                if tb >= 4:
                    t.wait_ge(ke.h, ke_base + (tb - 4) + 1)
                tile = psS_tiles[tb % 4]
                if layer == 0:
                    h = tb // 2
                    p0 = 64 * (h % 2)
                    wk_t = wka if h % 2 == 0 else wkb
                    for j in range(4):
                        inst = t.matmul(tile, wk_t[p0:p0 + 64, j, :],
                                        lat_r[p0:p0 + 64, h // 2, 512 * (tb % 2):512 * (tb % 2 + 1), j],
                                        start=(j == 0), stop=(j == 3), skip_group_check=True)
                else:
                    for lc in range(4):
                        inst = t.matmul(tile, wk1_s[:, lc, 128 * hh:128 * (hh + 1)],
                                        lat_s[:, lc, 512 * tb:512 * (tb + 1)],
                                        start=(lc == 0), stop=(lc == 3), skip_group_check=True)
                kb.inc(inst)
        kb_now = kb.n

        @blk.scalar
        def _(s):
            for tb in range(16):
                s.wait_ge(kb.h, kb_base + tb + 1)
                ke.inc(s.copy(k_fm[:, 512 * tb:512 * (tb + 1)], psS_tiles[tb % 4]))
        ke_now = ke.n

        vb_base = vb.n
        ve_base = ve.n

        nrm_prev = sm["nrm"].n

        vorder = [0, 2, 1, 3]

        @blk.tensor
        def _(t, hh=hh):
            if layer == 1 and hh > 0:
                t.wait_ge(sm["nrm"].h, nrm_prev)
            for i in range(64):
                if i >= 2:
                    t.wait_ge(ve.h, ve_base + (i - 2) + 1)
                tile = psav_tiles[vorder[i % 4]]
                if layer == 0:
                    h = i // 8
                    ms = i % 8
                    p0 = 64 * (h % 2)
                    wv_t = wva if h % 2 == 0 else wvb
                    for j in range(4):
                        inst = t.matmul(tile,
                                        lat_r[p0:p0 + 64, h // 2, 128 * ms:128 * (ms + 1), 4 + j],
                                        wv_t[p0:p0 + 64, j, :],
                                        start=(j == 0), stop=(j == 3), skip_group_check=True)
                else:
                    for lc in range(4):
                        inst = t.matmul(tile, lat_s[:, lc, 128 * i:128 * (i + 1)],
                                        wv1_s[:, lc, 128 * hh:128 * (hh + 1)],
                                        start=(lc == 0), stop=(lc == 3), skip_group_check=True)
                vb.inc(inst)

        @blk.vector
        def _(v):
            for i in range(64):
                v.wait_ge(vb.h, vb_base + i + 1)
                ve.inc(v.tensor_copy(v_s[:, i, 0:128], psav_tiles[vorder[i % 4]]))
        return ke_now, ve.n

    def emit_rope_k(hh):
        rotk = sm["rotk"]; rpd = sm["rpd"]; ke = sm["ke"]
        ke_now = ke.n
        for cc in range(S // CH):
            rot_base = rotk.n

            @blk.sync
            def _(sp, cc=cc):
                sp.wait_ge(ke.h, ke_now)
                if cc > 0:
                    sp.wait_ge(rpd.h, rpd.n)
                sl = slice(CH * cc, CH * (cc + 1))
                rotk.inc(sp.dma_start(out=krot[0:64, :], in_=k_fm[64:128, sl]), 16)
                rotk.inc(sp.dma_start(out=krot[64:128, :], in_=k_fm[0:64, sl]), 16)
            rot_now = rotk.n

            @blk.vector
            def _(v, cc=cc):
                ch = Chain(v, sm["chd"])
                v.wait_ge(rotk.h, rot_now)
                v.wait_ge(gld.h, gld_n)
                sl = slice(CH * cc, CH * (cc + 1))
                ch.op(v.tensor_scalar_mul(krot[0:64, :], krot[0:64, :], -1.0))
                ch.op(v.tensor_mul(tA[:], k_fm[:, sl], cos_s[:, sl]))
                ch.wait()
                ch.op(v.tensor_mul(tB[:], krot[:], sin_s[:, sl]))
                ch.wait()
                ch.op(v.tensor_add(k_fm[:, sl], tA[:], tB[:]), rpd)

    def emit_norms(sq_src, n_tiles, out_slot, waits):
        nq = sm["nq"]; nmx = sm["nmx"]; acn = sm["acn"]
        acn_base = acn.n

        @blk.scalar
        def _(s):
            for sem_, val_ in waits:
                s.wait_ge(sem_.h, val_)
            acn.inc(s.activation(out=sq_s[:], in_=sq_src[:], func=AF.Square))
        acn_now = acn.n
        nq_base = nq.n
        nmx_base = nmx.n

        @blk.tensor
        def _(t):
            t.wait_ge(acn.h, acn_now)
            for tb in range(16):
                if tb >= 4:
                    t.wait_ge(nmx.h, nmx_base + (tb - 4) + 1)
                else:
                    t.wait_ge(nmx.h, nmx_base)
                nq.inc(t.matmul(psS_tiles[tb % 4][0:1, :], ones_bf[:],
                                sq_s[:, 512 * tb:512 * (tb + 1)],
                                start=True, stop=True, skip_group_check=True))

        @blk.vector
        def _(v):
            ch = Chain(v, nmx)
            for tb in range(16):
                v.wait_ge(nq.h, nq_base + tb + 1)
                ch.op(v.reduce_max(nrow[0:1, 32 + tb:33 + tb], psS_tiles[tb % 4][0:1, :],
                                   axis=AX.X))
            ch.wait()
            ch.op(v.reduce_max(nrow[0:1, out_slot:out_slot + 1],
                               nrow[0:1, 32:48], axis=AX.X))
        return nmx.n

    def emit_M(kslot, qslot, waits):
        chd = sm["chd"]; cha = sm["cha"]; mcl = sm["mcl"]; acn = sm["acn"]
        nq = sm["nq"]

        @blk.vector
        def _(v):
            ch = Chain(v, chd)
            for sem_, val_ in waits:
                v.wait_ge(sem_.h, val_)
            ch.op(v.tensor_mul(nrow[0:1, 50:51], nrow[0:1, kslot:kslot + 1],
                               nrow[0:1, qslot:qslot + 1]))
        chd_1 = chd.n

        @blk.scalar
        def _(s):
            s.wait_ge(chd.h, chd_1)
            acn.inc(s.activation(out=nrow[0:1, 51:52], in_=nrow[0:1, 50:51], func=AF.Ln))
        acn_1 = acn.n

        @blk.vector
        def _(v):
            ch = Chain(v, chd)
            v.wait_ge(acn.h, acn_1)
            ch.op(v.tensor_scalar(nrow[0:1, 52:53], nrow[0:1, 51:52], 0.5, LOG_SCALE,
                                  OP.mult, OP.add))
        chd_2 = chd.n

        @blk.scalar
        def _(s):
            s.wait_ge(chd.h, chd_2)
            acn.inc(s.activation(out=nrow[0:1, 53:54], in_=nrow[0:1, 52:53], func=AF.Exp))
        acn_2 = acn.n

        @blk.vector
        def _(v):
            ch = Chain(v, chd)
            v.wait_ge(acn.h, acn_2)
            ch.op(v.tensor_scalar_mul(nrow[0:1, 54:55], nrow[0:1, 53:54], -1.0), sm["dvm"])
        dvm_1 = sm["dvm"].n

        mcl_prev = mcl.n
        sE_prev = sm["sE"].n

        @blk.tensor
        def _(t):
            t.wait_ge(sm["dvm"].h, dvm_1)
            t.wait_ge(mcl.h, mcl_prev)
            nq.inc(t.matmul(ps_n[:, 0:1], ones_f[:], nrow[0:1, 54:55],
                            start=True, stop=True, skip_group_check=True))
        nq_1 = sm["nq"].n

        @blk.vector
        def _(v):
            v.wait_ge(sm["nq"].h, nq_1)
            v.wait_ge(sm["sE"].h, sE_prev)
            mcl.inc(v.tensor_copy(mcol[:], ps_n[:, 0:1]))
        return mcl.n

    def units_for(qc):
        kbs = range(qc + 1) if layer == 0 else range(16)
        return [(kb, ksp) for kb in kbs for ksp in range(2)]

    def emit_sdpa_head(h, mcl_mark):
        sS = sm["sS"]; sE = sm["sE"]; sMm = sm["sM"]; sAV = sm["sAV"]
        nrm = sm["nrm"]; trP = sm["trP"]; ocp = sm["ocp"]
        sS_base = sS.n
        sE_base = sE.n
        sM_base = sMm.n
        sAV_base = sAV.n
        nrm_base = nrm.n
        trP_base = trP.n
        ocp_base = ocp.n
        st_base = st.n
        psS_par = [ps_S0, ps_S1]
        pT_par = [pT0, pT1]

        all_units = []
        for qc in range(16):
            for u in units_for(qc):
                all_units.append((qc,) + u)
        NU = len(all_units)
        qc_first = {}
        qc_last = {}
        for i, (qc, kb, ksp) in enumerate(all_units):
            qc_first.setdefault(qc, i)
            qc_last[qc] = i
        diag_units = {i for i, (qc, kb, ksp) in enumerate(all_units)
                      if layer == 0 and kb == qc}
        sM_cum = []
        acc = 0
        for i in range(NU):
            if i in diag_units:
                acc += 2
            sM_cum.append(acc)

        def emit_av(t, i):
            qc, kb, ksp = all_units[i]
            if i == qc_first[qc] and qc > 0:
                t.wait_ge(nrm.h, nrm_base + 4 * qc)
            t.wait_ge(sE.h, sE_base + i + 1)
            for j in range(2):
                ks = 4 * kb + 2 * ksp + j
                first = (i == qc_first[qc]) and (j == 0)
                last = (i == qc_last[qc]) and (j == 1)
                for qs in range(4):
                    inst = t.matmul(
                        (ps_av0 if qs < 2 else ps_av1)[:, qs % 2, 0:129],
                        pT_par[i % 2][:, j, 128 * qs:128 * (qs + 1)],
                        v_s[:, ks, 0:129],
                        start=(first and qs % 2 == 0), stop=last,
                        skip_group_check=True)
            sAV.inc(inst)

        def emit_transposes(t, qc):
            for qs in range(4):
                t.wait_ge(nrm.h, nrm_base + 4 * qc + qs + 1)
                if trP.n - trP_base > 0:
                    t.wait_ge(ocp.h, ocp_base + (trP.n - trP_base))
                trP.inc(t.transpose(ps_tr[:], ot[:, qs, :], ident_s[:]))

        @blk.tensor
        def _(t):
            for i, (qc, kb, ksp) in enumerate(all_units):
                if i < 2:
                    t.wait_ge(sm["nmx"].h, sm["nmx"].n)
                    t.wait_ge(sm["mcl"].h, mcl_mark)
                    t.wait_ge(sm["ve"].h, sm["ve"].n)
                    t.wait_ge(sm["ke"].h, sm["ke"].n)
                    if layer == 1:
                        t.wait_ge(sm["rpd"].h, sm["rpd"].n)
                for j in range(2):
                    ks = 4 * kb + 2 * ksp + j
                    inst = t.matmul(psS_par[i % 2][:, j, :],
                                    k_fm[:, 128 * ks:128 * (ks + 1)],
                                    q_s[:, 512 * qc:512 * (qc + 1)],
                                    start=True, stop=True, skip_group_check=True)
                sS.inc(inst)
                if i >= 1:
                    emit_av(t, i - 1)
                if i == NU - 1:
                    emit_av(t, i)
                prev_done = all_units[i - 1][0] if i >= 1 else None
                if i >= 1 and (i - 1) == qc_last[all_units[i - 1][0]]:
                    emit_transposes(t, all_units[i - 1][0])
                if i == NU - 1:
                    emit_transposes(t, 15)

        @blk.scalar
        def _(s):
            s.wait_ge(sm["mcl"].h, mcl_mark)
            for i, (qc, kb, ksp) in enumerate(all_units):
                if i in diag_units:
                    s.wait_ge(sMm.h, sM_base + sM_cum[i])
                else:
                    s.wait_ge(sS.h, sS_base + i + 1)
                if i >= 2:
                    s.wait_ge(sAV.h, sAV_base + (i - 2) + 1)
                sE.inc(s.activation(out=pT_par[i % 2][:],
                                    in_=psS_par[i % 2][:],
                                    func=AF.Exp, bias=mcol[:, 0:1], scale=SCALE))
        self_sE = sE.n

        @blk.vector
        def _(v):
            ch = Chain(v, nrm)
            if layer == 0:
                v.wait_ge(sld.h, sld_n)
            for i, (qc, kb, ksp) in enumerate(all_units):
                if i in diag_units:
                    v.wait_ge(sS.h, sS_base + i + 1)
                    for j in range(2):
                        sMm.inc(v.tensor_add(psS_par[i % 2][:, j, :],
                                             psS_par[i % 2][:, j, :],
                                             mask_s[:, 2 * ksp + j, :]))
                if i == qc_last[qc]:
                    v.wait_ge(sAV.h, sAV_base + i + 1)
                    if qc > 0:
                        v.wait_ge(st.h, st_base + 16 * qc)
                        v.wait_ge(trP.h, trP_base + 4 * qc)
                    for qs in range(4):
                        pav = (ps_av0 if qs < 2 else ps_av1)
                        ch.wait()
                        ch.op(v.reciprocal(rl_s[:, qs % 2:qs % 2 + 1],
                                           pav[:, qs % 2, 128:129]), sm["dvm"])
                        ch.wait()
                        ch.op(v.tensor_scalar_mul(ot[:, qs, :], pav[:, qs % 2, 0:128],
                                                  rl_s[:, qs % 2:qs % 2 + 1]))
                    for qs in range(4):
                        v.wait_ge(trP.h, trP_base + 4 * qc + qs + 1)
                        sm["ocp"].inc(v.tensor_copy(ostage[:, 128 * qs:128 * (qs + 1)],
                                                    ps_tr[:]))

        @blk.sync
        def _(sp):
            for qc in range(16):
                sp.wait_ge(sm["ocp"].h, ocp_base + 4 * (qc + 1))
                tb2 = qc // 2
                col = 512 * (qc % 2)
                st.inc(sp.dma_start(
                    out=out_d[512 * tb2 + 128 * h:512 * tb2 + 128 * (h + 1),
                              col:col + 512],
                    in_=ostage[:]), 16)

        return dict(sE=self_sE)

    if layer == 0:
        emit_kv_build(0)
        emit_norms(k_fm, 16, 48,
                   waits=[(sm["ke"], sm["ke"].n), (sm["ve"], sm["ve"].n)])
    for h in range(NH):
        sE_prev = sm["sE"].n

        if "qsel" in d:
            @blk.gpsimd
            def _(g, h=h, sE_prev=sE_prev):
                g.wait_ge(sm["sE"].h, sE_prev)
                g.wait_ge(gld.h, gld_n)
                for r in range(NCORE):
                    sm["qld"].inc(g.indirect_dma_start(
                        q_s[:, T * r:T * (r + 1)], None, qTg_d[:, :],
                        IndirectOffsetOnAxis(qsel_s[:, 8 * h + r:8 * h + r + 1], 0)), 16)
        else:
            @blk.sync
            def _(sp, h=h, sE_prev=sE_prev):
                sp.wait_ge(sm["sE"].h, sE_prev)
                for r in range(NCORE):
                    sm["qld"].inc(sp.dma_start(
                        out=q_s[:, T * r:T * (r + 1)],
                        in_=qTg_d[512 * r + 128 * h:512 * r + 128 * (h + 1), :]), 16)
        qld_target = sm["qld"].n

        if layer == 1:
            emit_kv_build(h)
            emit_rope_k(h)
            emit_norms(k_fm, 16, 48,
                       waits=[(sm["rpd"], sm["rpd"].n), (sm["ke"], sm["ke"].n),
                              (sm["nq"], sm["nq"].n), (sm["chd"], sm["chd"].n)])
        emit_norms(q_s, 16, 49,
                   waits=[(sm["qld"], qld_target), (sm["nq"], sm["nq"].n),
                          (sm["chd"], sm["chd"].n)])
        mcl_mark = emit_M(48, 49, waits=[(sm["nmx"], sm["nmx"].n)])
        emit_sdpa_head(h, mcl_mark)



def build_B(layer):
    P = Prog()
    d = {"qTg": P.din("qTg", [DM, T], BF16),
         "latg": P.din("latg", [NCORE * 512, T], BF16),
         "ident": P.din("ident", [128, 128], BF16),
         "outT": P.dout("outT", [DM, T], BF16)}
    if layer == 0:
        d.update(wkupT=P.din("wkupT", [256, 128], F32),
                 wvupT=P.din("wvupT", [256, 128], F32),
                 masks=P.din("masks", [4, 128, 512], F32))
    else:
        d.update(wk1=P.din("wk1", [512, 512], F32),
                 wv1=P.din("wv1", [512, 512], F32),
                 cosT=P.din("cosT", [64, S], F32),
                 sinT=P.din("sinT", [64, S], F32))
    sm = make_sems(P, B_SEMS, nslot=False)
    blk = P.ctx.enter_context(P.nc.Block())
    emit_B(P, blk, sm, layer, d, P.ctx)

    @blk.gpsimd
    def _(g):
        g.wait_ge(sm["st"].h, sm["st"].n)

    return P


def run_B(layer, qT_list, latT_list, w):
    P = build_B(layer)
    cosT, sinT, masks, ident = host_consts()
    in_maps = []
    latg = np.ascontiguousarray(np.concatenate(latT_list, axis=0))
    for c in range(NCORE):
        qTg = np.ascontiguousarray(np.concatenate(
            [qT_list[r][512 * c:512 * (c + 1), :] for r in range(NCORE)], axis=0))
        m = {"qTg": qTg, "latg": latg, "ident": ident}
        if layer == 0:
            m.update(wkupT=w["wkupT"][c], wvupT=w["wvupT"][c], masks=masks)
        else:
            m.update(wk1=w["wk1"][c], wv1=w["wv1"][c], cosT=cosT, sinT=sinT)
        in_maps.append(m)
    res = run_bass_kernel_spmd(P.nc, in_maps, core_ids=list(range(NCORE)))
    return [np.asarray(r["outT"]) for r in res.results]



C_SEMS = ["ald", "wld", "wld1", "pe", "evac", "st"]


def emit_C(P, blk, sm, layer, d, scope, tag=""):
    nc = P.nc
    attn_d = d["outg"]
    wo_d = d["woT"]
    xp_d = d["xp"]
    xout_d = d["xout"]

    def SB(name, shape, dt):
        return scope.enter_context(nc.sbuf_tensor(name + tag, shape, dt))

    at_s = SB("at_s", [128, KC, T], BF16)
    w_pan = SB("w_pan", [128, 2, KC, 256], BF16)
    stagef = SB("stagef", [128, 4, T], F32)
    xf = SB("xf", [128, 4, T], F32)

    xls = sm["xls"]

    if "csel" in d:
        csel_s = SB("csel_s", [128, 32], mybir.dt.int32)

        @blk.gpsimd
        def _(g):
            sm["ald"].inc(g.dma_start(out=csel_s[:], in_=d["csel"][:]), 16)
            g.wait_ge(sm["ald"].h, sm["ald"].n)
            for kc in range(KC):
                sm["ald"].inc(g.indirect_dma_start(
                    at_s[:, kc, :], None, attn_d[:, :],
                    IndirectOffsetOnAxis(csel_s[:, kc:kc + 1], 0)), 16)
    else:
        @blk.sync
        def _(sp):
            for r in range(NCORE):
                sm["ald"].inc(sp.dma_start(
                    out=at_s[:, 4 * r:4 * (r + 1), :],
                    in_=attn_d[512 * r:512 * (r + 1), :].rearrange("(a p) t -> p a t", p=128)), 16)
    ald_n = sm["ald"].n
    st = sm["st"]
    st_marks = [st.n]
    ev_marks = [sm["evac"].n]

    with nc.psum_tensor("ps_proj" + tag, [128, 4, T], F32) as ps_proj:
        def wo_writer(ocg, mm_snap):
            par = ocg % 2

            @blk.sync
            def _(sp, ocg=ocg):
                sp.wait_ge(sm["evac"].h, ev_marks[ocg])
                for ol in range(2):
                    og = 2 * ocg + ol
                    xls[og % 4].inc(sp.dma_start(out=xf[:, og % 4, :],
                                                 in_=xp_d[128 * og:128 * (og + 1), :]), 16)
            marks = tuple(xls[i].n for i in range(4))

            @blk.vector
            def _(v, ocg=ocg, mm_snap=mm_snap, marks=marks, par=par):
                v.wait_ge(sm["pe"].h, mm_snap)
                v.wait_ge(st.h, st_marks[ocg])
                for ol in range(2):
                    og = 2 * ocg + ol
                    v.wait_ge(xls[og % 4].h, marks[og % 4])
                    sm["evac"].inc(v.tensor_add(stagef[:, og % 4, :],
                                                ps_proj[:, 2 * par + ol, :],
                                                xf[:, og % 4, :]))
            ev_now = sm["evac"].n
            ev_marks.append(sm["evac"].n)

            @blk.sync
            def _(sp, ocg=ocg, ev_now=ev_now):
                sp.wait_ge(sm["evac"].h, ev_now)
                for ol in range(2):
                    og = 2 * ocg + ol
                    st.inc(sp.dma_start(out=xout_d[128 * og:128 * (og + 1), :],
                                        in_=stagef[:, og % 4, :]), 16)
            st_marks.append(st.n)

        emit_proj(P, blk, w_d=wo_d, n_ocg=16, xn_s=at_s, sm=sm, w_pan=w_pan,
                  ps_proj=ps_proj, writer=wo_writer,
                  xn_total=[(sm["ald"], ald_n)],
                  prev_pe_mark=sm["pe"].n, prev_evac_mark=sm["evac"].n)


def build_C():
    P = Prog()
    d = {"outg": P.din("outg", [DM, T], BF16),
         "woT": P.din("woT", [DM, DM], F32),
         "xp": P.din("xp", [DM, T], F32),
         "xout": P.dout("xoutT", [DM, T], F32)}
    sm = make_sems(P, C_SEMS, nslot=False)
    sm["xls"] = [P.sem(f"xls{i}") for i in range(4)]
    blk = P.ctx.enter_context(P.nc.Block())
    emit_C(P, blk, sm, 0, d, P.ctx)

    @blk.gpsimd
    def _(g):
        g.wait_ge(sm["st"].h, sm["st"].n)

    return P


def run_C(layer, outT_list, xp_cols, w):
    P = build_C()
    in_maps = []
    for c in range(NCORE):
        outg = np.ascontiguousarray(np.concatenate(
            [outT_list[r][512 * c:512 * (c + 1), :] for r in range(NCORE)], axis=0))
        in_maps.append({"outg": outg,
                        "woT": w["woT0"] if layer == 0 else w["woT1"],
                        "xp": xp_cols[c]})
    res = run_bass_kernel_spmd(P.nc, in_maps, core_ids=list(range(NCORE)))
    return [np.asarray(r["xoutT"]) for r in res.results]



def kernel(**inputs):
    if _UNFUSED:
        return kernel_unfused(**inputs)
    return kernel_fused(**inputs)


def kernel_unfused(**inputs):
    w = prep_weights(inputs)
    x = np.asarray(inputs["L0_attn_hidden_states"])[0]
    xT = np.ascontiguousarray(x.T)
    out_cols = [np.ascontiguousarray(xT[:, T * c:T * (c + 1)]) for c in range(NCORE)]

    for layer in (0, 1):
        qT, latT, xp = run_A(layer, out_cols, w)
        outT = run_B(layer, qT, latT, w)
        out_cols = run_C(layer, outT, xp, w)

    x2T = np.concatenate(out_cols, axis=1)
    return np.ascontiguousarray(x2T.T).reshape(1, S, DM)



def _all_sem_objs(sm):
    out = []
    for v in sm.values():
        if isinstance(v, list):
            out.extend(v)
        elif isinstance(v, Sem):
            out.append(v)
    return out


def emit_barrier(P, blk, sm):
    bar = sm["bar"]

    @blk.gpsimd
    def _(g):
        for s_ in _all_sem_objs(sm):
            if s_ is bar or s_.n == 0:
                continue
            g.wait_ge(s_.h, s_.n)
        bar.inc(g.nop())
    tgt = bar.n

    for en in ("tensor", "vector", "scalar", "sync"):
        @getattr(blk, en)
        def _(e):
            e.wait_ge(bar.h, tgt)


def build_fused():
    P = Prog()
    nc = P.nc
    xT_in = P.din("xT", [DM, T], F32)
    wq0 = P.din("wqT0", [DM, DM], F32)
    wlat0 = P.din("wlatT0", [DM, 512], F32)
    wb0 = P.din("wbT0", [DM, DM], F32)
    bias0 = P.din("biasb0", [128, KC], F32)
    ar0 = P.din("arT0", [DM, 80], F32)
    btr0 = P.din("BTr0", [64, 16, DM], F32)
    identF = P.din("identF", [128, 128], BF16)
    wku = P.din("wkupT", [256, 128], F32)
    wvu = P.din("wvupT", [256, 128], F32)
    masks = P.din("masks", [4, 128, 512], F32)
    wo0 = P.din("woT0", [DM, DM], F32)
    wq1 = P.din("wqT1", [DM, DM], F32)
    wlat1 = P.din("wlatT1", [DM, 512], F32)
    tern = P.din("ternT", [DM, DM], I8)
    scl = P.din("scalesT", [KC, DM], F32)
    cosq = P.din("cosq", [64, T], F32)
    sinq = P.din("sinq", [64, T], F32)
    wk1 = P.din("wk1", [512, 512], F32)
    wv1 = P.din("wv1", [512, 512], F32)
    cosT_d = P.din("cosT", [64, S], F32)
    sinT_d = P.din("sinT", [64, S], F32)
    wo1 = P.din("woT1", [DM, DM], F32)
    qsel_d = P.din("qsel", [128, 32], mybir.dt.int32)
    csel_d = P.din("csel", [128, 32], mybir.dt.int32)
    dint = lambda nm_, shp, dt: nc.dram_tensor(nm_, shp, dt, kind="Internal").ap()
    qT_loc = dint("qT_loc", [DM, T], BF16)
    lat_loc = dint("lat_loc", [512, T], BF16)
    out_loc = dint("out_loc", [DM, T], BF16)
    qTg = dint("qTg", [NCORE * DM, T], BF16)
    latg = dint("latg", [NCORE * 512, T], BF16)
    outg = dint("outg", [NCORE * DM, T], BF16)
    xp_scr = dint("xp_scr", [DM, T], F32)
    x1_scr = dint("x1_scr", [DM, T], F32)
    x2T = P.dout("x2T", [DM, T], F32)

    sem_names = sorted(set(A_SEMS + B_SEMS + C_SEMS + ["cc", "bar"]))
    sm = make_sems(P, sem_names)
    sm["xls"] = [P.sem(f"xls{i}") for i in range(4)]
    blk = P.ctx.enter_context(nc.Block())
    cc = sm["cc"]



    def collectives(pairs):
        @blk.gpsimd
        def _(g):
            for kind, src_, dst_ in pairs:
                cc.inc(g.collective_compute(
                    kind, OP.bypass, [list(range(NCORE))], [src_.opt()], [dst_.opt()]), 1)
        tgt = cc.n

        @blk.gpsimd
        def _(g):
            g.wait_ge(cc.h, tgt)

    for layer in (0, 1):
        tagA = f"_a{layer}"
        dA = {"xT": xT_in if layer == 0 else x1_scr,
              "qT": qT_loc, "lat": lat_loc, "xp": xp_scr}
        if layer == 0:
            dA.update(wqT=wq0, wlatT=wlat0, wbT=wb0, biasb=bias0, arT=ar0,
                      BTr=btr0, identF=identF)
        else:
            dA.update(wqT=wq1, wlatT=wlat1, ternT=tern, scalesT=scl,
                      cosq=cosq, sinq=sinq)
        def ag_hook(qn, latn):
            @blk.gpsimd
            def _(g):
                g.wait_ge(sm["st"].h, qn)
                cc.inc(g.collective_compute(
                    "AllGather", OP.bypass, [list(range(NCORE))],
                    [qT_loc.opt()], [qTg.opt()]), 1)
                g.wait_ge(sm["st"].h, latn)
                cc.inc(g.collective_compute(
                    "AllGather", OP.bypass, [list(range(NCORE))],
                    [lat_loc.opt()], [latg.opt()]), 1)

        sA = ExitStack()
        emit_A(P, blk, sm, layer, dA, sA, tag=tagA, ag_hook=ag_hook)
        sA.close()
        emit_barrier(P, blk, sm)

        dB = {"qTg": qTg, "latg": latg, "ident": identF, "outT": out_loc,
              "qsel": qsel_d}
        if layer == 0:
            dB.update(wkupT=wku, wvupT=wvu, masks=masks)
        else:
            dB.update(wk1=wk1, wv1=wv1, cosT=cosT_d, sinT=sinT_d)
        sB = ExitStack()
        emit_B(P, blk, sm, layer, dB, sB, tag=f"_b{layer}")
        sB.close()

        @blk.gpsimd
        def _(g):
            g.wait_ge(sm["st"].h, sm["st"].n)
            cc.inc(g.collective_compute(
                "AllGather", OP.bypass, [list(range(NCORE))],
                [out_loc.opt()], [outg.opt()]), 1)
        emit_barrier(P, blk, sm)

        dC = {"outg": outg, "woT": wo0 if layer == 0 else wo1, "xp": xp_scr,
              "xout": x1_scr if layer == 0 else x2T, "csel": csel_d}
        sC = ExitStack()
        emit_C(P, blk, sm, layer, dC, sC, tag=f"_c{layer}")
        sC.close()
        if layer == 0:
            emit_barrier(P, blk, sm)

    @blk.gpsimd
    def _(g):
        g.wait_ge(sm["st"].h, sm["st"].n)

    return P


def _qsel(c):
    p = np.arange(128, dtype=np.int64)
    out = np.zeros((128, 32), dtype=np.int32)
    for h in range(4):
        for r in range(8):
            out[:, 8 * h + r] = 4096 * r + 512 * c + 128 * h + p
    return out


def _csel(c):
    p = np.arange(128, dtype=np.int64)
    out = np.zeros((128, 32), dtype=np.int32)
    for r in range(8):
        for a in range(4):
            out[:, 4 * r + a] = 4096 * r + 512 * c + 128 * a + p
    return out


_FUSED_CACHE = {}


def kernel_fused(**inputs):
    w = prep_weights(inputs)
    cosT, sinT, masksv, ident = host_consts()
    x = np.asarray(inputs["L0_attn_hidden_states"])[0]
    xT = np.ascontiguousarray(x.T)

    if "P" not in _FUSED_CACHE:
        _FUSED_CACHE["P"] = build_fused()
    P = _FUSED_CACHE["P"]

    in_maps = []
    for c in range(NCORE):
        m = {"xT": np.ascontiguousarray(xT[:, T * c:T * (c + 1)]),
             "wqT0": w["wqT0"], "wlatT0": w["wlatT0"], "wbT0": w["wbT0"],
             "biasb0": w["biasb0"], "arT0": w["arT0"], "BTr0": w["BTr0"],
             "identF": ident, "wkupT": w["wkupT"][c], "wvupT": w["wvupT"][c],
             "masks": masksv, "woT0": w["woT0"],
             "wqT1": w["wqT1"], "wlatT1": w["wlatT1"], "ternT": w["ternT"],
             "scalesT": w["scalesT"],
             "cosq": np.ascontiguousarray(cosT[:, T * c:T * (c + 1)]),
             "sinq": np.ascontiguousarray(sinT[:, T * c:T * (c + 1)]),
             "wk1": w["wk1"][c], "wv1": w["wv1"][c],
             "cosT": cosT, "sinT": sinT, "woT1": w["woT1"],
             "qsel": _qsel(c), "csel": _csel(c)}
        in_maps.append(m)
    res = run_bass_kernel_spmd(P.nc, in_maps, core_ids=list(range(NCORE)))
    x2T_full = np.concatenate([np.asarray(r["x2T"]) for r in res.results], axis=1)
    return np.ascontiguousarray(x2T_full.T).reshape(1, S, DM)


# version 06
# speedup vs baseline: 1.1008x; 1.1008x over previous
import os
import numpy as np
import ml_dtypes
from contextlib import ExitStack

_A_STOP = os.environ.get("A_STOP", "")
_UNFUSED = os.environ.get("MK_UNFUSED", "") == "1"

import concourse.bass as bass
import concourse.mybir as mybir
from concourse.bass_utils import run_bass_kernel_spmd
from concourse.bass_core_rust import IndirectOffsetOnAxis

F32 = mybir.dt.float32
BF16 = mybir.dt.bfloat16
I8 = mybir.dt.int8
AF = mybir.ActivationFunctionType
OP = mybir.AluOpType
AX = mybir.AxisListType

NCORE = 8
S = 8192
DM = 4096
T = 1024
KC = 32
TH = T // 512
SCALE = float(1.0 / np.sqrt(128.0))
LOG_SCALE = float(np.log(1.0 / np.sqrt(128.0)))
LNEPS = 1e-5
NEG = -1e9
BF = ml_dtypes.bfloat16


class Sem:
    __slots__ = ("h", "n")

    def __init__(self, h):
        self.h = h
        self.n = 0

    def inc(self, inst, k=1):
        inst.then_inc(self.h, k)
        self.n += k
        return inst


class Chain:

    def __init__(self, eng, default_sem):
        self.eng = eng
        self.ds = default_sem
        self.pend = None

    def wait(self):
        if self.pend is not None:
            self.eng.wait_ge(self.pend[0].h, self.pend[1])
            self.pend = None

    def op(self, inst, sem=None):
        s = sem if sem is not None else self.ds
        s.inc(inst)
        self.pend = (s, s.n)
        return inst


class Prog:
    def __init__(self):
        self.nc = bass.Bass("TRN2", target_bir_lowering=False)
        self.ctx = ExitStack()

    def sem(self, name):
        return Sem(self.ctx.enter_context(self.nc.semaphore(name)))

    def sbuf(self, name, shape, dt):
        return self.ctx.enter_context(self.nc.sbuf_tensor(name, shape, dt))

    def din(self, name, shape, dt):
        return self.nc.dram_tensor(name, shape, dt, kind="ExternalInput").ap()

    def dout(self, name, shape, dt):
        return self.nc.dram_tensor(name, shape, dt, kind="ExternalOutput").ap()



def emit_ln(P, blk, xT_d, xn_s, b, sm):
    sq = sm["sq"]; pe = sm["pe"]; dv = sm["dv"]; ac = sm["ac"]
    xfd = sm["xfd"]; xfg = sm["xfg"]; xn = sm["xn"]; xng = sm["xng"]
    statd = sm["statd"]; bcp = sm["bcp"]; bcd = sm["bcd"]
    xbs = sm["xbs"]
    xfs = sm["xfs"]
    ps_stat = b["ps_stat"]; ps_bc = b["ps_bc"]; rows = b["rows"]

    @blk.vector
    def _(v):
        v.memset(b["ones_bf"][:], 1.0)
        dv.inc(v.memset(b["ones_f"][:], 1.0))
    ones_dv = dv.n

    pe_base = pe.n
    xb_marks = []

    @blk.gpsimd
    def _(g):
        for kc in range(KC):
            if kc >= 3:
                g.wait_ge(pe.h, pe_base + (kc - 2))
            xbs[kc % 3].inc(g.dma_start(out=b["xb"][:, kc % 3, :],
                                        in_=xT_d[128 * kc:128 * (kc + 1), :]), 16)
            xb_marks.append(xbs[kc % 3].n)

    sq_base = sq.n

    @blk.scalar
    def _(s):
        for kc in range(KC):
            s.wait_ge(xbs[kc % 3].h, xb_marks[kc])
            if kc >= 2:
                s.wait_ge(pe.h, pe_base + (kc - 1))
            sq.inc(s.activation(out=b["xsq"][:, kc % 2, :], in_=b["xb"][:, kc % 3, :],
                                func=AF.Square))

    @blk.tensor
    def _(t):
        t.wait_ge(dv.h, ones_dv)
        for kc in range(KC):
            t.wait_ge(xbs[kc % 3].h, xb_marks[kc])
            t.wait_ge(sq.h, sq_base + (kc + 1))
            insts = []
            for th in range(TH):
                insts.append(t.matmul(ps_stat[:, th, :], b["ones_bf"][:],
                                      b["xb"][:, kc % 3, 512 * th:512 * (th + 1)],
                                      start=(kc == 0), stop=(kc == KC - 1),
                                      skip_group_check=True))
            for th in range(TH):
                insts.append(t.matmul(ps_stat[:, TH + th, :], b["ones_bf"][:],
                                      b["xsq"][:, kc % 2, 512 * th:512 * (th + 1)],
                                      start=(kc == 0), stop=(kc == KC - 1),
                                      skip_group_check=True))
            if kc == KC - 1:
                for i in insts:
                    statd.inc(i)
            else:
                pe.inc(insts[-1])

    chd = sm["chd"]; cha = sm["cha"]

    @blk.vector
    def _(v):
        ch = Chain(v, chd)
        v.wait_ge(statd.h, statd.n)
        ch.op(v.tensor_scalar_mul(rows[:, 0:T], ps_stat[0:1, 0:TH, :], 1.0 / DM))
        ch.op(v.tensor_scalar_mul(rows[:, T:2 * T], ps_stat[0:1, TH:2 * TH, :], 1.0 / DM))
        ch.wait()
        ch.op(v.tensor_mul(rows[:, 2 * T:3 * T], rows[:, 0:T], rows[:, 0:T]))
        ch.wait()
        ch.op(v.tensor_sub(rows[:, T:2 * T], rows[:, T:2 * T], rows[:, 2 * T:3 * T]))
        ch.wait()
        ch.op(v.tensor_scalar_add(rows[:, T:2 * T], rows[:, T:2 * T], LNEPS), dv)
    dv_s0 = dv.n
    chd_stats = chd.n

    @blk.scalar
    def _(s):
        ch = Chain(s, cha)
        s.wait_ge(dv.h, dv_s0)
        ch.op(s.activation(out=rows[:, 2 * T:3 * T], in_=rows[:, T:2 * T], func=AF.Ln))
        ch.wait()
        ch.op(s.activation(out=rows[:, 2 * T:3 * T], in_=rows[:, 2 * T:3 * T],
                           func=AF.Exp, scale=-0.5), ac)
    ac_s = ac.n

    @blk.vector
    def _(v):
        v.wait_ge(ac.h, ac_s)
        v.wait_ge(chd.h, chd_stats)
        dv.inc(v.tensor_mul(rows[:, 3 * T:4 * T], rows[:, 0:T], rows[:, 2 * T:3 * T]))
    dv_s1 = dv.n

    @blk.tensor
    def _(t):
        t.wait_ge(dv.h, dv_s1)
        for th in range(TH):
            bcp.inc(t.matmul(ps_bc[:, 0, 512 * th:512 * (th + 1)], b["ones_f"][:],
                             rows[0:1, 2 * T + 512 * th:2 * T + 512 * (th + 1)],
                             start=True, stop=True, skip_group_check=True))
            bcp.inc(t.matmul(ps_bc[:, 1, 512 * th:512 * (th + 1)], b["ones_f"][:],
                             rows[0:1, 3 * T + 512 * th:3 * T + 512 * (th + 1)],
                             start=True, stop=True, skip_group_check=True))
    bcp_s = bcp.n

    @blk.vector
    def _(v):
        v.wait_ge(bcp.h, bcp_s)
        bcd.inc(v.tensor_copy(b["rstd_bc"][:], ps_bc[:, 0, :]))
        bcd.inc(v.tensor_copy(b["mr_bc"][:], ps_bc[:, 1, :]))
    bcd_s = bcd.n

    xfd_base = xfd.n
    xfg_base = xfg.n

    xf_marks = []

    @blk.sync
    def _(sp):
        for kc in range(KC):
            if kc >= 4:
                if kc % 2 == 0:
                    sp.wait_ge(xfd.h, xfd_base + (kc - 4) // 2 + 1)
                else:
                    sp.wait_ge(xfg.h, xfg_base + (kc - 4) // 2 + 1)
            xfs[kc % 4].inc(sp.dma_start(out=b["xf"][:, kc % 4, :],
                                         in_=xT_d[128 * kc:128 * (kc + 1), :]), 16)
            xf_marks.append(xfs[kc % 4].n)

    @blk.vector
    def _(v):
        prev_sub = None
        for kc in range(0, KC, 2):
            v.wait_ge(xfs[kc % 4].h, xf_marks[kc])
            v.wait_ge(bcd.h, bcd_s)
            if prev_sub is not None:
                v.wait_ge(xn.h, prev_sub)
            xfd.inc(v.tensor_mul(b["t1d"][:], b["xf"][:, kc % 4, :], b["rstd_bc"][:]))
            v.wait_ge(xfd.h, xfd.n)
            xn.inc(v.tensor_sub(xn_s[:, kc, :], b["t1d"][:], b["mr_bc"][:]))
            prev_sub = xn.n

    @blk.gpsimd
    def _(g):
        prev_sub = None
        for kc in range(1, KC, 2):
            g.wait_ge(xfs[kc % 4].h, xf_marks[kc])
            g.wait_ge(bcd.h, bcd_s)
            if prev_sub is not None:
                g.wait_ge(xng.h, prev_sub)
            xfg.inc(g.tensor_mul(b["t1g"][:], b["xf"][:, kc % 4, :], b["rstd_bc"][:]))
            g.wait_ge(xfg.h, xfg.n)
            xng.inc(g.tensor_sub(xn_s[:, kc, :], b["t1g"][:], b["mr_bc"][:]))
            prev_sub = xng.n



def emit_proj(P, blk, *, w_d, n_ocg, xn_s, sm, w_pan, ps_proj, writer, xn_total,
              prev_pe_mark=0, prev_evac_mark=0, panel_extra=None, mm_extra=None,
              mm_extra_wait=None, panel_sem_extra=None):
    lds = [sm["wld"], sm["wld1"]]; pe = sm["pe"]; ev = sm["evac"]
    pe_marks = [prev_pe_mark, prev_pe_mark]
    ev_marks = [prev_evac_mark, prev_evac_mark]

    for ocg in range(n_ocg):
        col0 = 256 * ocg
        par = ocg % 2
        ld = lds[par]

        @blk.gpsimd
        def _(g, ocg=ocg, col0=col0, par=par, ld=ld):
            g.wait_ge(pe.h, pe_marks[ocg])
            if panel_extra is not None:
                panel_extra(g, ocg, col0, par, ld)
            else:
                src_ = w_d[:, col0:col0 + 256].rearrange("(kc p) oc -> p kc oc", p=128)
                for sub in range(2):
                    ld.inc(g.dma_start(out=w_pan[:, par, 16 * sub:16 * sub + 16, :],
                                       in_=src_[:, 16 * sub:16 * sub + 16, :]), 16)
        ld_snap = ld.n
        pse_snap = panel_sem_extra.n if panel_sem_extra is not None else 0

        @blk.tensor
        def _(t, ocg=ocg, ld_snap=ld_snap, pse_snap=pse_snap, par=par, ld=ld):
            t.wait_ge(ld.h, ld_snap)
            if panel_sem_extra is not None:
                t.wait_ge(panel_sem_extra.h, pse_snap)
            for s_, c_ in xn_total:
                t.wait_ge(s_.h, c_)
            if mm_extra_wait is not None:
                t.wait_ge(mm_extra_wait[0].h, mm_extra_wait[1])
            t.wait_ge(ev.h, ev_marks[ocg])
            for ol in range(2):
                for th in range(TH):
                    inst = None
                    for kc in range(KC):
                        inst = t.matmul(ps_proj[:, 2 * par + ol, 512 * th:512 * (th + 1)],
                                        w_pan[:, par, kc, 128 * ol:128 * ol + 128],
                                        xn_s[:, kc, 512 * th:512 * (th + 1)],
                                        start=(kc == 0),
                                        stop=(kc == KC - 1 and mm_extra is None),
                                        skip_group_check=True)
                    if mm_extra is not None:
                        inst = mm_extra(t, ocg, 2 * par + ol, th)
                    pe.inc(inst)
        pe_marks.append(pe.n)

        writer(ocg, pe.n)
        ev_marks.append(ev.n)



A_SEMS = ["sq", "pe", "dv", "ac", "xfd", "xfg", "xn", "xng", "statd",
          "bcp", "bcd", "wld", "wld1", "evac", "st", "aux", "fr", "rscS", "trS", "cpS",
          "mulS", "chd", "cha", "chg", "rot"]


def make_sems(P, names, nslot=True):
    sm = {k: P.sem(k) for k in names}
    if nslot:
        sm["xbs"] = [P.sem(f"xbs{i}") for i in range(3)]
        sm["xfs"] = [P.sem(f"xfs{i}") for i in range(4)]
    return sm


def emit_A(P, blk, sm, layer, d, scope, tag="", ag_hook=None):
    nc = P.nc
    xT_d = d["xT"]
    wq_d = d["wqT"]
    wlat_d = d["wlatT"]
    if layer == 0:
        wb_d = d["wbT"]; bias_d = d["biasb"]; ar_d = d["arT"]
        btr_d = d["BTr"]; id_d = d["identF"]
    else:
        tern_d = d["ternT"]; scl_d = d["scalesT"]
        cosq_d = d["cosq"]; sinq_d = d["sinq"]
    qT_o = d["qT"]; lat_o = d["lat"]; xp_o = d["xp"]

    def SB(name, shape, dt):
        return scope.enter_context(nc.sbuf_tensor(name + tag, shape, dt))

    xn_s = SB("xn_s", [128, KC, T], BF16)
    w_pan = SB("w_pan", [128, 2, KC, 256], BF16)
    stageb = SB("stageb", [128, 4, T], BF16)
    stagef = SB("stagef", [128, 4, T], F32)
    xf = SB("xf", [128, 4, T], F32)
    ones_bf = SB("ones_bf", [128, 1], BF16)
    ones_f = SB("ones_f", [1, 128], F32)
    if layer == 1:
        cosq_s = SB("cosq_s", [128, T], BF16)
        sinq_s = SB("sinq_s", [128, T], BF16)
        qh_buf = SB("qh_buf", [128, T], BF16)
        qrot_buf = SB("qrot_buf", [128, T], BF16)
        tA = SB("tA", [128, T], BF16)
        tB = SB("tB", [128, T], BF16)

    with nc.sbuf_tensor("xb" + tag, [128, 3, T], BF16) as xb, \
         nc.sbuf_tensor("xsq" + tag, [128, 2, T], BF16) as xsq, \
         nc.sbuf_tensor("t1d" + tag, [128, T], F32) as t1d, \
         nc.sbuf_tensor("t1g" + tag, [128, T], F32) as t1g, \
         nc.sbuf_tensor("rstd_bc" + tag, [128, T], F32) as rstd_bc, \
         nc.sbuf_tensor("mr_bc" + tag, [128, T], F32) as mr_bc, \
         nc.sbuf_tensor("rows" + tag, [1, 4 * T], F32) as rows, \
         nc.psum_tensor("ps_stat" + tag, [1, 2 * TH, 512], F32) as ps_stat, \
         nc.psum_tensor("ps_bc" + tag, [128, 2, T], F32) as ps_bc:
        b = dict(xb=xb, xsq=xsq, t1d=t1d, t1g=t1g, rstd_bc=rstd_bc, mr_bc=mr_bc,
                 rows=rows, ones_bf=ones_bf, ones_f=ones_f, xf=xf,
                 ps_stat=ps_stat, ps_bc=ps_bc)
        emit_ln(P, blk, xT_d, xn_s, b, sm)
        xn_total = ((sm["xn"], sm["xn"].n), (sm["xng"], sm["xng"].n))

    st = sm["st"]
    st_marks = [0]

    if _A_STOP == "ln":
        @blk.gpsimd
        def _(g):
            for s_, c_ in xn_total:
                g.wait_ge(s_.h, c_)
        return P

    with nc.psum_tensor("ps_proj" + tag, [128, 4, T], F32) as ps_proj:
        if layer == 1:
            @blk.gpsimd
            def _(g):
                sm["aux"].inc(g.dma_start(out=cosq_s[0:64, :], in_=cosq_d[:]), 16)
                sm["aux"].inc(g.dma_start(out=cosq_s[64:128, :], in_=cosq_d[:]), 16)
                sm["aux"].inc(g.dma_start(out=sinq_s[0:64, :], in_=sinq_d[:]), 16)
                sm["aux"].inc(g.dma_start(out=sinq_s[64:128, :], in_=sinq_d[:]), 16)
            cs_ld = sm["aux"].n

        rope_prev = [None, 0, 0]
        tA_mark = [0]
        tB_mark = [0]

        def lat_writer(ocg, mm_snap):
            par = ocg % 2

            @blk.vector
            def _(v, mm_snap=mm_snap, ocg=ocg, par=par):
                v.wait_ge(sm["pe"].h, mm_snap)
                for ol in range(2):
                    og = 2 * ocg + ol
                    sm["evac"].inc(v.tensor_copy(stageb[:, og, :],
                                                 ps_proj[:, 2 * par + ol, :]))
            ev_now = sm["evac"].n

            @blk.sync
            def _(sp, ev_now=ev_now, ocg=ocg):
                sp.wait_ge(sm["evac"].h, ev_now)
                for ol in range(2):
                    og = 2 * ocg + ol
                    st.inc(sp.dma_start(out=lat_o[128 * og:128 * (og + 1), :],
                                        in_=stageb[:, og, :]), 16)
            st_marks.append(st.n)

        emit_proj(P, blk, w_d=wlat_d, n_ocg=2, xn_s=xn_s, sm=sm, w_pan=w_pan,
                  ps_proj=ps_proj, writer=lat_writer, xn_total=xn_total)
        latst_all = st.n
        if ag_hook is not None:
            ag_hook[0](latst_all)

        def q_writer(ocg, mm_snap):
            par = ocg % 2
            if layer == 0:
                @blk.vector
                def _(v, ocg=ocg, mm_snap=mm_snap, par=par):
                    v.wait_ge(sm["pe"].h, mm_snap)
                    v.wait_ge(st.h, st_marks[ocg + 1])
                    for ol in range(2):
                        og = 2 * ocg + ol
                        sm["evac"].inc(v.tensor_copy(stageb[:, og % 4, :],
                                                     ps_proj[:, 2 * par + ol, :]))
            else:
                rot = sm["rot"]
                for ol in range(2):
                    og = 2 * ocg + ol
                    @blk.vector
                    def _(v, ol=ol, mm_snap=mm_snap, par=par):
                        v.wait_ge(sm["pe"].h, mm_snap)
                        if rope_prev[0] is not None:
                            v.wait_ge(rot.h, rope_prev[0])
                            v.wait_ge(sm["chd"].h, rope_prev[1])
                        sm["evac"].inc(v.tensor_copy(qh_buf[:], ps_proj[:, 2 * par + ol, :]))
                    qh_done = sm["evac"].n

                    @blk.sync
                    def _(sp, qh_done=qh_done):
                        sp.wait_ge(sm["evac"].h, qh_done)
                        if rope_prev[0] is not None:
                            sp.wait_ge(sm["chd"].h, rope_prev[2])
                        rot.inc(sp.dma_start(out=qrot_buf[0:64, :], in_=qh_buf[64:128, :]), 16)
                        rot.inc(sp.dma_start(out=qrot_buf[64:128, :], in_=qh_buf[0:64, :]), 16)
                    rot_now = rot.n

                    @blk.vector
                    def _(v, og=og, rot_now=rot_now, ocg=ocg):
                        ch = Chain(v, sm["chd"])
                        v.wait_ge(rot.h, rot_now)
                        v.wait_ge(sm["aux"].h, cs_ld)
                        v.wait_ge(st.h, st_marks[ocg + 1])
                        ch.op(v.tensor_scalar_mul(qrot_buf[0:64, :], qrot_buf[0:64, :], -1.0))
                        ch.op(v.tensor_mul(tA[:], qh_buf[:], cosq_s[:]))
                        tA_mark[0] = sm["chd"].n
                        ch.wait()
                        ch.op(v.tensor_mul(tB[:], qrot_buf[:], sinq_s[:]))
                        tB_mark[0] = sm["chd"].n
                        ch.wait()
                        ch.op(v.tensor_add(stageb[:, og % 4, :], tA[:], tB[:]), sm["evac"])
                    rope_prev[0] = rot_now
                    rope_prev[1] = tA_mark[0]
                    rope_prev[2] = tB_mark[0]
            ev_now = sm["evac"].n

            @blk.sync
            def _(sp, ocg=ocg, ev_now=ev_now):
                sp.wait_ge(sm["evac"].h, ev_now)
                for ol in range(2):
                    og = 2 * ocg + ol
                    st.inc(sp.dma_start(out=qT_o[128 * og:128 * (og + 1), :],
                                        in_=stageb[:, og % 4, :]), 16)
            st_marks.append(st.n)

        emit_proj(P, blk, w_d=wq_d, n_ocg=16, xn_s=xn_s, sm=sm, w_pan=w_pan,
                  ps_proj=ps_proj, writer=q_writer, xn_total=xn_total,
                  prev_pe_mark=sm["pe"].n, prev_evac_mark=sm["evac"].n)
        qst_all = st.n
        if ag_hook is not None:
            ag_hook[1](qst_all)

    if _A_STOP in ("q", "lat"):
        @blk.gpsimd
        def _(g):
            g.wait_ge(st.h, st.n)
        return P

    ffn_scope = ExitStack()
    if layer == 0:
        ar_pan = ffn_scope.enter_context(nc.sbuf_tensor("ar_pan" + tag, [128, KC, 80], BF16))
        bt_pan = ffn_scope.enter_context(nc.sbuf_tensor("bt_pan" + tag, [64, 2, 16, 256], BF16))
        r_tm = ffn_scope.enter_context(nc.sbuf_tensor("r_tm" + tag, [128, 8, 64], BF16))
        reT = ffn_scope.enter_context(nc.sbuf_tensor("reT" + tag, [64, 16, T], BF16))
        rsc = ffn_scope.enter_context(nc.sbuf_tensor("rsc" + tag, [128, 16, 64], BF16))
        comb = ffn_scope.enter_context(nc.sbuf_tensor("comb" + tag, [128, 16], F32))
        esc = ffn_scope.enter_context(nc.sbuf_tensor("esc" + tag, [128, 48], F32))
        srow = ffn_scope.enter_context(nc.sbuf_tensor("srow" + tag, [128, 12], F32))
        bias_s = ffn_scope.enter_context(nc.sbuf_tensor("bias_s" + tag, [128, KC], F32))
        ident_s = ffn_scope.enter_context(nc.sbuf_tensor("ident_s" + tag, [128, 128], BF16))

        @blk.gpsimd
        def _(g):
            src = ar_d.rearrange("(kc p) a -> p kc a", p=128)
            sm["aux"].inc(g.dma_start(out=ar_pan[:, 0:16, :], in_=src[:, 0:16, :]), 16)
            sm["aux"].inc(g.dma_start(out=ar_pan[:, 16:32, :], in_=src[:, 16:32, :]), 16)
            sm["aux"].inc(g.dma_start(out=bias_s[:], in_=bias_d[:]), 16)
            sm["aux"].inc(g.dma_start(out=ident_s[:], in_=id_d[:]), 16)
        ffn_ld = sm["aux"].n

        fr = sm["fr"]; rscS = sm["rscS"]; trS = sm["trS"]; cpS = sm["cpS"]
        fr_base = fr.n
        rsc_base = rscS.n
        tr_base = trS.n
        cp_base = cpS.n
        dv_marks = {}
        ac_marks = {}
        pe_prev = sm["pe"].n

        with nc.psum_tensor("ps_r" + tag, [128, 80], F32) as ps_r, \
             nc.psum_tensor("ps_tr0" + tag, [64, 128], BF16) as ps_tr0, \
             nc.psum_tensor("ps_tr1" + tag, [64, 128], BF16) as ps_tr1:
            ps_trs = [ps_tr0, ps_tr1]
            for ts in range(8):
                @blk.tensor
                def _(t, ts=ts):
                    t.wait_ge(sm["aux"].h, ffn_ld)
                    for s_, c_ in xn_total:
                        t.wait_ge(s_.h, c_)

                    if ts == 0:
                        t.wait_ge(sm["evac"].h, sm["evac"].n)
                    else:
                        t.wait_ge(sm["dv"].h, dv_marks[ts - 1][0])
                        t.wait_ge(sm["chd"].h, dv_marks[ts - 1][1])
                        t.wait_ge(sm["ac"].h, ac_marks[ts - 1])
                    inst = None
                    for kc in range(KC):
                        inst = t.matmul(ps_r[:], xn_s[:, kc, 128 * ts:128 * (ts + 1)],
                                        ar_pan[:, kc, :], start=(kc == 0),
                                        stop=(kc == KC - 1), skip_group_check=True)
                    fr.inc(inst)
                fr_now = fr.n

                @blk.vector
                def _(v, fr_now=fr_now, ts=ts):
                    ch = Chain(v, sm["chd"])
                    v.wait_ge(fr.h, fr_now)
                    ch.op(v.tensor_copy(r_tm[:, ts, :], ps_r[:, 0:64]))
                    ch.op(v.reduce_max(srow[:, 0:1], ps_r[:, 64:80], axis=AX.X))
                    ch.wait()
                    ch.op(v.tensor_scalar_mul(srow[:, 1:2], srow[:, 0:1], -1.0), sm["dv"])
                dv_m = sm["dv"].n
                dv_marks[ts] = (dv_m, sm["chd"].n)

                @blk.scalar
                def _(s, dv_m=dv_m):
                    s.wait_ge(sm["dv"].h, dv_m)
                    sm["ac"].inc(s.activation(out=comb[:], in_=ps_r[:, 64:80],
                                              func=AF.Exp, bias=srow[:, 1:2]))
                ac_marks[ts] = sm["ac"].n
                ac_m = sm["ac"].n

                @blk.vector
                def _(v, ts=ts, ac_m=ac_m):
                    ch = Chain(v, sm["chd"])
                    v.wait_ge(sm["ac"].h, ac_m)
                    ch.op(v.reduce_sum(srow[:, 2:3], comb[:], axis=AX.X))
                    ch.wait()
                    ch.op(v.reciprocal(srow[:, 3:4], srow[:, 2:3]))
                    ch.wait()
                    ch.op(v.tensor_scalar_mul(comb[:], comb[:], srow[:, 3:4]))
                    ch.wait()
                    ch.op(v.reduce_max(srow[:, 4:5], comb[:], axis=AX.X))
                    ch.wait()
                    ch.op(v.tensor_scalar(esc[:, 0:16], comb[:], srow[:, 4:5], None,
                                          OP.is_equal))
                    ch.wait()
                    ch.op(v.scalar_tensor_tensor(esc[:, 16:32], esc[:, 0:16], NEG, comb[:],
                                                 OP.mult, OP.add))
                    ch.wait()
                    ch.op(v.reduce_max(srow[:, 5:6], esc[:, 16:32], axis=AX.X))
                    ch.wait()
                    ch.op(v.tensor_scalar(esc[:, 32:48], comb[:], srow[:, 5:6], None,
                                          OP.is_equal))
                    ch.op(v.tensor_add(srow[:, 6:7], srow[:, 4:5], srow[:, 5:6]))
                    ch.wait()
                    ch.op(v.tensor_scalar_add(srow[:, 6:7], srow[:, 6:7], 1e-6))
                    ch.wait()
                    ch.op(v.reciprocal(srow[:, 7:8], srow[:, 6:7]))
                    ch.wait()
                    ch.op(v.tensor_mul(srow[:, 8:9], srow[:, 4:5], srow[:, 7:8]))
                    ch.op(v.tensor_mul(srow[:, 9:10], srow[:, 5:6], srow[:, 7:8]))
                    ch.wait()
                    ch.op(v.tensor_scalar_mul(srow[:, 8:9], srow[:, 8:9], 0.5))
                    ch.op(v.tensor_scalar_mul(srow[:, 9:10], srow[:, 9:10], 0.5))
                    ch.wait()
                    ch.op(v.tensor_scalar_mul(esc[:, 32:48], esc[:, 32:48], srow[:, 9:10]))
                    ch.wait()
                    ch.op(v.scalar_tensor_tensor(comb[:], esc[:, 0:16], srow[:, 8:9],
                                                 esc[:, 32:48], OP.mult, OP.add))
                    for e in range(16):
                        k = 16 * ts + e
                        ch.wait()
                        if k >= 16:
                            v.wait_ge(trS.h, tr_base + (k - 16) + 1)
                        ch.op(v.tensor_scalar_mul(rsc[:, e, :], r_tm[:, ts, :],
                                                  comb[:, e:e + 1]), rscS)

                @blk.tensor
                def _(t, ts=ts):
                    for e in range(16):
                        k = 16 * ts + e
                        t.wait_ge(rscS.h, rsc_base + k + 1)
                        if k >= 2:
                            t.wait_ge(cpS.h, cp_base + (k - 2) + 1)
                        trS.inc(t.transpose(ps_trs[k % 2][:], rsc[:, e, :], ident_s[:]))

                @blk.vector
                def _(v, ts=ts):
                    for e in range(16):
                        k = 16 * ts + e
                        v.wait_ge(trS.h, tr_base + k + 1)
                        cpS.inc(v.tensor_copy(reT[:, e, 128 * ts:128 * (ts + 1)],
                                              ps_trs[k % 2][:]))
        reT_done = cpS.n

        if _A_STOP == "prep":
            @blk.gpsimd
            def _(g):
                g.wait_ge(st.h, st.n)
                g.wait_ge(cpS.h, reT_done)
            ffn_scope.close()
            return P

        def panel_extra(g, ocg, col0, par, ld):
            src = wb_d[:, col0:col0 + 256].rearrange("(kc p) oc -> p kc oc", p=128)
            for sub in range(2):
                ld.inc(g.dma_start(out=w_pan[:, par, 16 * sub:16 * sub + 16, :],
                                   in_=src[:, 16 * sub:16 * sub + 16, :]), 16)
            ld.inc(g.dma_start(out=bt_pan[:, par, :, :],
                               in_=btr_d[:, :, col0:col0 + 256]), 16)

        def mm_extra(t, ocg, ol_abs, th):
            par = ocg % 2
            ol = ol_abs - 2 * par
            inst = None
            for e in range(16):
                inst = t.matmul(ps_proj2[:, ol_abs, 512 * th:512 * (th + 1)],
                                bt_pan[:, par, e, 128 * ol:128 * ol + 128],
                                reT[:, e, 512 * th:512 * (th + 1)],
                                start=False, stop=(e == 15), skip_group_check=True)
            return inst
        mm_extra_wait = (sm["cpS"], reT_done)
    else:
        s_pan = ffn_scope.enter_context(nc.sbuf_tensor("s_pan" + tag, [128, KC, 256], BF16))

        def panel_extra(g, ocg, col0, par, ld):
            for sub in range(2):
                ld.inc(g.dma_start(
                    out=s_pan[:, 16 * sub:16 * sub + 16, :],
                    in_=scl_d[16 * sub:16 * sub + 16, col0:col0 + 256][None, :, :]
                        .partition_broadcast(128)), 16)
                ld.inc(g.dma_start(
                    out=w_pan[:, par, 16 * sub:16 * sub + 16, :],
                    in_=tern_d[:, col0:col0 + 256]
                        .rearrange("(kc p) oc -> p kc oc", p=128)[:, 16 * sub:16 * sub + 16, :]), 16)
            g.wait_ge(ld.h, ld.n)
            sm["mulS"].inc(g.tensor_mul(w_pan[:, par, :, :], w_pan[:, par, :, :], s_pan[:]))
        mm_extra = None
        mm_extra_wait = None

    xfs = sm["xfs"]
    ffn_ev_marks = [sm["evac"].n]
    ln_xf_done = (sm["xfd"].n, sm["xfg"].n)

    with nc.psum_tensor("ps_proj2" + tag, [128, 4, T], F32) as ps_proj2:

        def ffn_writer(ocg, mm_snap):
            par = ocg % 2

            @blk.sync
            def _(sp, ocg=ocg):
                sp.wait_ge(sm["evac"].h, ffn_ev_marks[ocg])
                if ocg == 0:
                    sp.wait_ge(sm["xfd"].h, ln_xf_done[0])
                    sp.wait_ge(sm["xfg"].h, ln_xf_done[1])
                for ol in range(2):
                    og = 2 * ocg + ol
                    xfs[og % 4].inc(sp.dma_start(out=xf[:, og % 4, :],
                                                 in_=xT_d[128 * og:128 * (og + 1), :]), 16)
            marks = tuple(xfs[i].n for i in range(4))

            @blk.vector
            def _(v, ocg=ocg, mm_snap=mm_snap, marks=marks, par=par):
                v.wait_ge(sm["pe"].h, mm_snap)
                v.wait_ge(st.h, st_marks[-1] if ocg == 0 else st_marks[-2])
                for ol in range(2):
                    og = 2 * ocg + ol
                    v.wait_ge(xfs[og % 4].h, marks[og % 4])
                    if layer == 0:
                        inst = v.scalar_tensor_tensor(stagef[:, og % 4, :],
                                                      ps_proj2[:, 2 * par + ol, :],
                                                      bias_s[:, og:og + 1], xf[:, og % 4, :],
                                                      OP.add, OP.add)
                    else:
                        inst = v.tensor_add(stagef[:, og % 4, :],
                                            ps_proj2[:, 2 * par + ol, :],
                                            xf[:, og % 4, :])
                    sm["evac"].inc(inst)
            ev_now = sm["evac"].n

            @blk.sync
            def _(sp, ocg=ocg, ev_now=ev_now):
                sp.wait_ge(sm["evac"].h, ev_now)
                for ol in range(2):
                    og = 2 * ocg + ol
                    st.inc(sp.dma_start(out=xp_o[128 * og:128 * (og + 1), :],
                                        in_=stagef[:, og % 4, :]), 16)
            st_marks.append(st.n)
            ffn_ev_marks.append(sm["evac"].n)

        emit_proj(P, blk, w_d=None, n_ocg=16, xn_s=xn_s, sm=sm, w_pan=w_pan,
                  ps_proj=ps_proj2, writer=ffn_writer, xn_total=xn_total,
                  prev_pe_mark=sm["pe"].n, prev_evac_mark=sm["evac"].n,
                  panel_extra=panel_extra, mm_extra=mm_extra,
                  mm_extra_wait=mm_extra_wait,
                  panel_sem_extra=(sm["mulS"] if layer == 1 else None))

    ffn_scope.close()


def build_A(layer):
    P = Prog()
    d = {"xT": P.din("xT", [DM, T], F32),
         "wqT": P.din("wqT", [DM, DM], F32),
         "wlatT": P.din("wlatT", [DM, 512], F32),
         "qT": P.dout("qT", [DM, T], BF16),
         "lat": P.dout("latT", [512, T], BF16),
         "xp": P.dout("xp", [DM, T], F32)}
    if layer == 0:
        d.update(wbT=P.din("wbT", [DM, DM], F32),
                 biasb=P.din("biasb", [128, KC], F32),
                 arT=P.din("arT", [DM, 80], F32),
                 BTr=P.din("BTr", [64, 16, DM], F32),
                 identF=P.din("identF", [128, 128], BF16))
    else:
        d.update(ternT=P.din("ternT", [DM, DM], I8),
                 scalesT=P.din("scalesT", [KC, DM], F32),
                 cosq=P.din("cosq", [64, T], F32),
                 sinq=P.din("sinq", [64, T], F32))
    sm = make_sems(P, A_SEMS)
    blk = P.ctx.enter_context(P.nc.Block())
    emit_A(P, blk, sm, layer, d, P.ctx)

    @blk.gpsimd
    def _(g):
        g.wait_ge(sm["st"].h, sm["st"].n)

    return P



_CONSTS = None


def host_consts():
    global _CONSTS
    if _CONSTS is None:
        inv = 1.0 / (10000.0 ** (np.arange(64) / 64.0))
        fr = np.arange(S, dtype=np.float64)[None, :] * inv[:, None]
        cosT = np.cos(fr).astype(np.float32)
        sinT = np.sin(fr).astype(np.float32)
        ks, i, j = np.meshgrid(np.arange(4), np.arange(128), np.arange(512),
                               indexing="ij")
        masks = np.where(128 * ks + i > j, np.float32(NEG), np.float32(0.0))
        ident = np.eye(128, dtype=BF)
        _CONSTS = (cosT, sinT, np.ascontiguousarray(masks), ident)
    return _CONSTS


def prep_weights(inputs):
    C = lambda a: np.ascontiguousarray(a)
    w = {}
    w["wqT0"] = C(inputs["L0_attn_Wq"].T)
    w["wlatT0"] = C(np.concatenate([inputs["L0_attn_Wk_down"],
                                    inputs["L0_attn_Wv_down"]], axis=0).T)
    w["wbT0"] = C(inputs["L0_ffn_base_w"].T)
    w["biasb0"] = C(inputs["L0_ffn_base_b"].reshape(KC, 128).T)
    w["arT0"] = C(np.concatenate([inputs["L0_ffn_A"],
                                  inputs["L0_ffn_router_w"]], axis=0).T)
    w["BTr0"] = C(inputs["L0_ffn_B"].transpose(2, 0, 1))
    w["wkupT"] = [C(inputs["L0_attn_Wk_up"][128 * c:128 * (c + 1), :].T)
                  for c in range(NCORE)]
    w["wvupT"] = [C(inputs["L0_attn_Wv_up"][128 * c:128 * (c + 1), :].T)
                  for c in range(NCORE)]
    w["woT0"] = C(inputs["L0_attn_Wo"].T)
    w["wqT1"] = C(inputs["L1_attn_Wq"])
    w["wlatT1"] = C(inputs["L1_attn_Wc"])
    w["wk1"] = [C(inputs["L1_attn_Wk"][:, 512 * c:512 * (c + 1)]) for c in range(NCORE)]
    w["wv1"] = [C(inputs["L1_attn_Wv"][:, 512 * c:512 * (c + 1)]) for c in range(NCORE)]
    w["woT1"] = C(inputs["L1_attn_Wo"])
    w["ternT"] = C(inputs["L1_ffn_ternary"].T)
    w["scalesT"] = C(inputs["L1_ffn_scales"].reshape(DM, KC).T)
    return w


def run_A(layer, xT_cols, w):
    P = build_A(layer)
    cosT, sinT, masks, ident = host_consts()
    in_maps = []
    for c in range(NCORE):
        m = {"xT": xT_cols[c]}
        if layer == 0:
            m.update(wqT=w["wqT0"], wlatT=w["wlatT0"], wbT=w["wbT0"],
                     biasb=w["biasb0"], arT=w["arT0"], BTr=w["BTr0"], identF=ident)
        else:
            m.update(wqT=w["wqT1"], wlatT=w["wlatT1"], ternT=w["ternT"],
                     scalesT=w["scalesT"],
                     cosq=np.ascontiguousarray(cosT[:, T * c:T * (c + 1)]),
                     sinq=np.ascontiguousarray(sinT[:, T * c:T * (c + 1)]))
        in_maps.append(m)
    res = run_bass_kernel_spmd(P.nc, in_maps, core_ids=list(range(NCORE)))
    qT = [np.asarray(r["qT"]) for r in res.results]
    latT = [np.asarray(r["latT"]) for r in res.results]
    xp = [np.asarray(r["xp"]) for r in res.results]
    return qT, latT, xp



B_SEMS = ["gld", "sld", "qld", "st", "kb", "ke", "vb", "ve", "nq", "nmx",
          "sS", "sE", "sM", "sAV", "nrm", "trP", "ocp", "mcl", "acn", "rpd",
          "rotk", "chd", "cha", "dvm"]


def emit_B(P, blk, sm, layer, d, scope, tag=""):
    nc = P.nc
    qTg_d = d["qTg"]
    latg_d = d["latg"]
    id_d = d["ident"]
    if layer == 0:
        wku_d = d["wkupT"]; wvu_d = d["wvupT"]; mask_d = d["masks"]
    else:
        wk1_d = d["wk1"]; wv1_d = d["wv1"]; cos_d = d["cosT"]; sin_d = d["sinT"]
    out_d = d["outT"]

    def SB(name, shape, dt):
        return scope.enter_context(nc.sbuf_tensor(name + tag, shape, dt))

    lat_s = SB("lat_s", [128, 4, S], BF16)
    k_fm = SB("k_fm", [128, S], BF16)
    v_s = SB("v_s", [128, 64, 130], BF16)
    q_s = SB("q_s", [128, S], BF16)
    pT0 = SB("pT0", [128, 2, 512], BF16)
    pT1 = SB("pT1", [128, 2, 512], BF16)
    ot = SB("ot", [128, 4, 128], BF16)
    ostage = SB("ostage", [128, 512], BF16)
    sq_s = SB("sq_s", [128, S], BF16)
    nrow = SB("nrow", [1, 64], F32)
    mcol = SB("mcol", [128, 1], F32)
    rl_s = SB("rl_s", [128, 2], F32)
    ones_bf = SB("ones_bf", [128, 1], BF16)
    ones_f = SB("ones_f", [1, 128], F32)
    ident_s = SB("ident_s", [128, 128], BF16)
    if layer == 0:
        wka = SB("wka", [128, 4, 128], BF16)
        wkb = SB("wkb", [128, 4, 128], BF16)
        wva = SB("wva", [128, 4, 128], BF16)
        wvb = SB("wvb", [128, 4, 128], BF16)
        mask_s = SB("mask_s", [128, 4, 512], F32)
    else:
        wk1_s = SB("wk1_s", [128, 4, 512], BF16)
        wv1_s = SB("wv1_s", [128, 4, 512], BF16)
        cos_s = SB("cos_s", [128, S], BF16)
        sin_s = SB("sin_s", [128, S], BF16)
        CH = 2048
        krot = SB("krot", [128, CH], BF16)
        tA = SB("tA", [128, CH], BF16)
        tB = SB("tB", [128, CH], BF16)

    ps_S0 = scope.enter_context(nc.psum_tensor("ps_S0" + tag, [128, 2, 512], F32))
    ps_S1 = scope.enter_context(nc.psum_tensor("ps_S1" + tag, [128, 2, 512], F32))
    ps_av0 = scope.enter_context(nc.psum_tensor("ps_av0" + tag, [128, 2, 130], F32))
    ps_av1 = scope.enter_context(nc.psum_tensor("ps_av1" + tag, [128, 2, 130], F32))
    ps_tr = scope.enter_context(nc.psum_tensor("ps_tr" + tag, [128, 128], BF16))
    ps_n = scope.enter_context(nc.psum_tensor("ps_n" + tag, [128, 128], F32))
    psS_tiles = [ps_S0[:, 0, :], ps_S0[:, 1, :], ps_S1[:, 0, :], ps_S1[:, 1, :]]
    psav_tiles = [ps_av0[:, 0, 0:128], ps_av0[:, 1, 0:128],
                  ps_av1[:, 0, 0:128], ps_av1[:, 1, 0:128]]

    gld = sm["gld"]; sld = sm["sld"]; qld = sm["qld"]; st = sm["st"]

    if "qsel" in d:
        qsel_s = SB("qsel_s", [128, 32], mybir.dt.int32)

    @blk.gpsimd
    def _(g):
        if "qsel" in d:
            gld.inc(g.dma_start(out=qsel_s[:], in_=d["qsel"][:]), 16)
        if layer == 0:
            src = wku_d.rearrange("(j u) d -> u j d", u=64)
            gld.inc(g.dma_start(out=wka[0:64, :, :], in_=src), 16)
            gld.inc(g.dma_start(out=wkb[64:128, :, :], in_=src), 16)
            srcv = wvu_d.rearrange("(j u) d -> u j d", u=64)
            gld.inc(g.dma_start(out=wva[0:64, :, :], in_=srcv), 16)
            gld.inc(g.dma_start(out=wvb[64:128, :, :], in_=srcv), 16)
        else:
            gld.inc(g.dma_start(out=wk1_s[:], in_=wk1_d.rearrange("(lc p) o -> p lc o", p=128)), 16)
            gld.inc(g.dma_start(out=wv1_s[:], in_=wv1_d.rearrange("(lc p) o -> p lc o", p=128)), 16)
            gld.inc(g.dma_start(out=cos_s[0:64, :], in_=cos_d[:]), 16)
            gld.inc(g.dma_start(out=cos_s[64:128, :], in_=cos_d[:]), 16)
            gld.inc(g.dma_start(out=sin_s[0:64, :], in_=sin_d[:]), 16)
            gld.inc(g.dma_start(out=sin_s[64:128, :], in_=sin_d[:]), 16)
    gld_n = gld.n

    @blk.sync
    def _(sp):
        if "ccw_lat" in d:
            sp.wait_ge(d["ccw_lat"][0].h, d["ccw_lat"][1])
        for r in range(NCORE):
            if layer == 0:
                for h in range(8):
                    p0 = 64 * (h % 2)
                    sld.inc(sp.dma_start(out=lat_s[p0:p0 + 32, h // 2, T * r:T * (r + 1)],
                                         in_=latg_d[512 * r + 32 * h:512 * r + 32 * h + 32, :]), 16)
                    sld.inc(sp.dma_start(out=lat_s[p0 + 32:p0 + 64, h // 2, T * r:T * (r + 1)],
                                         in_=latg_d[512 * r + 256 + 32 * h:512 * r + 256 + 32 * h + 32, :]), 16)
            else:
                sld.inc(sp.dma_start(
                    out=lat_s[:, :, T * r:T * (r + 1)],
                    in_=latg_d[512 * r:512 * (r + 1), :].rearrange("(lc p) t -> p lc t", p=128)), 16)
        if layer == 0:
            sld.inc(sp.dma_start(out=mask_s[:], in_=mask_d.rearrange("a p q -> p a q")), 16)
        sld.inc(sp.dma_start(out=ident_s[:], in_=id_d[:]), 16)
    sld_n = sld.n

    @blk.vector
    def _(v):
        ch = Chain(v, sm["dvm"])
        ch.op(v.memset(v_s[:, :, 128:129], 1.0))
        ch.op(v.memset(ones_bf[:], 1.0))
        ch.op(v.memset(ones_f[:], 1.0))
    memset_n = sm["dvm"].n

    NH = 4
    lat_r = lat_s.rearrange("p c (m j) -> p c m j", j=8) if layer == 0 else None

    def emit_kv_build(hh):
        kb = sm["kb"]; ke = sm["ke"]; vb = sm["vb"]; ve = sm["ve"]
        kb_base = kb.n
        ke_base = ke.n

        @blk.tensor
        def _(t, hh=hh):
            t.wait_ge(gld.h, gld_n)
            t.wait_ge(sld.h, sld_n)
            if layer == 1 and hh > 0:
                t.wait_ge(sm["sE"].h, sm["sE"].n)
            for tb in range(16):
                if tb >= 4:
                    t.wait_ge(ke.h, ke_base + (tb - 4) + 1)
                tile = psS_tiles[tb % 4]
                if layer == 0:
                    h = tb // 2
                    p0 = 64 * (h % 2)
                    wk_t = wka if h % 2 == 0 else wkb
                    for j in range(4):
                        inst = t.matmul(tile, wk_t[p0:p0 + 64, j, :],
                                        lat_r[p0:p0 + 64, h // 2, 512 * (tb % 2):512 * (tb % 2 + 1), j],
                                        start=(j == 0), stop=(j == 3), skip_group_check=True)
                else:
                    for lc in range(4):
                        inst = t.matmul(tile, wk1_s[:, lc, 128 * hh:128 * (hh + 1)],
                                        lat_s[:, lc, 512 * tb:512 * (tb + 1)],
                                        start=(lc == 0), stop=(lc == 3), skip_group_check=True)
                kb.inc(inst)
        kb_now = kb.n

        @blk.scalar
        def _(s):
            for tb in range(16):
                s.wait_ge(kb.h, kb_base + tb + 1)
                ke.inc(s.copy(k_fm[:, 512 * tb:512 * (tb + 1)], psS_tiles[tb % 4]))
        ke_now = ke.n

        vb_base = vb.n
        ve_base = ve.n

        nrm_prev = sm["nrm"].n

        vorder = [0, 2, 1, 3]

        @blk.tensor
        def _(t, hh=hh):
            if layer == 1 and hh > 0:
                t.wait_ge(sm["nrm"].h, nrm_prev)
            for i in range(64):
                if i >= 2:
                    t.wait_ge(ve.h, ve_base + (i - 2) + 1)
                tile = psav_tiles[vorder[i % 4]]
                if layer == 0:
                    h = i // 8
                    ms = i % 8
                    p0 = 64 * (h % 2)
                    wv_t = wva if h % 2 == 0 else wvb
                    for j in range(4):
                        inst = t.matmul(tile,
                                        lat_r[p0:p0 + 64, h // 2, 128 * ms:128 * (ms + 1), 4 + j],
                                        wv_t[p0:p0 + 64, j, :],
                                        start=(j == 0), stop=(j == 3), skip_group_check=True)
                else:
                    for lc in range(4):
                        inst = t.matmul(tile, lat_s[:, lc, 128 * i:128 * (i + 1)],
                                        wv1_s[:, lc, 128 * hh:128 * (hh + 1)],
                                        start=(lc == 0), stop=(lc == 3), skip_group_check=True)
                vb.inc(inst)

        @blk.vector
        def _(v):
            for i in range(64):
                v.wait_ge(vb.h, vb_base + i + 1)
                ve.inc(v.tensor_copy(v_s[:, i, 0:128], psav_tiles[vorder[i % 4]]))
        return ke_now, ve.n

    def emit_rope_k(hh):
        rotk = sm["rotk"]; rpd = sm["rpd"]; ke = sm["ke"]
        ke_now = ke.n
        for cc in range(S // CH):
            rot_base = rotk.n

            @blk.sync
            def _(sp, cc=cc):
                sp.wait_ge(ke.h, ke_now)
                if cc > 0:
                    sp.wait_ge(rpd.h, rpd.n)
                sl = slice(CH * cc, CH * (cc + 1))
                rotk.inc(sp.dma_start(out=krot[0:64, :], in_=k_fm[64:128, sl]), 16)
                rotk.inc(sp.dma_start(out=krot[64:128, :], in_=k_fm[0:64, sl]), 16)
            rot_now = rotk.n

            @blk.vector
            def _(v, cc=cc):
                ch = Chain(v, sm["chd"])
                v.wait_ge(rotk.h, rot_now)
                v.wait_ge(gld.h, gld_n)
                sl = slice(CH * cc, CH * (cc + 1))
                ch.op(v.tensor_scalar_mul(krot[0:64, :], krot[0:64, :], -1.0))
                ch.op(v.tensor_mul(tA[:], k_fm[:, sl], cos_s[:, sl]))
                ch.wait()
                ch.op(v.tensor_mul(tB[:], krot[:], sin_s[:, sl]))
                ch.wait()
                ch.op(v.tensor_add(k_fm[:, sl], tA[:], tB[:]), rpd)

    def emit_norms(sq_src, n_tiles, out_slot, waits):
        nq = sm["nq"]; nmx = sm["nmx"]; acn = sm["acn"]
        acn_base = acn.n

        @blk.scalar
        def _(s):
            for sem_, val_ in waits:
                s.wait_ge(sem_.h, val_)
            acn.inc(s.activation(out=sq_s[:], in_=sq_src[:], func=AF.Square))
        acn_now = acn.n
        nq_base = nq.n
        nmx_base = nmx.n

        @blk.tensor
        def _(t):
            t.wait_ge(acn.h, acn_now)
            for tb in range(16):
                if tb >= 4:
                    t.wait_ge(nmx.h, nmx_base + (tb - 4) + 1)
                else:
                    t.wait_ge(nmx.h, nmx_base)
                nq.inc(t.matmul(psS_tiles[tb % 4][0:1, :], ones_bf[:],
                                sq_s[:, 512 * tb:512 * (tb + 1)],
                                start=True, stop=True, skip_group_check=True))

        @blk.vector
        def _(v):
            ch = Chain(v, nmx)
            for tb in range(16):
                v.wait_ge(nq.h, nq_base + tb + 1)
                ch.op(v.reduce_max(nrow[0:1, 32 + tb:33 + tb], psS_tiles[tb % 4][0:1, :],
                                   axis=AX.X))
            ch.wait()
            ch.op(v.reduce_max(nrow[0:1, out_slot:out_slot + 1],
                               nrow[0:1, 32:48], axis=AX.X))
        return nmx.n

    def emit_M(kslot, qslot, waits):
        chd = sm["chd"]; cha = sm["cha"]; mcl = sm["mcl"]; acn = sm["acn"]
        nq = sm["nq"]

        @blk.vector
        def _(v):
            ch = Chain(v, chd)
            for sem_, val_ in waits:
                v.wait_ge(sem_.h, val_)
            ch.op(v.tensor_mul(nrow[0:1, 50:51], nrow[0:1, kslot:kslot + 1],
                               nrow[0:1, qslot:qslot + 1]))
        chd_1 = chd.n

        @blk.scalar
        def _(s):
            s.wait_ge(chd.h, chd_1)
            acn.inc(s.activation(out=nrow[0:1, 51:52], in_=nrow[0:1, 50:51], func=AF.Ln))
        acn_1 = acn.n

        @blk.vector
        def _(v):
            ch = Chain(v, chd)
            v.wait_ge(acn.h, acn_1)
            ch.op(v.tensor_scalar(nrow[0:1, 52:53], nrow[0:1, 51:52], 0.5, LOG_SCALE,
                                  OP.mult, OP.add))
        chd_2 = chd.n

        @blk.scalar
        def _(s):
            s.wait_ge(chd.h, chd_2)
            acn.inc(s.activation(out=nrow[0:1, 53:54], in_=nrow[0:1, 52:53], func=AF.Exp))
        acn_2 = acn.n

        @blk.vector
        def _(v):
            ch = Chain(v, chd)
            v.wait_ge(acn.h, acn_2)
            ch.op(v.tensor_scalar_mul(nrow[0:1, 54:55], nrow[0:1, 53:54], -1.0), sm["dvm"])
        dvm_1 = sm["dvm"].n

        mcl_prev = mcl.n
        sE_prev = sm["sE"].n

        @blk.tensor
        def _(t):
            t.wait_ge(sm["dvm"].h, dvm_1)
            t.wait_ge(mcl.h, mcl_prev)
            nq.inc(t.matmul(ps_n[:, 0:1], ones_f[:], nrow[0:1, 54:55],
                            start=True, stop=True, skip_group_check=True))
        nq_1 = sm["nq"].n

        @blk.vector
        def _(v):
            v.wait_ge(sm["nq"].h, nq_1)
            v.wait_ge(sm["sE"].h, sE_prev)
            mcl.inc(v.tensor_copy(mcol[:], ps_n[:, 0:1]))
        return mcl.n

    def units_for(qc):
        kbs = range(qc + 1) if layer == 0 else range(16)
        return [(kb, ksp) for kb in kbs for ksp in range(2)]

    def emit_sdpa_head(h, mcl_mark):
        sS = sm["sS"]; sE = sm["sE"]; sMm = sm["sM"]; sAV = sm["sAV"]
        nrm = sm["nrm"]; trP = sm["trP"]; ocp = sm["ocp"]
        sS_base = sS.n
        sE_base = sE.n
        sM_base = sMm.n
        sAV_base = sAV.n
        nrm_base = nrm.n
        trP_base = trP.n
        ocp_base = ocp.n
        st_base = st.n
        psS_par = [ps_S0, ps_S1]
        pT_par = [pT0, pT1]

        all_units = []
        for qc in range(16):
            for u in units_for(qc):
                all_units.append((qc,) + u)
        NU = len(all_units)
        qc_first = {}
        qc_last = {}
        for i, (qc, kb, ksp) in enumerate(all_units):
            qc_first.setdefault(qc, i)
            qc_last[qc] = i
        diag_units = {i for i, (qc, kb, ksp) in enumerate(all_units)
                      if layer == 0 and kb == qc}
        sM_cum = []
        acc = 0
        for i in range(NU):
            if i in diag_units:
                acc += 2
            sM_cum.append(acc)

        def emit_av(t, i):
            qc, kb, ksp = all_units[i]
            if i == qc_first[qc] and qc > 0:
                t.wait_ge(nrm.h, nrm_base + 4 * qc)
            t.wait_ge(sE.h, sE_base + i + 1)
            for j in range(2):
                ks = 4 * kb + 2 * ksp + j
                first = (i == qc_first[qc]) and (j == 0)
                last = (i == qc_last[qc]) and (j == 1)
                for qs in range(4):
                    inst = t.matmul(
                        (ps_av0 if qs < 2 else ps_av1)[:, qs % 2, 0:129],
                        pT_par[i % 2][:, j, 128 * qs:128 * (qs + 1)],
                        v_s[:, ks, 0:129],
                        start=(first and qs % 2 == 0), stop=last,
                        skip_group_check=True)
            sAV.inc(inst)

        def emit_transposes(t, qc):
            for qs in range(4):
                t.wait_ge(nrm.h, nrm_base + 4 * qc + qs + 1)
                if trP.n - trP_base > 0:
                    t.wait_ge(ocp.h, ocp_base + (trP.n - trP_base))
                trP.inc(t.transpose(ps_tr[:], ot[:, qs, :], ident_s[:]))

        @blk.tensor
        def _(t):
            for i, (qc, kb, ksp) in enumerate(all_units):
                if i < 2:
                    t.wait_ge(sm["nmx"].h, sm["nmx"].n)
                    t.wait_ge(sm["mcl"].h, mcl_mark)
                    t.wait_ge(sm["ve"].h, sm["ve"].n)
                    t.wait_ge(sm["ke"].h, sm["ke"].n)
                    if layer == 1:
                        t.wait_ge(sm["rpd"].h, sm["rpd"].n)
                for j in range(2):
                    ks = 4 * kb + 2 * ksp + j
                    inst = t.matmul(psS_par[i % 2][:, j, :],
                                    k_fm[:, 128 * ks:128 * (ks + 1)],
                                    q_s[:, 512 * qc:512 * (qc + 1)],
                                    start=True, stop=True, skip_group_check=True)
                sS.inc(inst)
                if i >= 1:
                    emit_av(t, i - 1)
                if i == NU - 1:
                    emit_av(t, i)
                prev_done = all_units[i - 1][0] if i >= 1 else None
                if i >= 1 and (i - 1) == qc_last[all_units[i - 1][0]]:
                    emit_transposes(t, all_units[i - 1][0])
                if i == NU - 1:
                    emit_transposes(t, 15)

        @blk.scalar
        def _(s):
            s.wait_ge(sm["mcl"].h, mcl_mark)
            for i, (qc, kb, ksp) in enumerate(all_units):
                if i in diag_units:
                    s.wait_ge(sMm.h, sM_base + sM_cum[i])
                else:
                    s.wait_ge(sS.h, sS_base + i + 1)
                if i >= 2:
                    s.wait_ge(sAV.h, sAV_base + (i - 2) + 1)
                sE.inc(s.activation(out=pT_par[i % 2][:],
                                    in_=psS_par[i % 2][:],
                                    func=AF.Exp, bias=mcol[:, 0:1], scale=SCALE))
        self_sE = sE.n

        @blk.vector
        def _(v):
            ch = Chain(v, nrm)
            if layer == 0:
                v.wait_ge(sld.h, sld_n)
            for i, (qc, kb, ksp) in enumerate(all_units):
                if i in diag_units:
                    v.wait_ge(sS.h, sS_base + i + 1)
                    for j in range(2):
                        sMm.inc(v.tensor_add(psS_par[i % 2][:, j, :],
                                             psS_par[i % 2][:, j, :],
                                             mask_s[:, 2 * ksp + j, :]))
                if i == qc_last[qc]:
                    v.wait_ge(sAV.h, sAV_base + i + 1)
                    if qc > 0:
                        v.wait_ge(st.h, st_base + 16 * qc)
                        v.wait_ge(trP.h, trP_base + 4 * qc)
                    for qs in range(4):
                        pav = (ps_av0 if qs < 2 else ps_av1)
                        ch.wait()
                        ch.op(v.reciprocal(rl_s[:, qs % 2:qs % 2 + 1],
                                           pav[:, qs % 2, 128:129]), sm["dvm"])
                        ch.wait()
                        ch.op(v.tensor_scalar_mul(ot[:, qs, :], pav[:, qs % 2, 0:128],
                                                  rl_s[:, qs % 2:qs % 2 + 1]))
                    for qs in range(4):
                        v.wait_ge(trP.h, trP_base + 4 * qc + qs + 1)
                        sm["ocp"].inc(v.tensor_copy(ostage[:, 128 * qs:128 * (qs + 1)],
                                                    ps_tr[:]))

        @blk.sync
        def _(sp):
            for qc in range(16):
                sp.wait_ge(sm["ocp"].h, ocp_base + 4 * (qc + 1))
                tb2 = qc // 2
                col = 512 * (qc % 2)
                st.inc(sp.dma_start(
                    out=out_d[512 * tb2 + 128 * h:512 * tb2 + 128 * (h + 1),
                              col:col + 512],
                    in_=ostage[:]), 16)

        return dict(sE=self_sE)

    if layer == 0:
        emit_kv_build(0)
        emit_norms(k_fm, 16, 48,
                   waits=[(sm["ke"], sm["ke"].n), (sm["ve"], sm["ve"].n)])
    for h in range(NH):
        sE_prev = sm["sE"].n

        if "qsel" in d:
            @blk.gpsimd
            def _(g, h=h, sE_prev=sE_prev):
                if h == 0 and "ccw_q" in d:
                    g.wait_ge(d["ccw_q"][0].h, d["ccw_q"][1])
                g.wait_ge(sm["sE"].h, sE_prev)
                g.wait_ge(gld.h, gld_n)
                for r in range(NCORE):
                    sm["qld"].inc(g.indirect_dma_start(
                        q_s[:, T * r:T * (r + 1)], None, qTg_d[:, :],
                        IndirectOffsetOnAxis(qsel_s[:, 8 * h + r:8 * h + r + 1], 0)), 16)
        else:
            @blk.sync
            def _(sp, h=h, sE_prev=sE_prev):
                sp.wait_ge(sm["sE"].h, sE_prev)
                for r in range(NCORE):
                    sm["qld"].inc(sp.dma_start(
                        out=q_s[:, T * r:T * (r + 1)],
                        in_=qTg_d[512 * r + 128 * h:512 * r + 128 * (h + 1), :]), 16)
        qld_target = sm["qld"].n

        if layer == 1:
            emit_kv_build(h)
            emit_rope_k(h)
            emit_norms(k_fm, 16, 48,
                       waits=[(sm["rpd"], sm["rpd"].n), (sm["ke"], sm["ke"].n),
                              (sm["nq"], sm["nq"].n), (sm["chd"], sm["chd"].n)])
        emit_norms(q_s, 16, 49,
                   waits=[(sm["qld"], qld_target), (sm["nq"], sm["nq"].n),
                          (sm["chd"], sm["chd"].n)])
        mcl_mark = emit_M(48, 49, waits=[(sm["nmx"], sm["nmx"].n)])
        emit_sdpa_head(h, mcl_mark)



def build_B(layer):
    P = Prog()
    d = {"qTg": P.din("qTg", [DM, T], BF16),
         "latg": P.din("latg", [NCORE * 512, T], BF16),
         "ident": P.din("ident", [128, 128], BF16),
         "outT": P.dout("outT", [DM, T], BF16)}
    if layer == 0:
        d.update(wkupT=P.din("wkupT", [256, 128], F32),
                 wvupT=P.din("wvupT", [256, 128], F32),
                 masks=P.din("masks", [4, 128, 512], F32))
    else:
        d.update(wk1=P.din("wk1", [512, 512], F32),
                 wv1=P.din("wv1", [512, 512], F32),
                 cosT=P.din("cosT", [64, S], F32),
                 sinT=P.din("sinT", [64, S], F32))
    sm = make_sems(P, B_SEMS, nslot=False)
    blk = P.ctx.enter_context(P.nc.Block())
    emit_B(P, blk, sm, layer, d, P.ctx)

    @blk.gpsimd
    def _(g):
        g.wait_ge(sm["st"].h, sm["st"].n)

    return P


def run_B(layer, qT_list, latT_list, w):
    P = build_B(layer)
    cosT, sinT, masks, ident = host_consts()
    in_maps = []
    latg = np.ascontiguousarray(np.concatenate(latT_list, axis=0))
    for c in range(NCORE):
        qTg = np.ascontiguousarray(np.concatenate(
            [qT_list[r][512 * c:512 * (c + 1), :] for r in range(NCORE)], axis=0))
        m = {"qTg": qTg, "latg": latg, "ident": ident}
        if layer == 0:
            m.update(wkupT=w["wkupT"][c], wvupT=w["wvupT"][c], masks=masks)
        else:
            m.update(wk1=w["wk1"][c], wv1=w["wv1"][c], cosT=cosT, sinT=sinT)
        in_maps.append(m)
    res = run_bass_kernel_spmd(P.nc, in_maps, core_ids=list(range(NCORE)))
    return [np.asarray(r["outT"]) for r in res.results]



C_SEMS = ["ald", "wld", "wld1", "pe", "evac", "st"]


def emit_C(P, blk, sm, layer, d, scope, tag=""):
    nc = P.nc
    attn_d = d["outg"]
    wo_d = d["woT"]
    xp_d = d["xp"]
    xout_d = d["xout"]

    def SB(name, shape, dt):
        return scope.enter_context(nc.sbuf_tensor(name + tag, shape, dt))

    at_s = SB("at_s", [128, KC, T], BF16)
    w_pan = SB("w_pan", [128, 2, KC, 256], BF16)
    stagef = SB("stagef", [128, 4, T], F32)
    xf = SB("xf", [128, 4, T], F32)

    xls = sm["xls"]

    if "csel" in d:
        csel_s = SB("csel_s", [128, 32], mybir.dt.int32)

        @blk.gpsimd
        def _(g):
            sm["ald"].inc(g.dma_start(out=csel_s[:], in_=d["csel"][:]), 16)
            g.wait_ge(sm["ald"].h, sm["ald"].n)
            for kc in range(KC):
                sm["ald"].inc(g.indirect_dma_start(
                    at_s[:, kc, :], None, attn_d[:, :],
                    IndirectOffsetOnAxis(csel_s[:, kc:kc + 1], 0)), 16)
    else:
        @blk.sync
        def _(sp):
            for r in range(NCORE):
                sm["ald"].inc(sp.dma_start(
                    out=at_s[:, 4 * r:4 * (r + 1), :],
                    in_=attn_d[512 * r:512 * (r + 1), :].rearrange("(a p) t -> p a t", p=128)), 16)
    ald_n = sm["ald"].n
    st = sm["st"]
    st_marks = [st.n]
    ev_marks = [sm["evac"].n]

    with nc.psum_tensor("ps_proj" + tag, [128, 4, T], F32) as ps_proj:
        def wo_writer(ocg, mm_snap):
            par = ocg % 2

            @blk.sync
            def _(sp, ocg=ocg):
                sp.wait_ge(sm["evac"].h, ev_marks[ocg])
                for ol in range(2):
                    og = 2 * ocg + ol
                    xls[og % 4].inc(sp.dma_start(out=xf[:, og % 4, :],
                                                 in_=xp_d[128 * og:128 * (og + 1), :]), 16)
            marks = tuple(xls[i].n for i in range(4))

            @blk.vector
            def _(v, ocg=ocg, mm_snap=mm_snap, marks=marks, par=par):
                v.wait_ge(sm["pe"].h, mm_snap)
                v.wait_ge(st.h, st_marks[ocg])
                for ol in range(2):
                    og = 2 * ocg + ol
                    v.wait_ge(xls[og % 4].h, marks[og % 4])
                    sm["evac"].inc(v.tensor_add(stagef[:, og % 4, :],
                                                ps_proj[:, 2 * par + ol, :],
                                                xf[:, og % 4, :]))
            ev_now = sm["evac"].n
            ev_marks.append(sm["evac"].n)

            @blk.sync
            def _(sp, ocg=ocg, ev_now=ev_now):
                sp.wait_ge(sm["evac"].h, ev_now)
                for ol in range(2):
                    og = 2 * ocg + ol
                    st.inc(sp.dma_start(out=xout_d[128 * og:128 * (og + 1), :],
                                        in_=stagef[:, og % 4, :]), 16)
            st_marks.append(st.n)

        emit_proj(P, blk, w_d=wo_d, n_ocg=16, xn_s=at_s, sm=sm, w_pan=w_pan,
                  ps_proj=ps_proj, writer=wo_writer,
                  xn_total=[(sm["ald"], ald_n)],
                  prev_pe_mark=sm["pe"].n, prev_evac_mark=sm["evac"].n)


def build_C():
    P = Prog()
    d = {"outg": P.din("outg", [DM, T], BF16),
         "woT": P.din("woT", [DM, DM], F32),
         "xp": P.din("xp", [DM, T], F32),
         "xout": P.dout("xoutT", [DM, T], F32)}
    sm = make_sems(P, C_SEMS, nslot=False)
    sm["xls"] = [P.sem(f"xls{i}") for i in range(4)]
    blk = P.ctx.enter_context(P.nc.Block())
    emit_C(P, blk, sm, 0, d, P.ctx)

    @blk.gpsimd
    def _(g):
        g.wait_ge(sm["st"].h, sm["st"].n)

    return P


def run_C(layer, outT_list, xp_cols, w):
    P = build_C()
    in_maps = []
    for c in range(NCORE):
        outg = np.ascontiguousarray(np.concatenate(
            [outT_list[r][512 * c:512 * (c + 1), :] for r in range(NCORE)], axis=0))
        in_maps.append({"outg": outg,
                        "woT": w["woT0"] if layer == 0 else w["woT1"],
                        "xp": xp_cols[c]})
    res = run_bass_kernel_spmd(P.nc, in_maps, core_ids=list(range(NCORE)))
    return [np.asarray(r["xoutT"]) for r in res.results]



def kernel(**inputs):
    if _UNFUSED:
        return kernel_unfused(**inputs)
    return kernel_fused(**inputs)


def kernel_unfused(**inputs):
    w = prep_weights(inputs)
    x = np.asarray(inputs["L0_attn_hidden_states"])[0]
    xT = np.ascontiguousarray(x.T)
    out_cols = [np.ascontiguousarray(xT[:, T * c:T * (c + 1)]) for c in range(NCORE)]

    for layer in (0, 1):
        qT, latT, xp = run_A(layer, out_cols, w)
        outT = run_B(layer, qT, latT, w)
        out_cols = run_C(layer, outT, xp, w)

    x2T = np.concatenate(out_cols, axis=1)
    return np.ascontiguousarray(x2T.T).reshape(1, S, DM)



def _all_sem_objs(sm):
    out = []
    for v in sm.values():
        if isinstance(v, list):
            out.extend(v)
        elif isinstance(v, Sem):
            out.append(v)
    return out


def emit_barrier(P, blk, sm, skip=()):
    bar = sm["bar"]
    skipset = {sm[k] for k in skip}

    @blk.gpsimd
    def _(g):
        for s_ in _all_sem_objs(sm):
            if s_ is bar or s_ in skipset or s_.n == 0:
                continue
            g.wait_ge(s_.h, s_.n)
        bar.inc(g.nop())
    tgt = bar.n

    for en in ("tensor", "vector", "scalar", "sync"):
        @getattr(blk, en)
        def _(e):
            e.wait_ge(bar.h, tgt)


def build_fused():
    P = Prog()
    nc = P.nc
    xT_in = P.din("xT", [DM, T], F32)
    wq0 = P.din("wqT0", [DM, DM], F32)
    wlat0 = P.din("wlatT0", [DM, 512], F32)
    wb0 = P.din("wbT0", [DM, DM], F32)
    bias0 = P.din("biasb0", [128, KC], F32)
    ar0 = P.din("arT0", [DM, 80], F32)
    btr0 = P.din("BTr0", [64, 16, DM], F32)
    identF = P.din("identF", [128, 128], BF16)
    wku = P.din("wkupT", [256, 128], F32)
    wvu = P.din("wvupT", [256, 128], F32)
    masks = P.din("masks", [4, 128, 512], F32)
    wo0 = P.din("woT0", [DM, DM], F32)
    wq1 = P.din("wqT1", [DM, DM], F32)
    wlat1 = P.din("wlatT1", [DM, 512], F32)
    tern = P.din("ternT", [DM, DM], I8)
    scl = P.din("scalesT", [KC, DM], F32)
    cosq = P.din("cosq", [64, T], F32)
    sinq = P.din("sinq", [64, T], F32)
    wk1 = P.din("wk1", [512, 512], F32)
    wv1 = P.din("wv1", [512, 512], F32)
    cosT_d = P.din("cosT", [64, S], F32)
    sinT_d = P.din("sinT", [64, S], F32)
    wo1 = P.din("woT1", [DM, DM], F32)
    qsel_d = P.din("qsel", [128, 32], mybir.dt.int32)
    csel_d = P.din("csel", [128, 32], mybir.dt.int32)
    dint = lambda nm_, shp, dt: nc.dram_tensor(nm_, shp, dt, kind="Internal").ap()
    qT_loc = dint("qT_loc", [DM, T], BF16)
    lat_loc = dint("lat_loc", [512, T], BF16)
    out_loc = dint("out_loc", [DM, T], BF16)
    qTg = dint("qTg", [NCORE * DM, T], BF16)
    latg = dint("latg", [NCORE * 512, T], BF16)
    outg = dint("outg", [NCORE * DM, T], BF16)
    xp_scr = dint("xp_scr", [DM, T], F32)
    x1_scr = dint("x1_scr", [DM, T], F32)
    x2T = P.dout("x2T", [DM, T], F32)

    sem_names = sorted(set(A_SEMS + B_SEMS + C_SEMS + ["cc", "bar"]))
    sm = make_sems(P, sem_names)
    sm["xls"] = [P.sem(f"xls{i}") for i in range(4)]
    blk = P.ctx.enter_context(nc.Block())
    cc = sm["cc"]



    def collectives(pairs):
        @blk.gpsimd
        def _(g):
            for kind, src_, dst_ in pairs:
                cc.inc(g.collective_compute(
                    kind, OP.bypass, [list(range(NCORE))], [src_.opt()], [dst_.opt()]), 1)
        tgt = cc.n

        @blk.gpsimd
        def _(g):
            g.wait_ge(cc.h, tgt)

    for layer in (0, 1):
        tagA = f"_a{layer}"
        dA = {"xT": xT_in if layer == 0 else x1_scr,
              "qT": qT_loc, "lat": lat_loc, "xp": xp_scr}
        if layer == 0:
            dA.update(wqT=wq0, wlatT=wlat0, wbT=wb0, biasb=bias0, arT=ar0,
                      BTr=btr0, identF=identF)
        else:
            dA.update(wqT=wq1, wlatT=wlat1, ternT=tern, scalesT=scl,
                      cosq=cosq, sinq=sinq)
        ag_tgt = {}

        def hook_lat(latn):
            @blk.gpsimd
            def _(g):
                g.wait_ge(sm["st"].h, latn)
                cc.inc(g.collective_compute(
                    "AllGather", OP.bypass, [list(range(NCORE))],
                    [lat_loc.opt()], [latg.opt()]), 1)
            ag_tgt["lat"] = cc.n

        def hook_q(qn):
            @blk.gpsimd
            def _(g):
                g.wait_ge(sm["st"].h, qn)
                cc.inc(g.collective_compute(
                    "AllGather", OP.bypass, [list(range(NCORE))],
                    [qT_loc.opt()], [qTg.opt()]), 1)
            ag_tgt["q"] = cc.n

        sA = ExitStack()
        emit_A(P, blk, sm, layer, dA, sA, tag=tagA, ag_hook=(hook_lat, hook_q))
        sA.close()
        emit_barrier(P, blk, sm, skip=("cc",))

        dB = {"qTg": qTg, "latg": latg, "ident": identF, "outT": out_loc,
              "qsel": qsel_d, "ccw_lat": (cc, ag_tgt["lat"]),
              "ccw_q": (cc, ag_tgt["q"])}
        if layer == 0:
            dB.update(wkupT=wku, wvupT=wvu, masks=masks)
        else:
            dB.update(wk1=wk1, wv1=wv1, cosT=cosT_d, sinT=sinT_d)
        sB = ExitStack()
        emit_B(P, blk, sm, layer, dB, sB, tag=f"_b{layer}")
        sB.close()

        @blk.gpsimd
        def _(g):
            g.wait_ge(sm["st"].h, sm["st"].n)
            cc.inc(g.collective_compute(
                "AllGather", OP.bypass, [list(range(NCORE))],
                [out_loc.opt()], [outg.opt()]), 1)
        emit_barrier(P, blk, sm)

        dC = {"outg": outg, "woT": wo0 if layer == 0 else wo1, "xp": xp_scr,
              "xout": x1_scr if layer == 0 else x2T, "csel": csel_d}
        sC = ExitStack()
        emit_C(P, blk, sm, layer, dC, sC, tag=f"_c{layer}")
        sC.close()
        if layer == 0:
            emit_barrier(P, blk, sm)

    @blk.gpsimd
    def _(g):
        g.wait_ge(sm["st"].h, sm["st"].n)

    return P


def _qsel(c):
    p = np.arange(128, dtype=np.int64)
    out = np.zeros((128, 32), dtype=np.int32)
    for h in range(4):
        for r in range(8):
            out[:, 8 * h + r] = 4096 * r + 512 * c + 128 * h + p
    return out


def _csel(c):
    p = np.arange(128, dtype=np.int64)
    out = np.zeros((128, 32), dtype=np.int32)
    for r in range(8):
        for a in range(4):
            out[:, 4 * r + a] = 4096 * r + 512 * c + 128 * a + p
    return out


_FUSED_CACHE = {}


def kernel_fused(**inputs):
    w = prep_weights(inputs)
    cosT, sinT, masksv, ident = host_consts()
    x = np.asarray(inputs["L0_attn_hidden_states"])[0]
    xT = np.ascontiguousarray(x.T)

    if "P" not in _FUSED_CACHE:
        _FUSED_CACHE["P"] = build_fused()
    P = _FUSED_CACHE["P"]

    in_maps = []
    for c in range(NCORE):
        m = {"xT": np.ascontiguousarray(xT[:, T * c:T * (c + 1)]),
             "wqT0": w["wqT0"], "wlatT0": w["wlatT0"], "wbT0": w["wbT0"],
             "biasb0": w["biasb0"], "arT0": w["arT0"], "BTr0": w["BTr0"],
             "identF": ident, "wkupT": w["wkupT"][c], "wvupT": w["wvupT"][c],
             "masks": masksv, "woT0": w["woT0"],
             "wqT1": w["wqT1"], "wlatT1": w["wlatT1"], "ternT": w["ternT"],
             "scalesT": w["scalesT"],
             "cosq": np.ascontiguousarray(cosT[:, T * c:T * (c + 1)]),
             "sinq": np.ascontiguousarray(sinT[:, T * c:T * (c + 1)]),
             "wk1": w["wk1"][c], "wv1": w["wv1"][c],
             "cosT": cosT, "sinT": sinT, "woT1": w["woT1"],
             "qsel": _qsel(c), "csel": _csel(c)}
        in_maps.append(m)
    res = run_bass_kernel_spmd(P.nc, in_maps, core_ids=list(range(NCORE)))
    x2T_full = np.concatenate([np.asarray(r["x2T"]) for r in res.results], axis=1)
    return np.ascontiguousarray(x2T_full.T).reshape(1, S, DM)


# version 07
# speedup vs baseline: 1.1420x; 1.0374x over previous
import os
import numpy as np
import ml_dtypes
from contextlib import ExitStack

_A_STOP = os.environ.get("A_STOP", "")
_UNFUSED = os.environ.get("MK_UNFUSED", "") == "1"

import concourse.bass as bass
import concourse.mybir as mybir
from concourse.bass_utils import run_bass_kernel_spmd
from concourse.bass_core_rust import IndirectOffsetOnAxis

F32 = mybir.dt.float32
BF16 = mybir.dt.bfloat16
I8 = mybir.dt.int8
AF = mybir.ActivationFunctionType
OP = mybir.AluOpType
AX = mybir.AxisListType

NCORE = 8
S = 8192
DM = 4096
T = 1024
KC = 32
TH = T // 512
SCALE = float(1.0 / np.sqrt(128.0))
LOG_SCALE = float(np.log(1.0 / np.sqrt(128.0)))
LNEPS = 1e-5
NEG = -1e9
BF = ml_dtypes.bfloat16


class Sem:
    __slots__ = ("h", "n")

    def __init__(self, h):
        self.h = h
        self.n = 0

    def inc(self, inst, k=1):
        inst.then_inc(self.h, k)
        self.n += k
        return inst


class Chain:

    def __init__(self, eng, default_sem):
        self.eng = eng
        self.ds = default_sem
        self.pend = None

    def wait(self):
        if self.pend is not None:
            self.eng.wait_ge(self.pend[0].h, self.pend[1])
            self.pend = None

    def op(self, inst, sem=None):
        s = sem if sem is not None else self.ds
        s.inc(inst)
        self.pend = (s, s.n)
        return inst


class Prog:
    def __init__(self):
        self.nc = bass.Bass("TRN2", target_bir_lowering=False)
        self.ctx = ExitStack()

    def sem(self, name):
        return Sem(self.ctx.enter_context(self.nc.semaphore(name)))

    def sbuf(self, name, shape, dt):
        return self.ctx.enter_context(self.nc.sbuf_tensor(name, shape, dt))

    def din(self, name, shape, dt):
        return self.nc.dram_tensor(name, shape, dt, kind="ExternalInput").ap()

    def dout(self, name, shape, dt):
        return self.nc.dram_tensor(name, shape, dt, kind="ExternalOutput").ap()



def emit_ln(P, blk, xT_d, xn_s, b, sm):
    sq = sm["sq"]; pe = sm["pe"]; dv = sm["dv"]; ac = sm["ac"]
    xfd = sm["xfd"]; xfg = sm["xfg"]; xn = sm["xn"]; xng = sm["xng"]
    statd = sm["statd"]; bcp = sm["bcp"]; bcd = sm["bcd"]
    xbs = sm["xbs"]
    xfs = sm["xfs"]
    ps_stat = b["ps_stat"]; ps_bc = b["ps_bc"]; rows = b["rows"]

    @blk.vector
    def _(v):
        v.memset(b["ones_bf"][:], 1.0)
        dv.inc(v.memset(b["ones_f"][:], 1.0))
    ones_dv = dv.n

    pe_base = pe.n
    xb_marks = []

    @blk.gpsimd
    def _(g):
        for kc in range(KC):
            if kc >= 3:
                g.wait_ge(pe.h, pe_base + (kc - 2))
            xbs[kc % 3].inc(g.dma_start(out=b["xb"][:, kc % 3, :],
                                        in_=xT_d[128 * kc:128 * (kc + 1), :]), 16)
            xb_marks.append(xbs[kc % 3].n)

    sq_base = sq.n

    @blk.scalar
    def _(s):
        for kc in range(KC):
            s.wait_ge(xbs[kc % 3].h, xb_marks[kc])
            if kc >= 2:
                s.wait_ge(pe.h, pe_base + (kc - 1))
            sq.inc(s.activation(out=b["xsq"][:, kc % 2, :], in_=b["xb"][:, kc % 3, :],
                                func=AF.Square))

    @blk.tensor
    def _(t):
        t.wait_ge(dv.h, ones_dv)
        for kc in range(KC):
            t.wait_ge(xbs[kc % 3].h, xb_marks[kc])
            t.wait_ge(sq.h, sq_base + (kc + 1))
            insts = []
            for th in range(TH):
                insts.append(t.matmul(ps_stat[:, th, :], b["ones_bf"][:],
                                      b["xb"][:, kc % 3, 512 * th:512 * (th + 1)],
                                      start=(kc == 0), stop=(kc == KC - 1),
                                      skip_group_check=True))
            for th in range(TH):
                insts.append(t.matmul(ps_stat[:, TH + th, :], b["ones_bf"][:],
                                      b["xsq"][:, kc % 2, 512 * th:512 * (th + 1)],
                                      start=(kc == 0), stop=(kc == KC - 1),
                                      skip_group_check=True))
            if kc == KC - 1:
                for i in insts:
                    statd.inc(i)
            else:
                pe.inc(insts[-1])

    chd = sm["chd"]; cha = sm["cha"]

    @blk.vector
    def _(v):
        ch = Chain(v, chd)
        v.wait_ge(statd.h, statd.n)
        ch.op(v.tensor_scalar_mul(rows[:, 0:T], ps_stat[0:1, 0:TH, :], 1.0 / DM))
        ch.op(v.tensor_scalar_mul(rows[:, T:2 * T], ps_stat[0:1, TH:2 * TH, :], 1.0 / DM))
        ch.wait()
        ch.op(v.tensor_mul(rows[:, 2 * T:3 * T], rows[:, 0:T], rows[:, 0:T]))
        ch.wait()
        ch.op(v.tensor_sub(rows[:, T:2 * T], rows[:, T:2 * T], rows[:, 2 * T:3 * T]))
        ch.wait()
        ch.op(v.tensor_scalar_add(rows[:, T:2 * T], rows[:, T:2 * T], LNEPS), dv)
    dv_s0 = dv.n
    chd_stats = chd.n

    @blk.scalar
    def _(s):
        ch = Chain(s, cha)
        s.wait_ge(dv.h, dv_s0)
        ch.op(s.activation(out=rows[:, 2 * T:3 * T], in_=rows[:, T:2 * T], func=AF.Ln))
        ch.wait()
        ch.op(s.activation(out=rows[:, 2 * T:3 * T], in_=rows[:, 2 * T:3 * T],
                           func=AF.Exp, scale=-0.5), ac)
    ac_s = ac.n

    @blk.vector
    def _(v):
        v.wait_ge(ac.h, ac_s)
        v.wait_ge(chd.h, chd_stats)
        dv.inc(v.tensor_mul(rows[:, 3 * T:4 * T], rows[:, 0:T], rows[:, 2 * T:3 * T]))
    dv_s1 = dv.n

    @blk.tensor
    def _(t):
        t.wait_ge(dv.h, dv_s1)
        for th in range(TH):
            bcp.inc(t.matmul(ps_bc[:, 0, 512 * th:512 * (th + 1)], b["ones_f"][:],
                             rows[0:1, 2 * T + 512 * th:2 * T + 512 * (th + 1)],
                             start=True, stop=True, skip_group_check=True))
            bcp.inc(t.matmul(ps_bc[:, 1, 512 * th:512 * (th + 1)], b["ones_f"][:],
                             rows[0:1, 3 * T + 512 * th:3 * T + 512 * (th + 1)],
                             start=True, stop=True, skip_group_check=True))
    bcp_s = bcp.n

    @blk.vector
    def _(v):
        v.wait_ge(bcp.h, bcp_s)
        bcd.inc(v.tensor_copy(b["rstd_bc"][:], ps_bc[:, 0, :]))
        bcd.inc(v.tensor_copy(b["mr_bc"][:], ps_bc[:, 1, :]))
    bcd_s = bcd.n

    xfd_base = xfd.n
    xfg_base = xfg.n

    xf_marks = []

    @blk.sync
    def _(sp):
        for kc in range(KC):
            if kc >= 4:
                if kc % 2 == 0:
                    sp.wait_ge(xfd.h, xfd_base + (kc - 4) // 2 + 1)
                else:
                    sp.wait_ge(xfg.h, xfg_base + (kc - 4) // 2 + 1)
            xfs[kc % 4].inc(sp.dma_start(out=b["xf"][:, kc % 4, :],
                                         in_=xT_d[128 * kc:128 * (kc + 1), :]), 16)
            xf_marks.append(xfs[kc % 4].n)

    @blk.vector
    def _(v):
        prev_sub = None
        for kc in range(0, KC, 2):
            v.wait_ge(xfs[kc % 4].h, xf_marks[kc])
            v.wait_ge(bcd.h, bcd_s)
            if prev_sub is not None:
                v.wait_ge(xn.h, prev_sub)
            xfd.inc(v.tensor_mul(b["t1d"][:], b["xf"][:, kc % 4, :], b["rstd_bc"][:]))
            v.wait_ge(xfd.h, xfd.n)
            xn.inc(v.tensor_sub(xn_s[:, kc, :], b["t1d"][:], b["mr_bc"][:]))
            prev_sub = xn.n

    @blk.gpsimd
    def _(g):
        prev_sub = None
        for kc in range(1, KC, 2):
            g.wait_ge(xfs[kc % 4].h, xf_marks[kc])
            g.wait_ge(bcd.h, bcd_s)
            if prev_sub is not None:
                g.wait_ge(xng.h, prev_sub)
            xfg.inc(g.tensor_mul(b["t1g"][:], b["xf"][:, kc % 4, :], b["rstd_bc"][:]))
            g.wait_ge(xfg.h, xfg.n)
            xng.inc(g.tensor_sub(xn_s[:, kc, :], b["t1g"][:], b["mr_bc"][:]))
            prev_sub = xng.n



def emit_proj(P, blk, *, w_d, n_ocg, xn_s, sm, w_pan, ps_proj, writer, xn_total,
              prev_pe_mark=0, prev_evac_mark=0, panel_extra=None, mm_extra=None,
              mm_extra_wait=None, panel_sem_extra=None):
    lds = [sm["wld"], sm["wld1"]]; pe = sm["pe"]; ev = sm["evac"]
    pe_marks = [prev_pe_mark, prev_pe_mark]
    ev_marks = [prev_evac_mark, prev_evac_mark]

    for ocg in range(n_ocg):
        col0 = 256 * ocg
        par = ocg % 2
        ld = lds[par]

        @blk.gpsimd
        def _(g, ocg=ocg, col0=col0, par=par, ld=ld):
            g.wait_ge(pe.h, pe_marks[ocg])
            if panel_extra is not None:
                panel_extra(g, ocg, col0, par, ld)
            else:
                src_ = w_d[:, col0:col0 + 256].rearrange("(kc p) oc -> p kc oc", p=128)
                for sub in range(2):
                    ld.inc(g.dma_start(out=w_pan[:, par, 16 * sub:16 * sub + 16, :],
                                       in_=src_[:, 16 * sub:16 * sub + 16, :]), 16)
        ld_snap = ld.n
        pse_snap = panel_sem_extra.n if panel_sem_extra is not None else 0

        @blk.tensor
        def _(t, ocg=ocg, ld_snap=ld_snap, pse_snap=pse_snap, par=par, ld=ld):
            t.wait_ge(ld.h, ld_snap)
            if panel_sem_extra is not None:
                t.wait_ge(panel_sem_extra.h, pse_snap)
            for s_, c_ in xn_total:
                t.wait_ge(s_.h, c_)
            if mm_extra_wait is not None:
                t.wait_ge(mm_extra_wait[0].h, mm_extra_wait[1])
            t.wait_ge(ev.h, ev_marks[ocg])
            for ol in range(2):
                for th in range(TH):
                    inst = None
                    for kc in range(KC):
                        inst = t.matmul(ps_proj[:, 2 * par + ol, 512 * th:512 * (th + 1)],
                                        w_pan[:, par, kc, 128 * ol:128 * ol + 128],
                                        xn_s[:, kc, 512 * th:512 * (th + 1)],
                                        start=(kc == 0),
                                        stop=(kc == KC - 1 and mm_extra is None),
                                        skip_group_check=True)
                    if mm_extra is not None:
                        inst = mm_extra(t, ocg, 2 * par + ol, th)
                    pe.inc(inst)
        pe_marks.append(pe.n)

        writer(ocg, pe.n)
        ev_marks.append(ev.n)



A_SEMS = ["sq", "pe", "dv", "ac", "xfd", "xfg", "xn", "xng", "statd",
          "bcp", "bcd", "wld", "wld1", "evac", "st", "aux", "fr", "rscS", "trS", "cpS",
          "mulS", "chd", "cha", "chg", "rot"]


def make_sems(P, names, nslot=True):
    sm = {k: P.sem(k) for k in names}
    if nslot:
        sm["xbs"] = [P.sem(f"xbs{i}") for i in range(3)]
        sm["xfs"] = [P.sem(f"xfs{i}") for i in range(4)]
    return sm


def emit_A(P, blk, sm, layer, d, scope, tag="", ag_hook=None):
    nc = P.nc
    xT_d = d["xT"]
    wq_d = d["wqT"]
    wlat_d = d["wlatT"]
    if layer == 0:
        wb_d = d["wbT"]; bias_d = d["biasb"]; ar_d = d["arT"]
        btr_d = d["BTr"]; id_d = d["identF"]
    else:
        tern_d = d["ternT"]; scl_d = d["scalesT"]
        cosq_d = d["cosq"]; sinq_d = d["sinq"]
    qT_o = d["qT"]; lat_o = d["lat"]; xp_o = d["xp"]

    def SB(name, shape, dt):
        return scope.enter_context(nc.sbuf_tensor(name + tag, shape, dt))

    xn_s = SB("xn_s", [128, KC, T], BF16)
    w_pan = SB("w_pan", [128, 2, KC, 256], BF16)
    stageb = SB("stageb", [128, 4, T], BF16)
    stagef = SB("stagef", [128, 4, T], F32)
    xf = SB("xf", [128, 4, T], F32)
    ones_bf = SB("ones_bf", [128, 1], BF16)
    ones_f = SB("ones_f", [1, 128], F32)
    if layer == 1:
        cosq_s = SB("cosq_s", [128, T], BF16)
        sinq_s = SB("sinq_s", [128, T], BF16)
        qh_buf = SB("qh_buf", [128, T], BF16)
        qrot_buf = SB("qrot_buf", [128, T], BF16)
        tA = SB("tA", [128, T], BF16)
        tB = SB("tB", [128, T], BF16)

    with nc.sbuf_tensor("xb" + tag, [128, 3, T], BF16) as xb, \
         nc.sbuf_tensor("xsq" + tag, [128, 2, T], BF16) as xsq, \
         nc.sbuf_tensor("t1d" + tag, [128, T], F32) as t1d, \
         nc.sbuf_tensor("t1g" + tag, [128, T], F32) as t1g, \
         nc.sbuf_tensor("rstd_bc" + tag, [128, T], F32) as rstd_bc, \
         nc.sbuf_tensor("mr_bc" + tag, [128, T], F32) as mr_bc, \
         nc.sbuf_tensor("rows" + tag, [1, 4 * T], F32) as rows, \
         nc.psum_tensor("ps_stat" + tag, [1, 2 * TH, 512], F32) as ps_stat, \
         nc.psum_tensor("ps_bc" + tag, [128, 2, T], F32) as ps_bc:
        b = dict(xb=xb, xsq=xsq, t1d=t1d, t1g=t1g, rstd_bc=rstd_bc, mr_bc=mr_bc,
                 rows=rows, ones_bf=ones_bf, ones_f=ones_f, xf=xf,
                 ps_stat=ps_stat, ps_bc=ps_bc)
        emit_ln(P, blk, xT_d, xn_s, b, sm)
        xn_total = ((sm["xn"], sm["xn"].n), (sm["xng"], sm["xng"].n))

    st = sm["st"]
    st_marks = [0]

    if _A_STOP == "ln":
        @blk.gpsimd
        def _(g):
            for s_, c_ in xn_total:
                g.wait_ge(s_.h, c_)
        return P

    with nc.psum_tensor("ps_proj" + tag, [128, 4, T], F32) as ps_proj:
        if layer == 1:
            @blk.gpsimd
            def _(g):
                sm["aux"].inc(g.dma_start(out=cosq_s[0:64, :], in_=cosq_d[:]), 16)
                sm["aux"].inc(g.dma_start(out=cosq_s[64:128, :], in_=cosq_d[:]), 16)
                sm["aux"].inc(g.dma_start(out=sinq_s[0:64, :], in_=sinq_d[:]), 16)
                sm["aux"].inc(g.dma_start(out=sinq_s[64:128, :], in_=sinq_d[:]), 16)
            cs_ld = sm["aux"].n

        rope_prev = [None, 0, 0]
        tA_mark = [0]
        tB_mark = [0]

        def lat_writer(ocg, mm_snap):
            par = ocg % 2

            @blk.vector
            def _(v, mm_snap=mm_snap, ocg=ocg, par=par):
                v.wait_ge(sm["pe"].h, mm_snap)
                for ol in range(2):
                    og = 2 * ocg + ol
                    sm["evac"].inc(v.tensor_copy(stageb[:, og, :],
                                                 ps_proj[:, 2 * par + ol, :]))
            ev_now = sm["evac"].n

            @blk.sync
            def _(sp, ev_now=ev_now, ocg=ocg):
                sp.wait_ge(sm["evac"].h, ev_now)
                for ol in range(2):
                    og = 2 * ocg + ol
                    st.inc(sp.dma_start(out=lat_o[128 * og:128 * (og + 1), :],
                                        in_=stageb[:, og, :]), 16)
            st_marks.append(st.n)

        emit_proj(P, blk, w_d=wlat_d, n_ocg=2, xn_s=xn_s, sm=sm, w_pan=w_pan,
                  ps_proj=ps_proj, writer=lat_writer, xn_total=xn_total)
        latst_all = st.n
        if ag_hook is not None:
            ag_hook[0](latst_all)

        def q_writer(ocg, mm_snap):
            par = ocg % 2
            if layer == 0:
                @blk.vector
                def _(v, ocg=ocg, mm_snap=mm_snap, par=par):
                    v.wait_ge(sm["pe"].h, mm_snap)
                    v.wait_ge(st.h, st_marks[ocg + 1])
                    for ol in range(2):
                        og = 2 * ocg + ol
                        sm["evac"].inc(v.tensor_copy(stageb[:, og % 4, :],
                                                     ps_proj[:, 2 * par + ol, :]))
            else:
                rot = sm["rot"]
                for ol in range(2):
                    og = 2 * ocg + ol
                    @blk.vector
                    def _(v, ol=ol, mm_snap=mm_snap, par=par):
                        v.wait_ge(sm["pe"].h, mm_snap)
                        if rope_prev[0] is not None:
                            v.wait_ge(rot.h, rope_prev[0])
                            v.wait_ge(sm["chd"].h, rope_prev[1])
                        sm["evac"].inc(v.tensor_copy(qh_buf[:], ps_proj[:, 2 * par + ol, :]))
                    qh_done = sm["evac"].n

                    @blk.sync
                    def _(sp, qh_done=qh_done):
                        sp.wait_ge(sm["evac"].h, qh_done)
                        if rope_prev[0] is not None:
                            sp.wait_ge(sm["chd"].h, rope_prev[2])
                        rot.inc(sp.dma_start(out=qrot_buf[0:64, :], in_=qh_buf[64:128, :]), 16)
                        rot.inc(sp.dma_start(out=qrot_buf[64:128, :], in_=qh_buf[0:64, :]), 16)
                    rot_now = rot.n

                    @blk.vector
                    def _(v, og=og, rot_now=rot_now, ocg=ocg):
                        ch = Chain(v, sm["chd"])
                        v.wait_ge(rot.h, rot_now)
                        v.wait_ge(sm["aux"].h, cs_ld)
                        v.wait_ge(st.h, st_marks[ocg + 1])
                        ch.op(v.tensor_scalar_mul(qrot_buf[0:64, :], qrot_buf[0:64, :], -1.0))
                        ch.op(v.tensor_mul(tA[:], qh_buf[:], cosq_s[:]))
                        tA_mark[0] = sm["chd"].n
                        ch.wait()
                        ch.op(v.tensor_mul(tB[:], qrot_buf[:], sinq_s[:]))
                        tB_mark[0] = sm["chd"].n
                        ch.wait()
                        ch.op(v.tensor_add(stageb[:, og % 4, :], tA[:], tB[:]), sm["evac"])
                    rope_prev[0] = rot_now
                    rope_prev[1] = tA_mark[0]
                    rope_prev[2] = tB_mark[0]
            ev_now = sm["evac"].n

            @blk.sync
            def _(sp, ocg=ocg, ev_now=ev_now):
                sp.wait_ge(sm["evac"].h, ev_now)
                for ol in range(2):
                    og = 2 * ocg + ol
                    st.inc(sp.dma_start(out=qT_o[128 * og:128 * (og + 1), :],
                                        in_=stageb[:, og % 4, :]), 16)
            st_marks.append(st.n)

        emit_proj(P, blk, w_d=wq_d, n_ocg=16, xn_s=xn_s, sm=sm, w_pan=w_pan,
                  ps_proj=ps_proj, writer=q_writer, xn_total=xn_total,
                  prev_pe_mark=sm["pe"].n, prev_evac_mark=sm["evac"].n)
        qst_all = st.n
        if ag_hook is not None:
            ag_hook[1](qst_all)

    if _A_STOP in ("q", "lat"):
        @blk.gpsimd
        def _(g):
            g.wait_ge(st.h, st.n)
        return P

    ffn_scope = ExitStack()
    if layer == 0:
        ar_pan = ffn_scope.enter_context(nc.sbuf_tensor("ar_pan" + tag, [128, KC, 80], BF16))
        bt_pan = ffn_scope.enter_context(nc.sbuf_tensor("bt_pan" + tag, [64, 2, 16, 256], BF16))
        r_tm = ffn_scope.enter_context(nc.sbuf_tensor("r_tm" + tag, [128, 8, 64], BF16))
        reT = ffn_scope.enter_context(nc.sbuf_tensor("reT" + tag, [64, 16, T], BF16))
        rsc = ffn_scope.enter_context(nc.sbuf_tensor("rsc" + tag, [128, 16, 64], BF16))
        comb = ffn_scope.enter_context(nc.sbuf_tensor("comb" + tag, [128, 16], F32))
        esc = ffn_scope.enter_context(nc.sbuf_tensor("esc" + tag, [128, 48], F32))
        srow = ffn_scope.enter_context(nc.sbuf_tensor("srow" + tag, [128, 12], F32))
        bias_s = ffn_scope.enter_context(nc.sbuf_tensor("bias_s" + tag, [128, KC], F32))
        ident_s = ffn_scope.enter_context(nc.sbuf_tensor("ident_s" + tag, [128, 128], BF16))

        @blk.gpsimd
        def _(g):
            src = ar_d.rearrange("(kc p) a -> p kc a", p=128)
            sm["aux"].inc(g.dma_start(out=ar_pan[:, 0:16, :], in_=src[:, 0:16, :]), 16)
            sm["aux"].inc(g.dma_start(out=ar_pan[:, 16:32, :], in_=src[:, 16:32, :]), 16)
            sm["aux"].inc(g.dma_start(out=bias_s[:], in_=bias_d[:]), 16)
            sm["aux"].inc(g.dma_start(out=ident_s[:], in_=id_d[:]), 16)
        ffn_ld = sm["aux"].n

        fr = sm["fr"]; rscS = sm["rscS"]; trS = sm["trS"]; cpS = sm["cpS"]
        fr_base = fr.n
        rsc_base = rscS.n
        tr_base = trS.n
        cp_base = cpS.n
        dv_marks = {}
        ac_marks = {}
        pe_prev = sm["pe"].n

        with nc.psum_tensor("ps_r" + tag, [128, 80], F32) as ps_r, \
             nc.psum_tensor("ps_tr0" + tag, [64, 128], BF16) as ps_tr0, \
             nc.psum_tensor("ps_tr1" + tag, [64, 128], BF16) as ps_tr1:
            ps_trs = [ps_tr0, ps_tr1]
            for ts in range(8):
                @blk.tensor
                def _(t, ts=ts):
                    t.wait_ge(sm["aux"].h, ffn_ld)
                    for s_, c_ in xn_total:
                        t.wait_ge(s_.h, c_)

                    if ts == 0:
                        t.wait_ge(sm["evac"].h, sm["evac"].n)
                    else:
                        t.wait_ge(sm["dv"].h, dv_marks[ts - 1][0])
                        t.wait_ge(sm["chd"].h, dv_marks[ts - 1][1])
                        t.wait_ge(sm["ac"].h, ac_marks[ts - 1])
                    inst = None
                    for kc in range(KC):
                        inst = t.matmul(ps_r[:], xn_s[:, kc, 128 * ts:128 * (ts + 1)],
                                        ar_pan[:, kc, :], start=(kc == 0),
                                        stop=(kc == KC - 1), skip_group_check=True)
                    fr.inc(inst)
                fr_now = fr.n

                @blk.vector
                def _(v, fr_now=fr_now, ts=ts):
                    ch = Chain(v, sm["chd"])
                    v.wait_ge(fr.h, fr_now)
                    ch.op(v.tensor_copy(r_tm[:, ts, :], ps_r[:, 0:64]))
                    ch.op(v.reduce_max(srow[:, 0:1], ps_r[:, 64:80], axis=AX.X))
                    ch.wait()
                    ch.op(v.tensor_scalar_mul(srow[:, 1:2], srow[:, 0:1], -1.0), sm["dv"])
                dv_m = sm["dv"].n
                dv_marks[ts] = (dv_m, sm["chd"].n)

                @blk.scalar
                def _(s, dv_m=dv_m):
                    s.wait_ge(sm["dv"].h, dv_m)
                    sm["ac"].inc(s.activation(out=comb[:], in_=ps_r[:, 64:80],
                                              func=AF.Exp, bias=srow[:, 1:2]))
                ac_marks[ts] = sm["ac"].n
                ac_m = sm["ac"].n

                @blk.vector
                def _(v, ts=ts, ac_m=ac_m):
                    ch = Chain(v, sm["chd"])
                    v.wait_ge(sm["ac"].h, ac_m)
                    ch.op(v.reduce_sum(srow[:, 2:3], comb[:], axis=AX.X))
                    ch.wait()
                    ch.op(v.reciprocal(srow[:, 3:4], srow[:, 2:3]))
                    ch.wait()
                    ch.op(v.tensor_scalar_mul(comb[:], comb[:], srow[:, 3:4]))
                    ch.wait()
                    ch.op(v.reduce_max(srow[:, 4:5], comb[:], axis=AX.X))
                    ch.wait()
                    ch.op(v.tensor_scalar(esc[:, 0:16], comb[:], srow[:, 4:5], None,
                                          OP.is_equal))
                    ch.wait()
                    ch.op(v.scalar_tensor_tensor(esc[:, 16:32], esc[:, 0:16], NEG, comb[:],
                                                 OP.mult, OP.add))
                    ch.wait()
                    ch.op(v.reduce_max(srow[:, 5:6], esc[:, 16:32], axis=AX.X))
                    ch.wait()
                    ch.op(v.tensor_scalar(esc[:, 32:48], comb[:], srow[:, 5:6], None,
                                          OP.is_equal))
                    ch.op(v.tensor_add(srow[:, 6:7], srow[:, 4:5], srow[:, 5:6]))
                    ch.wait()
                    ch.op(v.tensor_scalar_add(srow[:, 6:7], srow[:, 6:7], 1e-6))
                    ch.wait()
                    ch.op(v.reciprocal(srow[:, 7:8], srow[:, 6:7]))
                    ch.wait()
                    ch.op(v.tensor_mul(srow[:, 8:9], srow[:, 4:5], srow[:, 7:8]))
                    ch.op(v.tensor_mul(srow[:, 9:10], srow[:, 5:6], srow[:, 7:8]))
                    ch.wait()
                    ch.op(v.tensor_scalar_mul(srow[:, 8:9], srow[:, 8:9], 0.5))
                    ch.op(v.tensor_scalar_mul(srow[:, 9:10], srow[:, 9:10], 0.5))
                    ch.wait()
                    ch.op(v.tensor_scalar_mul(esc[:, 32:48], esc[:, 32:48], srow[:, 9:10]))
                    ch.wait()
                    ch.op(v.scalar_tensor_tensor(comb[:], esc[:, 0:16], srow[:, 8:9],
                                                 esc[:, 32:48], OP.mult, OP.add))
                    for e in range(16):
                        k = 16 * ts + e
                        ch.wait()
                        if k >= 16:
                            v.wait_ge(trS.h, tr_base + (k - 16) + 1)
                        ch.op(v.tensor_scalar_mul(rsc[:, e, :], r_tm[:, ts, :],
                                                  comb[:, e:e + 1]), rscS)

                @blk.tensor
                def _(t, ts=ts):
                    for e in range(16):
                        k = 16 * ts + e
                        t.wait_ge(rscS.h, rsc_base + k + 1)
                        if k >= 2:
                            t.wait_ge(cpS.h, cp_base + (k - 2) + 1)
                        trS.inc(t.transpose(ps_trs[k % 2][:], rsc[:, e, :], ident_s[:]))

                @blk.vector
                def _(v, ts=ts):
                    for e in range(16):
                        k = 16 * ts + e
                        v.wait_ge(trS.h, tr_base + k + 1)
                        cpS.inc(v.tensor_copy(reT[:, e, 128 * ts:128 * (ts + 1)],
                                              ps_trs[k % 2][:]))
        reT_done = cpS.n

        if _A_STOP == "prep":
            @blk.gpsimd
            def _(g):
                g.wait_ge(st.h, st.n)
                g.wait_ge(cpS.h, reT_done)
            ffn_scope.close()
            return P

        def panel_extra(g, ocg, col0, par, ld):
            src = wb_d[:, col0:col0 + 256].rearrange("(kc p) oc -> p kc oc", p=128)
            for sub in range(2):
                ld.inc(g.dma_start(out=w_pan[:, par, 16 * sub:16 * sub + 16, :],
                                   in_=src[:, 16 * sub:16 * sub + 16, :]), 16)
            ld.inc(g.dma_start(out=bt_pan[:, par, :, :],
                               in_=btr_d[:, :, col0:col0 + 256]), 16)

        def mm_extra(t, ocg, ol_abs, th):
            par = ocg % 2
            ol = ol_abs - 2 * par
            inst = None
            for e in range(16):
                inst = t.matmul(ps_proj2[:, ol_abs, 512 * th:512 * (th + 1)],
                                bt_pan[:, par, e, 128 * ol:128 * ol + 128],
                                reT[:, e, 512 * th:512 * (th + 1)],
                                start=False, stop=(e == 15), skip_group_check=True)
            return inst
        mm_extra_wait = (sm["cpS"], reT_done)
    else:
        s_pan = ffn_scope.enter_context(nc.sbuf_tensor("s_pan" + tag, [128, KC, 256], BF16))

        def panel_extra(g, ocg, col0, par, ld):
            for sub in range(2):
                ld.inc(g.dma_start(
                    out=s_pan[:, 16 * sub:16 * sub + 16, :],
                    in_=scl_d[16 * sub:16 * sub + 16, col0:col0 + 256][None, :, :]
                        .partition_broadcast(128)), 16)
                ld.inc(g.dma_start(
                    out=w_pan[:, par, 16 * sub:16 * sub + 16, :],
                    in_=tern_d[:, col0:col0 + 256]
                        .rearrange("(kc p) oc -> p kc oc", p=128)[:, 16 * sub:16 * sub + 16, :]), 16)
            g.wait_ge(ld.h, ld.n)
            sm["mulS"].inc(g.tensor_mul(w_pan[:, par, :, :], w_pan[:, par, :, :], s_pan[:]))
        mm_extra = None
        mm_extra_wait = None

    xfs = sm["xfs"]
    ffn_ev_marks = [sm["evac"].n]
    ln_xf_done = (sm["xfd"].n, sm["xfg"].n)

    with nc.psum_tensor("ps_proj2" + tag, [128, 4, T], F32) as ps_proj2:

        def ffn_writer(ocg, mm_snap):
            par = ocg % 2

            @blk.sync
            def _(sp, ocg=ocg):
                sp.wait_ge(sm["evac"].h, ffn_ev_marks[ocg])
                if ocg == 0:
                    sp.wait_ge(sm["xfd"].h, ln_xf_done[0])
                    sp.wait_ge(sm["xfg"].h, ln_xf_done[1])
                for ol in range(2):
                    og = 2 * ocg + ol
                    xfs[og % 4].inc(sp.dma_start(out=xf[:, og % 4, :],
                                                 in_=xT_d[128 * og:128 * (og + 1), :]), 16)
            marks = tuple(xfs[i].n for i in range(4))

            @blk.vector
            def _(v, ocg=ocg, mm_snap=mm_snap, marks=marks, par=par):
                v.wait_ge(sm["pe"].h, mm_snap)
                v.wait_ge(st.h, st_marks[-1] if ocg == 0 else st_marks[-2])
                for ol in range(2):
                    og = 2 * ocg + ol
                    v.wait_ge(xfs[og % 4].h, marks[og % 4])
                    if layer == 0:
                        inst = v.scalar_tensor_tensor(stagef[:, og % 4, :],
                                                      ps_proj2[:, 2 * par + ol, :],
                                                      bias_s[:, og:og + 1], xf[:, og % 4, :],
                                                      OP.add, OP.add)
                    else:
                        inst = v.tensor_add(stagef[:, og % 4, :],
                                            ps_proj2[:, 2 * par + ol, :],
                                            xf[:, og % 4, :])
                    sm["evac"].inc(inst)
            ev_now = sm["evac"].n

            @blk.sync
            def _(sp, ocg=ocg, ev_now=ev_now):
                sp.wait_ge(sm["evac"].h, ev_now)
                for ol in range(2):
                    og = 2 * ocg + ol
                    st.inc(sp.dma_start(out=xp_o[128 * og:128 * (og + 1), :],
                                        in_=stagef[:, og % 4, :]), 16)
            st_marks.append(st.n)
            ffn_ev_marks.append(sm["evac"].n)

        emit_proj(P, blk, w_d=None, n_ocg=16, xn_s=xn_s, sm=sm, w_pan=w_pan,
                  ps_proj=ps_proj2, writer=ffn_writer, xn_total=xn_total,
                  prev_pe_mark=sm["pe"].n, prev_evac_mark=sm["evac"].n,
                  panel_extra=panel_extra, mm_extra=mm_extra,
                  mm_extra_wait=mm_extra_wait,
                  panel_sem_extra=(sm["mulS"] if layer == 1 else None))

    ffn_scope.close()


def build_A(layer):
    P = Prog()
    d = {"xT": P.din("xT", [DM, T], F32),
         "wqT": P.din("wqT", [DM, DM], F32),
         "wlatT": P.din("wlatT", [DM, 512], F32),
         "qT": P.dout("qT", [DM, T], BF16),
         "lat": P.dout("latT", [512, T], BF16),
         "xp": P.dout("xp", [DM, T], F32)}
    if layer == 0:
        d.update(wbT=P.din("wbT", [DM, DM], F32),
                 biasb=P.din("biasb", [128, KC], F32),
                 arT=P.din("arT", [DM, 80], F32),
                 BTr=P.din("BTr", [64, 16, DM], F32),
                 identF=P.din("identF", [128, 128], BF16))
    else:
        d.update(ternT=P.din("ternT", [DM, DM], I8),
                 scalesT=P.din("scalesT", [KC, DM], F32),
                 cosq=P.din("cosq", [64, T], F32),
                 sinq=P.din("sinq", [64, T], F32))
    sm = make_sems(P, A_SEMS)
    blk = P.ctx.enter_context(P.nc.Block())
    emit_A(P, blk, sm, layer, d, P.ctx)

    @blk.gpsimd
    def _(g):
        g.wait_ge(sm["st"].h, sm["st"].n)

    return P



_CONSTS = None


def host_consts():
    global _CONSTS
    if _CONSTS is None:
        inv = 1.0 / (10000.0 ** (np.arange(64) / 64.0))
        fr = np.arange(S, dtype=np.float64)[None, :] * inv[:, None]
        cosT = np.cos(fr).astype(np.float32)
        sinT = np.sin(fr).astype(np.float32)
        ks, i, j = np.meshgrid(np.arange(4), np.arange(128), np.arange(512),
                               indexing="ij")
        masks = np.where(128 * ks + i > j, np.float32(NEG), np.float32(0.0))
        ident = np.eye(128, dtype=BF)
        _CONSTS = (cosT, sinT, np.ascontiguousarray(masks), ident)
    return _CONSTS


def prep_weights(inputs):
    C = lambda a: np.ascontiguousarray(a)
    w = {}
    w["wqT0"] = C(inputs["L0_attn_Wq"].T)
    w["wlatT0"] = C(np.concatenate([inputs["L0_attn_Wk_down"],
                                    inputs["L0_attn_Wv_down"]], axis=0).T)
    w["wbT0"] = C(inputs["L0_ffn_base_w"].T)
    w["biasb0"] = C(inputs["L0_ffn_base_b"].reshape(KC, 128).T)
    w["arT0"] = C(np.concatenate([inputs["L0_ffn_A"],
                                  inputs["L0_ffn_router_w"]], axis=0).T)
    w["BTr0"] = C(inputs["L0_ffn_B"].transpose(2, 0, 1))
    w["wkupT"] = [C(inputs["L0_attn_Wk_up"][128 * c:128 * (c + 1), :].T)
                  for c in range(NCORE)]
    w["wvupT"] = [C(inputs["L0_attn_Wv_up"][128 * c:128 * (c + 1), :].T)
                  for c in range(NCORE)]
    w["woT0"] = C(inputs["L0_attn_Wo"].T)
    w["wqT1"] = C(inputs["L1_attn_Wq"])
    w["wlatT1"] = C(inputs["L1_attn_Wc"])
    w["wk1"] = [C(inputs["L1_attn_Wk"][:, 512 * c:512 * (c + 1)]) for c in range(NCORE)]
    w["wv1"] = [C(inputs["L1_attn_Wv"][:, 512 * c:512 * (c + 1)]) for c in range(NCORE)]
    w["woT1"] = C(inputs["L1_attn_Wo"])
    w["ternT"] = C(inputs["L1_ffn_ternary"].T)
    w["scalesT"] = C(inputs["L1_ffn_scales"].reshape(DM, KC).T)
    return w


def run_A(layer, xT_cols, w):
    P = build_A(layer)
    cosT, sinT, masks, ident = host_consts()
    in_maps = []
    for c in range(NCORE):
        m = {"xT": xT_cols[c]}
        if layer == 0:
            m.update(wqT=w["wqT0"], wlatT=w["wlatT0"], wbT=w["wbT0"],
                     biasb=w["biasb0"], arT=w["arT0"], BTr=w["BTr0"], identF=ident)
        else:
            m.update(wqT=w["wqT1"], wlatT=w["wlatT1"], ternT=w["ternT"],
                     scalesT=w["scalesT"],
                     cosq=np.ascontiguousarray(cosT[:, T * c:T * (c + 1)]),
                     sinq=np.ascontiguousarray(sinT[:, T * c:T * (c + 1)]))
        in_maps.append(m)
    res = run_bass_kernel_spmd(P.nc, in_maps, core_ids=list(range(NCORE)))
    qT = [np.asarray(r["qT"]) for r in res.results]
    latT = [np.asarray(r["latT"]) for r in res.results]
    xp = [np.asarray(r["xp"]) for r in res.results]
    return qT, latT, xp



B_SEMS = ["gld", "sld", "qld", "st", "kb", "ke", "vb", "ve", "nq", "nmx",
          "sS", "sE", "sM", "sAV", "nrm", "trP", "ocp", "mcl", "acn", "rpd",
          "rotk", "chd", "cha", "dvm", "stq0", "stq1", "stq2"]


def emit_B(P, blk, sm, layer, d, scope, tag=""):
    nc = P.nc
    qTg_d = d["qTg"]
    latg_d = d["latg"]
    id_d = d["ident"]
    if layer == 0:
        wku_d = d["wkupT"]; wvu_d = d["wvupT"]; mask_d = d["masks"]
    else:
        wk1_d = d["wk1"]; wv1_d = d["wv1"]; cos_d = d["cosT"]; sin_d = d["sinT"]
    out_d = d["outT"]

    def SB(name, shape, dt):
        return scope.enter_context(nc.sbuf_tensor(name + tag, shape, dt))

    lat_s = SB("lat_s", [128, 4, S], BF16)
    k_fm = SB("k_fm", [128, S], BF16)
    v_s = SB("v_s", [128, 64, 130], BF16)
    q_s = SB("q_s", [128, S], BF16)
    pT0 = SB("pT0", [128, 2, 512], BF16)
    pT1 = SB("pT1", [128, 2, 512], BF16)
    ot = SB("ot", [128, 4, 128], BF16)
    ostage = SB("ostage", [128, 512], BF16)
    sq_s = SB("sq_s", [128, S], BF16)
    nrow = SB("nrow", [1, 64], F32)
    mcol = SB("mcol", [128, 1], F32)
    rl_s = SB("rl_s", [128, 2], F32)
    ones_bf = SB("ones_bf", [128, 1], BF16)
    ones_f = SB("ones_f", [1, 128], F32)
    ident_s = SB("ident_s", [128, 128], BF16)
    if layer == 0:
        wka = SB("wka", [128, 4, 128], BF16)
        wkb = SB("wkb", [128, 4, 128], BF16)
        wva = SB("wva", [128, 4, 128], BF16)
        wvb = SB("wvb", [128, 4, 128], BF16)
        mask_s = SB("mask_s", [128, 4, 512], F32)
    else:
        wk1_s = SB("wk1_s", [128, 4, 512], BF16)
        wv1_s = SB("wv1_s", [128, 4, 512], BF16)
        cos_s = SB("cos_s", [128, S], BF16)
        sin_s = SB("sin_s", [128, S], BF16)
        CH = 2048
        krot = SB("krot", [128, CH], BF16)
        tA = SB("tA", [128, CH], BF16)
        tB = SB("tB", [128, CH], BF16)

    ps_S0 = scope.enter_context(nc.psum_tensor("ps_S0" + tag, [128, 2, 512], F32))
    ps_S1 = scope.enter_context(nc.psum_tensor("ps_S1" + tag, [128, 2, 512], F32))
    ps_av0 = scope.enter_context(nc.psum_tensor("ps_av0" + tag, [128, 2, 130], F32))
    ps_av1 = scope.enter_context(nc.psum_tensor("ps_av1" + tag, [128, 2, 130], F32))
    ps_tr = scope.enter_context(nc.psum_tensor("ps_tr" + tag, [128, 128], BF16))
    ps_n = scope.enter_context(nc.psum_tensor("ps_n" + tag, [128, 128], F32))
    psS_tiles = [ps_S0[:, 0, :], ps_S0[:, 1, :], ps_S1[:, 0, :], ps_S1[:, 1, :]]
    psav_tiles = [ps_av0[:, 0, 0:128], ps_av0[:, 1, 0:128],
                  ps_av1[:, 0, 0:128], ps_av1[:, 1, 0:128]]

    gld = sm["gld"]; sld = sm["sld"]; qld = sm["qld"]; st = sm["st"]

    if "qsel" in d:
        qsel_s = SB("qsel_s", [128, 32], mybir.dt.int32)

    @blk.gpsimd
    def _(g):
        if "qsel" in d:
            gld.inc(g.dma_start(out=qsel_s[:], in_=d["qsel"][:]), 16)
        if layer == 0:
            src = wku_d.rearrange("(j u) d -> u j d", u=64)
            gld.inc(g.dma_start(out=wka[0:64, :, :], in_=src), 16)
            gld.inc(g.dma_start(out=wkb[64:128, :, :], in_=src), 16)
            srcv = wvu_d.rearrange("(j u) d -> u j d", u=64)
            gld.inc(g.dma_start(out=wva[0:64, :, :], in_=srcv), 16)
            gld.inc(g.dma_start(out=wvb[64:128, :, :], in_=srcv), 16)
        else:
            gld.inc(g.dma_start(out=wk1_s[:], in_=wk1_d.rearrange("(lc p) o -> p lc o", p=128)), 16)
            gld.inc(g.dma_start(out=wv1_s[:], in_=wv1_d.rearrange("(lc p) o -> p lc o", p=128)), 16)
            gld.inc(g.dma_start(out=cos_s[0:64, :], in_=cos_d[:]), 16)
            gld.inc(g.dma_start(out=cos_s[64:128, :], in_=cos_d[:]), 16)
            gld.inc(g.dma_start(out=sin_s[0:64, :], in_=sin_d[:]), 16)
            gld.inc(g.dma_start(out=sin_s[64:128, :], in_=sin_d[:]), 16)
    gld_n = gld.n

    @blk.sync
    def _(sp):
        if "ccw_lat" in d:
            sp.wait_ge(d["ccw_lat"][0].h, d["ccw_lat"][1])
        for r in range(NCORE):
            if layer == 0:
                for h in range(8):
                    p0 = 64 * (h % 2)
                    sld.inc(sp.dma_start(out=lat_s[p0:p0 + 32, h // 2, T * r:T * (r + 1)],
                                         in_=latg_d[512 * r + 32 * h:512 * r + 32 * h + 32, :]), 16)
                    sld.inc(sp.dma_start(out=lat_s[p0 + 32:p0 + 64, h // 2, T * r:T * (r + 1)],
                                         in_=latg_d[512 * r + 256 + 32 * h:512 * r + 256 + 32 * h + 32, :]), 16)
            else:
                sld.inc(sp.dma_start(
                    out=lat_s[:, :, T * r:T * (r + 1)],
                    in_=latg_d[512 * r:512 * (r + 1), :].rearrange("(lc p) t -> p lc t", p=128)), 16)
        if layer == 0:
            sld.inc(sp.dma_start(out=mask_s[:], in_=mask_d.rearrange("a p q -> p a q")), 16)
        sld.inc(sp.dma_start(out=ident_s[:], in_=id_d[:]), 16)
    sld_n = sld.n

    @blk.vector
    def _(v):
        ch = Chain(v, sm["dvm"])
        ch.op(v.memset(v_s[:, :, 128:129], 1.0))
        ch.op(v.memset(ones_bf[:], 1.0))
        ch.op(v.memset(ones_f[:], 1.0))
    memset_n = sm["dvm"].n

    NH = 4
    lat_r = lat_s.rearrange("p c (m j) -> p c m j", j=8) if layer == 0 else None

    def emit_kv_build(hh):
        kb = sm["kb"]; ke = sm["ke"]; vb = sm["vb"]; ve = sm["ve"]
        kb_base = kb.n
        ke_base = ke.n

        @blk.tensor
        def _(t, hh=hh):
            t.wait_ge(gld.h, gld_n)
            t.wait_ge(sld.h, sld_n)
            if layer == 1 and hh > 0:
                t.wait_ge(sm["sE"].h, sm["sE"].n)
            for tb in range(16):
                if tb >= 4:
                    t.wait_ge(ke.h, ke_base + (tb - 4) + 1)
                tile = psS_tiles[tb % 4]
                if layer == 0:
                    h = tb // 2
                    p0 = 64 * (h % 2)
                    wk_t = wka if h % 2 == 0 else wkb
                    for j in range(4):
                        inst = t.matmul(tile, wk_t[p0:p0 + 64, j, :],
                                        lat_r[p0:p0 + 64, h // 2, 512 * (tb % 2):512 * (tb % 2 + 1), j],
                                        start=(j == 0), stop=(j == 3), skip_group_check=True)
                else:
                    for lc in range(4):
                        inst = t.matmul(tile, wk1_s[:, lc, 128 * hh:128 * (hh + 1)],
                                        lat_s[:, lc, 512 * tb:512 * (tb + 1)],
                                        start=(lc == 0), stop=(lc == 3), skip_group_check=True)
                kb.inc(inst)
        kb_now = kb.n

        @blk.scalar
        def _(s):
            for tb in range(16):
                s.wait_ge(kb.h, kb_base + tb + 1)
                ke.inc(s.copy(k_fm[:, 512 * tb:512 * (tb + 1)], psS_tiles[tb % 4]))
        ke_now = ke.n

        vb_base = vb.n
        ve_base = ve.n

        nrm_prev = sm["nrm"].n

        vorder = [0, 2, 1, 3]

        @blk.tensor
        def _(t, hh=hh):
            if layer == 1 and hh > 0:
                t.wait_ge(sm["nrm"].h, nrm_prev)
            for i in range(64):
                if i >= 2:
                    t.wait_ge(ve.h, ve_base + (i - 2) + 1)
                tile = psav_tiles[vorder[i % 4]]
                if layer == 0:
                    h = i // 8
                    ms = i % 8
                    p0 = 64 * (h % 2)
                    wv_t = wva if h % 2 == 0 else wvb
                    for j in range(4):
                        inst = t.matmul(tile,
                                        lat_r[p0:p0 + 64, h // 2, 128 * ms:128 * (ms + 1), 4 + j],
                                        wv_t[p0:p0 + 64, j, :],
                                        start=(j == 0), stop=(j == 3), skip_group_check=True)
                else:
                    for lc in range(4):
                        inst = t.matmul(tile, lat_s[:, lc, 128 * i:128 * (i + 1)],
                                        wv1_s[:, lc, 128 * hh:128 * (hh + 1)],
                                        start=(lc == 0), stop=(lc == 3), skip_group_check=True)
                vb.inc(inst)

        @blk.vector
        def _(v):
            for i in range(64):
                v.wait_ge(vb.h, vb_base + i + 1)
                ve.inc(v.tensor_copy(v_s[:, i, 0:128], psav_tiles[vorder[i % 4]]))
        return ke_now, ve.n

    def emit_rope_k(hh):
        rotk = sm["rotk"]; rpd = sm["rpd"]; ke = sm["ke"]
        ke_now = ke.n
        for cc in range(S // CH):
            rot_base = rotk.n

            @blk.sync
            def _(sp, cc=cc):
                sp.wait_ge(ke.h, ke_now)
                if cc > 0:
                    sp.wait_ge(rpd.h, rpd.n)
                sl = slice(CH * cc, CH * (cc + 1))
                rotk.inc(sp.dma_start(out=krot[0:64, :], in_=k_fm[64:128, sl]), 16)
                rotk.inc(sp.dma_start(out=krot[64:128, :], in_=k_fm[0:64, sl]), 16)
            rot_now = rotk.n

            @blk.vector
            def _(v, cc=cc):
                ch = Chain(v, sm["chd"])
                v.wait_ge(rotk.h, rot_now)
                v.wait_ge(gld.h, gld_n)
                sl = slice(CH * cc, CH * (cc + 1))
                ch.op(v.tensor_scalar_mul(krot[0:64, :], krot[0:64, :], -1.0))
                ch.op(v.tensor_mul(tA[:], k_fm[:, sl], cos_s[:, sl]))
                ch.wait()
                ch.op(v.tensor_mul(tB[:], krot[:], sin_s[:, sl]))
                ch.wait()
                ch.op(v.tensor_add(k_fm[:, sl], tA[:], tB[:]), rpd)

    def emit_norms(sq_src, n_tiles, out_slot, waits):
        nq = sm["nq"]; nmx = sm["nmx"]; acn = sm["acn"]
        acn_base = acn.n

        @blk.scalar
        def _(s):
            for sem_, val_ in waits:
                s.wait_ge(sem_.h, val_)
            acn.inc(s.activation(out=sq_s[:], in_=sq_src[:], func=AF.Square))
        acn_now = acn.n
        nq_base = nq.n
        nmx_base = nmx.n

        @blk.tensor
        def _(t):
            t.wait_ge(acn.h, acn_now)
            for tb in range(16):
                if tb >= 4:
                    t.wait_ge(nmx.h, nmx_base + (tb - 4) + 1)
                else:
                    t.wait_ge(nmx.h, nmx_base)
                nq.inc(t.matmul(psS_tiles[tb % 4][0:1, :], ones_bf[:],
                                sq_s[:, 512 * tb:512 * (tb + 1)],
                                start=True, stop=True, skip_group_check=True))

        @blk.vector
        def _(v):
            ch = Chain(v, nmx)
            for tb in range(16):
                v.wait_ge(nq.h, nq_base + tb + 1)
                ch.op(v.reduce_max(nrow[0:1, 32 + tb:33 + tb], psS_tiles[tb % 4][0:1, :],
                                   axis=AX.X))
            ch.wait()
            ch.op(v.reduce_max(nrow[0:1, out_slot:out_slot + 1],
                               nrow[0:1, 32:48], axis=AX.X))
        return nmx.n

    def emit_M(kslot, qslot, waits):
        chd = sm["chd"]; cha = sm["cha"]; mcl = sm["mcl"]; acn = sm["acn"]
        nq = sm["nq"]

        @blk.vector
        def _(v):
            ch = Chain(v, chd)
            for sem_, val_ in waits:
                v.wait_ge(sem_.h, val_)
            ch.op(v.tensor_mul(nrow[0:1, 50:51], nrow[0:1, kslot:kslot + 1],
                               nrow[0:1, qslot:qslot + 1]))
        chd_1 = chd.n

        @blk.scalar
        def _(s):
            s.wait_ge(chd.h, chd_1)
            acn.inc(s.activation(out=nrow[0:1, 51:52], in_=nrow[0:1, 50:51], func=AF.Ln))
        acn_1 = acn.n

        @blk.vector
        def _(v):
            ch = Chain(v, chd)
            v.wait_ge(acn.h, acn_1)
            ch.op(v.tensor_scalar(nrow[0:1, 52:53], nrow[0:1, 51:52], 0.5, LOG_SCALE,
                                  OP.mult, OP.add))
        chd_2 = chd.n

        @blk.scalar
        def _(s):
            s.wait_ge(chd.h, chd_2)
            acn.inc(s.activation(out=nrow[0:1, 53:54], in_=nrow[0:1, 52:53], func=AF.Exp))
        acn_2 = acn.n

        @blk.vector
        def _(v):
            ch = Chain(v, chd)
            v.wait_ge(acn.h, acn_2)
            ch.op(v.tensor_scalar_mul(nrow[0:1, 54:55], nrow[0:1, 53:54], -1.0), sm["dvm"])
        dvm_1 = sm["dvm"].n

        mcl_prev = mcl.n
        sE_prev = sm["sE"].n

        @blk.tensor
        def _(t):
            t.wait_ge(sm["dvm"].h, dvm_1)
            t.wait_ge(mcl.h, mcl_prev)
            nq.inc(t.matmul(ps_n[:, 0:1], ones_f[:], nrow[0:1, 54:55],
                            start=True, stop=True, skip_group_check=True))
        nq_1 = sm["nq"].n

        @blk.vector
        def _(v):
            v.wait_ge(sm["nq"].h, nq_1)
            v.wait_ge(sm["sE"].h, sE_prev)
            mcl.inc(v.tensor_copy(mcol[:], ps_n[:, 0:1]))
        return mcl.n

    def units_for(qc):
        kbs = range(qc + 1) if layer == 0 else range(16)
        return [(kb, ksp) for kb in kbs for ksp in range(2)]

    def emit_sdpa_head(h, mcl_mark):
        sS = sm["sS"]; sE = sm["sE"]; sMm = sm["sM"]; sAV = sm["sAV"]
        nrm = sm["nrm"]; trP = sm["trP"]; ocp = sm["ocp"]
        sS_base = sS.n
        sE_base = sE.n
        sM_base = sMm.n
        sAV_base = sAV.n
        nrm_base = nrm.n
        trP_base = trP.n
        ocp_base = ocp.n
        q_bases = {0: sm["stq0"].n, 1: sm["stq1"].n, 2: sm["stq2"].n, 3: st.n}
        psS_par = [ps_S0, ps_S1]
        pT_par = [pT0, pT1]

        all_units = []
        for qc in range(16):
            for u in units_for(qc):
                all_units.append((qc,) + u)
        NU = len(all_units)
        qc_first = {}
        qc_last = {}
        for i, (qc, kb, ksp) in enumerate(all_units):
            qc_first.setdefault(qc, i)
            qc_last[qc] = i
        diag_units = {i for i, (qc, kb, ksp) in enumerate(all_units)
                      if layer == 0 and kb == qc}
        sM_cum = []
        acc = 0
        for i in range(NU):
            if i in diag_units:
                acc += 2
            sM_cum.append(acc)

        def emit_av(t, i):
            qc, kb, ksp = all_units[i]
            if i == qc_first[qc] and qc > 0:
                t.wait_ge(nrm.h, nrm_base + 4 * qc)
            t.wait_ge(sE.h, sE_base + i + 1)
            for j in range(2):
                ks = 4 * kb + 2 * ksp + j
                first = (i == qc_first[qc]) and (j == 0)
                last = (i == qc_last[qc]) and (j == 1)
                for qs in range(4):
                    inst = t.matmul(
                        (ps_av0 if qs < 2 else ps_av1)[:, qs % 2, 0:129],
                        pT_par[i % 2][:, j, 128 * qs:128 * (qs + 1)],
                        v_s[:, ks, 0:129],
                        start=(first and qs % 2 == 0), stop=last,
                        skip_group_check=True)
            sAV.inc(inst)

        def emit_transposes(t, qc):
            for qs in range(4):
                t.wait_ge(nrm.h, nrm_base + 4 * qc + qs + 1)
                if trP.n - trP_base > 0:
                    t.wait_ge(ocp.h, ocp_base + (trP.n - trP_base))
                trP.inc(t.transpose(ps_tr[:], ot[:, qs, :], ident_s[:]))

        @blk.tensor
        def _(t):
            for i, (qc, kb, ksp) in enumerate(all_units):
                if i < 2:
                    t.wait_ge(sm["nmx"].h, sm["nmx"].n)
                    t.wait_ge(sm["mcl"].h, mcl_mark)
                    t.wait_ge(sm["ve"].h, sm["ve"].n)
                    t.wait_ge(sm["ke"].h, sm["ke"].n)
                    if layer == 1:
                        t.wait_ge(sm["rpd"].h, sm["rpd"].n)
                for j in range(2):
                    ks = 4 * kb + 2 * ksp + j
                    inst = t.matmul(psS_par[i % 2][:, j, :],
                                    k_fm[:, 128 * ks:128 * (ks + 1)],
                                    q_s[:, 512 * qc:512 * (qc + 1)],
                                    start=True, stop=True, skip_group_check=True)
                sS.inc(inst)
                if i >= 1:
                    emit_av(t, i - 1)
                if i == NU - 1:
                    emit_av(t, i)
                prev_done = all_units[i - 1][0] if i >= 1 else None
                if i >= 1 and (i - 1) == qc_last[all_units[i - 1][0]]:
                    emit_transposes(t, all_units[i - 1][0])
                if i == NU - 1:
                    emit_transposes(t, 15)

        @blk.scalar
        def _(s):
            s.wait_ge(sm["mcl"].h, mcl_mark)
            for i, (qc, kb, ksp) in enumerate(all_units):
                if i in diag_units:
                    s.wait_ge(sMm.h, sM_base + sM_cum[i])
                else:
                    s.wait_ge(sS.h, sS_base + i + 1)
                if i >= 2:
                    s.wait_ge(sAV.h, sAV_base + (i - 2) + 1)
                sE.inc(s.activation(out=pT_par[i % 2][:],
                                    in_=psS_par[i % 2][:],
                                    func=AF.Exp, bias=mcol[:, 0:1], scale=SCALE))
        self_sE = sE.n

        @blk.vector
        def _(v):
            ch = Chain(v, nrm)
            if layer == 0:
                v.wait_ge(sld.h, sld_n)
            for i, (qc, kb, ksp) in enumerate(all_units):
                if i in diag_units:
                    v.wait_ge(sS.h, sS_base + i + 1)
                    for j in range(2):
                        sMm.inc(v.tensor_add(psS_par[i % 2][:, j, :],
                                             psS_par[i % 2][:, j, :],
                                             mask_s[:, 2 * ksp + j, :]))
                if i == qc_last[qc]:
                    v.wait_ge(sAV.h, sAV_base + i + 1)
                    if qc > 0:
                        jq = (qc - 1) // 4
                        qsem_w = st if jq == 3 else sm[f"stq{jq}"]
                        v.wait_ge(qsem_w.h, q_bases[jq] + 16 * (((qc - 1) % 4) + 1))
                        v.wait_ge(trP.h, trP_base + 4 * qc)
                    for qs in range(4):
                        pav = (ps_av0 if qs < 2 else ps_av1)
                        ch.wait()
                        ch.op(v.reciprocal(rl_s[:, qs % 2:qs % 2 + 1],
                                           pav[:, qs % 2, 128:129]), sm["dvm"])
                        ch.wait()
                        ch.op(v.tensor_scalar_mul(ot[:, qs, :], pav[:, qs % 2, 0:128],
                                                  rl_s[:, qs % 2:qs % 2 + 1]))
                    for qs in range(4):
                        v.wait_ge(trP.h, trP_base + 4 * qc + qs + 1)
                        sm["ocp"].inc(v.tensor_copy(ostage[:, 128 * qs:128 * (qs + 1)],
                                                    ps_tr[:]))

        @blk.sync
        def _(sp):
            for qc in range(16):
                sp.wait_ge(sm["ocp"].h, ocp_base + 4 * (qc + 1))
                tb2 = qc // 2
                col = 512 * (qc % 2)
                qsem = st if qc >= 12 else sm[f"stq{qc // 4}"]
                qsem.inc(sp.dma_start(
                    out=out_d[512 * tb2 + 128 * h:512 * tb2 + 128 * (h + 1),
                              col:col + 512],
                    in_=ostage[:]), 16)

        return dict(sE=self_sE)

    if layer == 0:
        emit_kv_build(0)
        emit_norms(k_fm, 16, 48,
                   waits=[(sm["ke"], sm["ke"].n), (sm["ve"], sm["ve"].n)])
    for h in range(NH):
        sE_prev = sm["sE"].n

        if "qsel" in d:
            @blk.gpsimd
            def _(g, h=h, sE_prev=sE_prev):
                if h == 0 and "ccw_q" in d:
                    g.wait_ge(d["ccw_q"][0].h, d["ccw_q"][1])
                g.wait_ge(sm["sE"].h, sE_prev)
                g.wait_ge(gld.h, gld_n)
                for r in range(NCORE):
                    sm["qld"].inc(g.indirect_dma_start(
                        q_s[:, T * r:T * (r + 1)], None, qTg_d[:, :],
                        IndirectOffsetOnAxis(qsel_s[:, 8 * h + r:8 * h + r + 1], 0)), 16)
        else:
            @blk.sync
            def _(sp, h=h, sE_prev=sE_prev):
                sp.wait_ge(sm["sE"].h, sE_prev)
                for r in range(NCORE):
                    sm["qld"].inc(sp.dma_start(
                        out=q_s[:, T * r:T * (r + 1)],
                        in_=qTg_d[512 * r + 128 * h:512 * r + 128 * (h + 1), :]), 16)
        qld_target = sm["qld"].n

        if layer == 1:
            emit_kv_build(h)
            emit_rope_k(h)
            emit_norms(k_fm, 16, 48,
                       waits=[(sm["rpd"], sm["rpd"].n), (sm["ke"], sm["ke"].n),
                              (sm["nq"], sm["nq"].n), (sm["chd"], sm["chd"].n)])
        emit_norms(q_s, 16, 49,
                   waits=[(sm["qld"], qld_target), (sm["nq"], sm["nq"].n),
                          (sm["chd"], sm["chd"].n)])
        mcl_mark = emit_M(48, 49, waits=[(sm["nmx"], sm["nmx"].n)])
        emit_sdpa_head(h, mcl_mark)



def build_B(layer):
    P = Prog()
    d = {"qTg": P.din("qTg", [DM, T], BF16),
         "latg": P.din("latg", [NCORE * 512, T], BF16),
         "ident": P.din("ident", [128, 128], BF16),
         "outT": P.dout("outT", [DM, T], BF16)}
    if layer == 0:
        d.update(wkupT=P.din("wkupT", [256, 128], F32),
                 wvupT=P.din("wvupT", [256, 128], F32),
                 masks=P.din("masks", [4, 128, 512], F32))
    else:
        d.update(wk1=P.din("wk1", [512, 512], F32),
                 wv1=P.din("wv1", [512, 512], F32),
                 cosT=P.din("cosT", [64, S], F32),
                 sinT=P.din("sinT", [64, S], F32))
    sm = make_sems(P, B_SEMS, nslot=False)
    blk = P.ctx.enter_context(P.nc.Block())
    emit_B(P, blk, sm, layer, d, P.ctx)

    @blk.gpsimd
    def _(g):
        g.wait_ge(sm["st"].h, sm["st"].n)
        for k in range(3):
            g.wait_ge(sm[f"stq{k}"].h, sm[f"stq{k}"].n)

    return P


def run_B(layer, qT_list, latT_list, w):
    P = build_B(layer)
    cosT, sinT, masks, ident = host_consts()
    in_maps = []
    latg = np.ascontiguousarray(np.concatenate(latT_list, axis=0))
    for c in range(NCORE):
        qTg = np.ascontiguousarray(np.concatenate(
            [qT_list[r][512 * c:512 * (c + 1), :] for r in range(NCORE)], axis=0))
        m = {"qTg": qTg, "latg": latg, "ident": ident}
        if layer == 0:
            m.update(wkupT=w["wkupT"][c], wvupT=w["wvupT"][c], masks=masks)
        else:
            m.update(wk1=w["wk1"][c], wv1=w["wv1"][c], cosT=cosT, sinT=sinT)
        in_maps.append(m)
    res = run_bass_kernel_spmd(P.nc, in_maps, core_ids=list(range(NCORE)))
    return [np.asarray(r["outT"]) for r in res.results]



C_SEMS = ["ald", "wld", "wld1", "pe", "evac", "st"]


def emit_C(P, blk, sm, layer, d, scope, tag=""):
    nc = P.nc
    attn_d = d["outg"]
    wo_d = d["woT"]
    xp_d = d["xp"]
    xout_d = d["xout"]

    def SB(name, shape, dt):
        return scope.enter_context(nc.sbuf_tensor(name + tag, shape, dt))

    at_s = SB("at_s", [128, KC, T], BF16)
    w_pan = SB("w_pan", [128, 2, KC, 256], BF16)
    stagef = SB("stagef", [128, 4, T], F32)
    xf = SB("xf", [128, 4, T], F32)

    xls = sm["xls"]

    if "csel" in d:
        csel_s = SB("csel_s", [128, 32], mybir.dt.int32)

        @blk.gpsimd
        def _(g):
            sm["ald"].inc(g.dma_start(out=csel_s[:], in_=d["csel"][:]), 16)
            g.wait_ge(sm["ald"].h, sm["ald"].n)
            for kc in range(KC):
                sm["ald"].inc(g.indirect_dma_start(
                    at_s[:, kc, :], None, attn_d[:, :],
                    IndirectOffsetOnAxis(csel_s[:, kc:kc + 1], 0)), 16)
    else:
        @blk.sync
        def _(sp):
            for r in range(NCORE):
                sm["ald"].inc(sp.dma_start(
                    out=at_s[:, 4 * r:4 * (r + 1), :],
                    in_=attn_d[512 * r:512 * (r + 1), :].rearrange("(a p) t -> p a t", p=128)), 16)
    ald_n = sm["ald"].n
    st = sm["st"]
    st_marks = [st.n]
    ev_marks = [sm["evac"].n]

    with nc.psum_tensor("ps_proj" + tag, [128, 4, T], F32) as ps_proj:
        def wo_writer(ocg, mm_snap):
            par = ocg % 2

            @blk.sync
            def _(sp, ocg=ocg):
                sp.wait_ge(sm["evac"].h, ev_marks[ocg])
                for ol in range(2):
                    og = 2 * ocg + ol
                    xls[og % 4].inc(sp.dma_start(out=xf[:, og % 4, :],
                                                 in_=xp_d[128 * og:128 * (og + 1), :]), 16)
            marks = tuple(xls[i].n for i in range(4))

            @blk.vector
            def _(v, ocg=ocg, mm_snap=mm_snap, marks=marks, par=par):
                v.wait_ge(sm["pe"].h, mm_snap)
                v.wait_ge(st.h, st_marks[ocg])
                for ol in range(2):
                    og = 2 * ocg + ol
                    v.wait_ge(xls[og % 4].h, marks[og % 4])
                    sm["evac"].inc(v.tensor_add(stagef[:, og % 4, :],
                                                ps_proj[:, 2 * par + ol, :],
                                                xf[:, og % 4, :]))
            ev_now = sm["evac"].n
            ev_marks.append(sm["evac"].n)

            @blk.sync
            def _(sp, ocg=ocg, ev_now=ev_now):
                sp.wait_ge(sm["evac"].h, ev_now)
                for ol in range(2):
                    og = 2 * ocg + ol
                    st.inc(sp.dma_start(out=xout_d[128 * og:128 * (og + 1), :],
                                        in_=stagef[:, og % 4, :]), 16)
            st_marks.append(st.n)

        emit_proj(P, blk, w_d=wo_d, n_ocg=16, xn_s=at_s, sm=sm, w_pan=w_pan,
                  ps_proj=ps_proj, writer=wo_writer,
                  xn_total=[(sm["ald"], ald_n)],
                  prev_pe_mark=sm["pe"].n, prev_evac_mark=sm["evac"].n)


def build_C():
    P = Prog()
    d = {"outg": P.din("outg", [DM, T], BF16),
         "woT": P.din("woT", [DM, DM], F32),
         "xp": P.din("xp", [DM, T], F32),
         "xout": P.dout("xoutT", [DM, T], F32)}
    sm = make_sems(P, C_SEMS, nslot=False)
    sm["xls"] = [P.sem(f"xls{i}") for i in range(4)]
    blk = P.ctx.enter_context(P.nc.Block())
    emit_C(P, blk, sm, 0, d, P.ctx)

    @blk.gpsimd
    def _(g):
        g.wait_ge(sm["st"].h, sm["st"].n)

    return P


def run_C(layer, outT_list, xp_cols, w):
    P = build_C()
    in_maps = []
    for c in range(NCORE):
        outg = np.ascontiguousarray(np.concatenate(
            [outT_list[r][512 * c:512 * (c + 1), :] for r in range(NCORE)], axis=0))
        in_maps.append({"outg": outg,
                        "woT": w["woT0"] if layer == 0 else w["woT1"],
                        "xp": xp_cols[c]})
    res = run_bass_kernel_spmd(P.nc, in_maps, core_ids=list(range(NCORE)))
    return [np.asarray(r["xoutT"]) for r in res.results]



def kernel(**inputs):
    if _UNFUSED:
        return kernel_unfused(**inputs)
    return kernel_fused(**inputs)


def kernel_unfused(**inputs):
    w = prep_weights(inputs)
    x = np.asarray(inputs["L0_attn_hidden_states"])[0]
    xT = np.ascontiguousarray(x.T)
    out_cols = [np.ascontiguousarray(xT[:, T * c:T * (c + 1)]) for c in range(NCORE)]

    for layer in (0, 1):
        qT, latT, xp = run_A(layer, out_cols, w)
        outT = run_B(layer, qT, latT, w)
        out_cols = run_C(layer, outT, xp, w)

    x2T = np.concatenate(out_cols, axis=1)
    return np.ascontiguousarray(x2T.T).reshape(1, S, DM)



def _all_sem_objs(sm):
    out = []
    for v in sm.values():
        if isinstance(v, list):
            out.extend(v)
        elif isinstance(v, Sem):
            out.append(v)
    return out


def emit_barrier(P, blk, sm, skip=()):
    bar = sm["bar"]
    skipset = {sm[k] for k in skip}

    @blk.gpsimd
    def _(g):
        for s_ in _all_sem_objs(sm):
            if s_ is bar or s_ in skipset or s_.n == 0:
                continue
            g.wait_ge(s_.h, s_.n)
        bar.inc(g.nop())
    tgt = bar.n

    for en in ("tensor", "vector", "scalar", "sync"):
        @getattr(blk, en)
        def _(e):
            e.wait_ge(bar.h, tgt)


def build_fused():
    P = Prog()
    nc = P.nc
    xT_in = P.din("xT", [DM, T], F32)
    wq0 = P.din("wqT0", [DM, DM], F32)
    wlat0 = P.din("wlatT0", [DM, 512], F32)
    wb0 = P.din("wbT0", [DM, DM], F32)
    bias0 = P.din("biasb0", [128, KC], F32)
    ar0 = P.din("arT0", [DM, 80], F32)
    btr0 = P.din("BTr0", [64, 16, DM], F32)
    identF = P.din("identF", [128, 128], BF16)
    wku = P.din("wkupT", [256, 128], F32)
    wvu = P.din("wvupT", [256, 128], F32)
    masks = P.din("masks", [4, 128, 512], F32)
    wo0 = P.din("woT0", [DM, DM], F32)
    wq1 = P.din("wqT1", [DM, DM], F32)
    wlat1 = P.din("wlatT1", [DM, 512], F32)
    tern = P.din("ternT", [DM, DM], I8)
    scl = P.din("scalesT", [KC, DM], F32)
    cosq = P.din("cosq", [64, T], F32)
    sinq = P.din("sinq", [64, T], F32)
    wk1 = P.din("wk1", [512, 512], F32)
    wv1 = P.din("wv1", [512, 512], F32)
    cosT_d = P.din("cosT", [64, S], F32)
    sinT_d = P.din("sinT", [64, S], F32)
    wo1 = P.din("woT1", [DM, DM], F32)
    qsel_d = P.din("qsel", [128, 32], mybir.dt.int32)
    csel_d = P.din("csel", [128, 32], mybir.dt.int32)
    dint = lambda nm_, shp, dt: nc.dram_tensor(nm_, shp, dt, kind="Internal").ap()
    qT_loc = dint("qT_loc", [DM, T], BF16)
    lat_loc = dint("lat_loc", [512, T], BF16)
    out_loc = dint("out_loc", [DM, T], BF16)
    qTg = dint("qTg", [NCORE * DM, T], BF16)
    latg = dint("latg", [NCORE * 512, T], BF16)
    outg = dint("outg", [NCORE * DM, T], BF16)
    xp_scr = dint("xp_scr", [DM, T], F32)
    x1_scr = dint("x1_scr", [DM, T], F32)
    x2T = P.dout("x2T", [DM, T], F32)

    sem_names = sorted(set(A_SEMS + B_SEMS + C_SEMS + ["cc", "bar"]))
    sm = make_sems(P, sem_names)
    sm["xls"] = [P.sem(f"xls{i}") for i in range(4)]
    blk = P.ctx.enter_context(nc.Block())
    cc = sm["cc"]



    def collectives(pairs):
        @blk.gpsimd
        def _(g):
            for kind, src_, dst_ in pairs:
                cc.inc(g.collective_compute(
                    kind, OP.bypass, [list(range(NCORE))], [src_.opt()], [dst_.opt()]), 1)
        tgt = cc.n

        @blk.gpsimd
        def _(g):
            g.wait_ge(cc.h, tgt)

    for layer in (0, 1):
        tagA = f"_a{layer}"
        dA = {"xT": xT_in if layer == 0 else x1_scr,
              "qT": qT_loc, "lat": lat_loc, "xp": xp_scr}
        if layer == 0:
            dA.update(wqT=wq0, wlatT=wlat0, wbT=wb0, biasb=bias0, arT=ar0,
                      BTr=btr0, identF=identF)
        else:
            dA.update(wqT=wq1, wlatT=wlat1, ternT=tern, scalesT=scl,
                      cosq=cosq, sinq=sinq)
        ag_tgt = {}

        def hook_lat(latn):
            @blk.gpsimd
            def _(g):
                g.wait_ge(sm["st"].h, latn)
                cc.inc(g.collective_compute(
                    "AllGather", OP.bypass, [list(range(NCORE))],
                    [lat_loc.opt()], [latg.opt()]), 1)
            ag_tgt["lat"] = cc.n

        def hook_q(qn):
            @blk.gpsimd
            def _(g):
                g.wait_ge(sm["st"].h, qn)
                cc.inc(g.collective_compute(
                    "AllGather", OP.bypass, [list(range(NCORE))],
                    [qT_loc.opt()], [qTg.opt()]), 1)
            ag_tgt["q"] = cc.n

        sA = ExitStack()
        emit_A(P, blk, sm, layer, dA, sA, tag=tagA, ag_hook=(hook_lat, hook_q))
        sA.close()
        emit_barrier(P, blk, sm, skip=("cc",))

        dB = {"qTg": qTg, "latg": latg, "ident": identF, "outT": out_loc,
              "qsel": qsel_d, "ccw_lat": (cc, ag_tgt["lat"]),
              "ccw_q": (cc, ag_tgt["q"])}
        if layer == 0:
            dB.update(wkupT=wku, wvupT=wvu, masks=masks)
        else:
            dB.update(wk1=wk1, wv1=wv1, cosT=cosT_d, sinT=sinT_d)
        sB = ExitStack()
        emit_B(P, blk, sm, layer, dB, sB, tag=f"_b{layer}")
        sB.close()

        @blk.gpsimd
        def _(g):
            for k in range(4):
                qs = sm["st"] if k == 3 else sm[f"stq{k}"]
                g.wait_ge(qs.h, qs.n)
                cc.inc(g.collective_compute(
                    "AllGather", OP.bypass, [list(range(NCORE))],
                    [out_loc[1024 * k:1024 * (k + 1), :].opt()],
                    [outg[8192 * k:8192 * (k + 1), :].opt()]), 1)
        emit_barrier(P, blk, sm)

        dC = {"outg": outg, "woT": wo0 if layer == 0 else wo1, "xp": xp_scr,
              "xout": x1_scr if layer == 0 else x2T, "csel": csel_d}
        sC = ExitStack()
        emit_C(P, blk, sm, layer, dC, sC, tag=f"_c{layer}")
        sC.close()
        if layer == 0:
            emit_barrier(P, blk, sm)

    @blk.gpsimd
    def _(g):
        g.wait_ge(sm["st"].h, sm["st"].n)

    return P


def _qsel(c):
    p = np.arange(128, dtype=np.int64)
    out = np.zeros((128, 32), dtype=np.int32)
    for h in range(4):
        for r in range(8):
            out[:, 8 * h + r] = 4096 * r + 512 * c + 128 * h + p
    return out


def _csel(c):
    p = np.arange(128, dtype=np.int64)
    out = np.zeros((128, 32), dtype=np.int32)
    for r in range(8):
        for a in range(4):
            out[:, 4 * r + a] = (8192 * (c // 2) + 1024 * r + 512 * (c % 2)
                                 + 128 * a + p)
    return out


_FUSED_CACHE = {}


def kernel_fused(**inputs):
    w = prep_weights(inputs)
    cosT, sinT, masksv, ident = host_consts()
    x = np.asarray(inputs["L0_attn_hidden_states"])[0]
    xT = np.ascontiguousarray(x.T)

    if "P" not in _FUSED_CACHE:
        _FUSED_CACHE["P"] = build_fused()
    P = _FUSED_CACHE["P"]

    in_maps = []
    for c in range(NCORE):
        m = {"xT": np.ascontiguousarray(xT[:, T * c:T * (c + 1)]),
             "wqT0": w["wqT0"], "wlatT0": w["wlatT0"], "wbT0": w["wbT0"],
             "biasb0": w["biasb0"], "arT0": w["arT0"], "BTr0": w["BTr0"],
             "identF": ident, "wkupT": w["wkupT"][c], "wvupT": w["wvupT"][c],
             "masks": masksv, "woT0": w["woT0"],
             "wqT1": w["wqT1"], "wlatT1": w["wlatT1"], "ternT": w["ternT"],
             "scalesT": w["scalesT"],
             "cosq": np.ascontiguousarray(cosT[:, T * c:T * (c + 1)]),
             "sinq": np.ascontiguousarray(sinT[:, T * c:T * (c + 1)]),
             "wk1": w["wk1"][c], "wv1": w["wv1"][c],
             "cosT": cosT, "sinT": sinT, "woT1": w["woT1"],
             "qsel": _qsel(c), "csel": _csel(c)}
        in_maps.append(m)
    res = run_bass_kernel_spmd(P.nc, in_maps, core_ids=list(range(NCORE)))
    x2T_full = np.concatenate([np.asarray(r["x2T"]) for r in res.results], axis=1)
    return np.ascontiguousarray(x2T_full.T).reshape(1, S, DM)
